# version 10
# baseline (speedup 1.0000x reference)
import numpy as np
from contextlib import ExitStack

import concourse.bass as bass
import concourse.mybir as mybir
from concourse.bass import IndirectOffsetOnAxis
from concourse.bass_utils import run_bass_kernel_spmd
from concourse.bass_isa import ReduceOp

F32 = mybir.dt.float32
F32R = mybir.dt.float32r
BF16 = mybir.dt.bfloat16
U32 = mybir.dt.uint32
AF = mybir.ActivationFunctionType
ALU = mybir.AluOpType
AX = mybir.AxisListType.X

P = 128
T = 2048
KC = 8
NTB = 4
NTC = 16
D = 1024
NHEAD = 16
HD = 64
G = 409
EPS = 1e-5

LAUNCHES = [["p1", "p2", "p3", "p4", "p5"]]
DEBUG = None

_GS = [0]


def gs(p):
    _GS[0] += 1
    return f"{p}_{_GS[0]}"


def ts512(i):
    return slice(i * 512, (i + 1) * 512)


def ts128(i):
    return slice(i * 128, (i + 1) * 128)


class Sem:
    def __init__(self, h):
        self.h = h
        self.n = 0


class Ctx:
    def __init__(self, nc, es, io):
        self.nc = nc
        self.es = es
        self.io = io
        self.sem = {}
        for name in ["ldw", "ldg", "ldt", "st", "pe", "ve", "ac", "gp", "cc",
                     "edv", "eac", "egp", "aux"]:
            self.sem[name] = Sem(es.enter_context(nc.semaphore(gs("sm_" + name))))
        self.t = {}

    def sbuf(self, name, shape, dtype, es=None):
        return (es or self.es).enter_context(
            self.nc.sbuf_tensor(gs(name), list(shape), dtype))

    def psum(self, name, shape, dtype=F32, es=None):
        return (es or self.es).enter_context(
            self.nc.psum_tensor(gs(name), list(shape), dtype))

    def bank(self, i):
        return (self.t["psA"] if i < 4 else self.t["psB"])[:, i % 4, :]

    def bankb(self, i):
        return self.bank(i).bitcast(BF16)


def blk(ctx, **fns):
    with ctx.nc.Block() as b:
        for k, f in fns.items():
            getattr(b, k)(f)


def edge(ctx, eng, instr, which):
    s = ctx.sem[which]
    s.n += 1
    instr.then_inc(s.h, 1)
    eng.wait_ge(s.h, s.n)


def pipe(ctx, jobs, emit_mm, emit_ev, banks=(0, 1), ev_eng="vector"):
    pe, ve = ctx.sem["pe"], ctx.sem["ve"]
    pe0, ve0 = pe.n, ve.n
    nbuf = len(banks)

    def f_mm(tensor):
        for i, job in enumerate(jobs):
            if i >= nbuf:
                tensor.wait_ge(ve.h, ve0 + i - nbuf + 1)
            emit_mm(tensor, job, banks[i % nbuf]).then_inc(pe.h, 1)

    def f_ev(eng):
        for i, job in enumerate(jobs):
            eng.wait_ge(pe.h, pe0 + i + 1)
            emit_ev(eng, job, banks[i % nbuf]).then_inc(ve.h, 1)

    pe.n += len(jobs)
    ve.n += len(jobs)
    blk(ctx, **{"tensor": f_mm, ev_eng: f_ev})


def stats(ctx, src, flavor, es, bcast=(), nsq=2):
    nc = ctx.nc
    rows = ctx.sbuf("st_rows", [1, 6, T], F32, es)
    rc = ctx.sbuf("st_rcol", [P, NTC], F32, es)
    bts = {k: ctx.sbuf("st_b_" + k, [P, NTB, 512], F32R, es) for k in bcast}
    es_t = ExitStack()
    sq = [ctx.sbuf("st_sq", [P, T], F32R, es_t) for _ in range(nsq)]
    sq = [sq[i % nsq] for i in range(2)]
    ac, pe = ctx.sem["ac"], ctx.sem["pe"]
    ac0, pe0 = ac.n, pe.n
    psA, psB = ctx.t["psA"], ctx.t["psB"]
    ones_col = ctx.t["ones_col"]

    def f_scalar(scalar):
        for kc in range(KC):
            if kc >= nsq:
                scalar.wait_ge(pe.h, pe0 + kc - nsq + 1)
            scalar.activation(sq[kc % 2][:, :], src[:, kc, :], AF.Square).then_inc(ac.h, 1)

    def f_tensor(tensor):
        for kc in range(KC):
            for tb in range(NTB):
                tensor.matmul(psA[0:1, tb, :], ones_col[:, 0:1], src[:, kc, ts512(tb)],
                              start=(kc == 0), stop=(kc == KC - 1))
            tensor.wait_ge(ac.h, ac0 + kc + 1)
            ins = None
            for tb in range(NTB):
                ins = tensor.matmul(psB[0:1, tb, :], ones_col[:, 0:1],
                                    sq[kc % 2][:, ts512(tb)],
                                    start=(kc == 0), stop=(kc == KC - 1))
            ins.then_inc(pe.h, 1)

    ac.n += KC
    pe.n += KC
    blk(ctx, scalar=f_scalar, tensor=f_tensor)

    inv_d = 1.0 / D
    r = lambda i: rows[0:1, i, :]

    def f_rowmath(vector):
        for tb in range(NTB):
            vector.tensor_copy(rows[0:1, 0, ts512(tb)], psA[0:1, tb, :])
            ins = vector.tensor_copy(rows[0:1, 1, ts512(tb)], psB[0:1, tb, :])
            edge(ctx, vector, ins, "edv")
        edge(ctx, vector, vector.tensor_scalar(r(2), r(0), inv_d, None, ALU.mult), "edv")
        edge(ctx, vector, vector.tensor_scalar(r(3), r(2), -1.0, None, ALU.mult), "edv")
        edge(ctx, vector, vector.tensor_mul(r(5), r(2), r(2)), "edv")
        edge(ctx, vector, vector.tensor_scalar(r(4), r(1), inv_d, None, ALU.mult), "edv")
        edge(ctx, vector, vector.tensor_sub(r(4), r(4), r(5)), "edv")
        edge(ctx, vector, vector.tensor_scalar(r(5), r(4), EPS, None, ALU.add), "edv")
        edge(ctx, vector, vector.reciprocal(r(0), r(5)), "edv")

    blk(ctx, vector=f_rowmath)

    if flavor == "ln":
        def f_sqrt(scalar):
            edge(ctx, scalar, scalar.activation(r(1), r(0), AF.Sqrt), "eac")
            edge(ctx, scalar, scalar.activation(r(5), r(5), AF.Sqrt), "eac")
        blk(ctx, scalar=f_sqrt)
        mult_slot, inv_slot, plain_slot = 1, 5, 1
    else:
        def f_v2(vector):
            edge(ctx, vector, vector.tensor_mul(r(5), r(4), r(0)), "edv")
            edge(ctx, vector, vector.tensor_scalar(r(5), r(5), EPS, None, ALU.add), "edv")
        blk(ctx, vector=f_v2)

        def f_sq2(scalar):
            edge(ctx, scalar, scalar.activation(r(1), r(0), AF.Sqrt), "eac")
            edge(ctx, scalar, scalar.activation(r(5), r(5), AF.Sqrt), "eac")
        blk(ctx, scalar=f_sq2)

        def f_rr(vector):
            edge(ctx, vector, vector.reciprocal(r(0), r(5)), "edv")
            edge(ctx, vector, vector.tensor_mul(r(4), r(1), r(0)), "edv")
            edge(ctx, vector, vector.reciprocal(r(5), r(4)), "edv")
        blk(ctx, vector=f_rr)
        mult_slot, inv_slot, plain_slot = 4, 5, 1

    out = {"_rows": rows,
           "mu": r(2), "negmu": r(3),
           "rstd": r(mult_slot), "sig": r(inv_slot), "plain": r(plain_slot)}

    def f_tr(tensor):
        for tc in range(NTC):
            tensor.transpose(psA[:, 0, tc:tc + 1], rows[0:1, mult_slot, ts128(tc)],
                             ctx.t["eye_f"][0:1, 0:1])

    blk(ctx, tensor=f_tr)

    def f_trc(vector):
        vector.tensor_copy(rc[:, :], psA[:, 0, 0:NTC])

    blk(ctx, vector=f_trc)
    out["rstd_col"] = rc

    slot_of = {"mult": mult_slot, "mu": 2, "plain": plain_slot}
    for bk in bcast:
        bc = bts[bk]
        out["b_" + bk] = bc
        sl = slot_of[bk]

        def e_mm(tensor, tb, bi, sl=sl):
            return tensor.matmul(ctx.bank(bi), ctx.t["ones_row"][0:1, 0:P],
                                 rows[0:1, sl, ts512(tb)], start=True, stop=True)

        def e_ev(vector, tb, bi, bc=bc):
            return vector.tensor_copy(bc[:, tb, :], ctx.bank(bi))

        pipe(ctx, list(range(NTB)), e_mm, e_ev, banks=(0, 1))
    es_t.close()
    return out


def load_w(ctx, wap, mslice, dtype, es, name="w", want_sum=False, wcol=None,
           bcol=None):
    nc = ctx.nc
    m0, m1 = mslice
    M = m1 - m0
    wsb = ctx.sbuf(name, [P, KC, M], dtype, es)
    ldg = ctx.sem["ldg"]

    def f_load(gpsimd):
        src = wap.rearrange("(kc p) m -> p kc m", p=P)[:, :, m0:m1]
        gpsimd.dma_start(out=wsb[:, :, :], in_=src).then_inc(ldg.h, 16)

    ldg.n += 16
    blk(ctx, gpsimd=f_load)
    wait_n = ldg.n

    out = {"w": wsb}
    nmm = (M + 511) // 512
    need_rows = want_sum or (bcol is not None)
    rows = ctx.sbuf(name + "_r", [1, 3, max(M, 16)], F32, es) if need_rows else None

    if bcol is not None:
        bcol_r = ctx.sbuf(name + "_bc", [P, KC], F32R, es)

        def f_bcr(vector):
            vector.tensor_copy(bcol_r[:, :], bcol[:, :])
        blk(ctx, vector=f_bcr)

        def f_qb_mm(tensor):
            tensor.wait_ge(ldg.h, wait_n)
            for j in range(nmm):
                sl = slice(j * 512, min(M, (j + 1) * 512))
                w = sl.stop - sl.start
                for kc in range(KC):
                    tensor.matmul(ctx.bank(j)[0:1, 0:w], bcol_r[:, kc:kc + 1],
                                  wsb[:, kc, sl], start=(kc == 0), stop=(kc == KC - 1))
        blk(ctx, tensor=f_qb_mm)

        def f_qb_ev(vector):
            for j in range(nmm):
                sl = slice(j * 512, min(M, (j + 1) * 512))
                vector.tensor_copy(rows[0:1, 2, sl], ctx.bank(j)[0:1, 0:sl.stop - sl.start])
        blk(ctx, vector=f_qb_ev)
        out["qb"] = rows[0:1, 2, 0:M]

    if wcol is not None:
        def f_scale(vector):
            vector.wait_ge(ldg.h, wait_n)
            for kc in range(KC):
                vector.tensor_scalar(wsb[:, kc, :], wsb[:, kc, :], wcol[:, kc:kc + 1],
                                     None, ALU.mult)
        blk(ctx, vector=f_scale)

    if want_sum:
        def f_sum(tensor):
            tensor.wait_ge(ldg.h, wait_n)
            for j in range(nmm):
                sl = slice(j * 512, min(M, (j + 1) * 512))
                w = sl.stop - sl.start
                for kc in range(KC):
                    tensor.matmul(ctx.bank(j)[0:1, 0:w], ctx.t["ones_col"][:, 0:1],
                                  wsb[:, kc, sl], start=(kc == 0), stop=(kc == KC - 1))
        blk(ctx, tensor=f_sum)

        def f_ev(vector):
            for j in range(nmm):
                sl = slice(j * 512, min(M, (j + 1) * 512))
                w = sl.stop - sl.start
                vector.tensor_copy(rows[0:1, 0, sl], ctx.bank(j)[0:1, 0:w])
                vector.tensor_scalar(rows[0:1, 1, sl], ctx.bank(j)[0:1, 0:w], -1.0,
                                     None, ALU.mult)
        blk(ctx, vector=f_ev)
        out["wsum"] = rows[0:1, 0, 0:M]
        out["negwsum"] = rows[0:1, 1, 0:M]

    if not need_rows and wcol is None:
        def f_w(vector):
            vector.wait_ge(ldg.h, wait_n)
        blk(ctx, vector=f_w)
    return out


def proj_tm(ctx, wd, st, src, M, evac):
    nj = (M + 511) // 512
    jobs = [(tc, j) for tc in range(NTC) for j in range(nj)]

    def e_mm(tensor, job, bi):
        tc, j = job
        sl = slice(j * 512, min(M, (j + 1) * 512))
        w = sl.stop - sl.start
        ps = ctx.bank(bi)[:, 0:w]
        for kc in range(KC):
            ins = tensor.matmul(ps, src[:, kc, ts128(tc)], wd["w"][:, kc, sl],
                                start=(kc == 0), stop=False)
        ins = tensor.matmul(ps, st["negmu"][0:1, ts128(tc)], wd["wsum"][0:1, sl],
                            start=False, stop=("qb" not in wd))
        if "qb" in wd:
            ins = tensor.matmul(ps, st["sig"][0:1, ts128(tc)], wd["qb"][0:1, sl],
                                start=False, stop=True)
        return ins

    def e_ev(vector, job, bi):
        tc, j = job
        sl = slice(j * 512, min(M, (j + 1) * 512))
        return evac(vector, ctx.bank(bi)[:, 0:sl.stop - sl.start], tc, j)

    pipe(ctx, jobs, e_mm, e_ev, banks=(0, 1))


def proj_fm(ctx, wsb, src, evac, mcs=range(KC), st=None, wd=None, m0=0):
    jobs = [(mc, tb) for mc in mcs for tb in range(NTB)]

    def e_mm(tensor, job, bi):
        mc, tb = job
        ml = mc - m0
        ps = ctx.bank(bi)
        for kc in range(KC):
            ins = tensor.matmul(ps, wsb[:, kc, ts128(ml)], src[:, kc, ts512(tb)],
                                start=(kc == 0),
                                stop=(kc == KC - 1 and st is None))
        if st is not None:
            last = ("negwsum" not in wd, "qb" not in wd)
            if "negwsum" in wd:
                ins = tensor.matmul(ps, wd["negwsum"][0:1, ts128(ml)],
                                    st["mu"][0:1, ts512(tb)], start=False,
                                    stop=("qb" not in wd))
            if "qb" in wd:
                ins = tensor.matmul(ps, wd["qb"][0:1, ts128(ml)],
                                    st["sig"][0:1, ts512(tb)], start=False, stop=True)
        return ins

    def e_ev(vector, job, bi):
        mc, tb = job
        return evac(vector, ctx.bank(bi), mc, tb)

    pipe(ctx, jobs, e_mm, e_ev, banks=(0, 1))


def declare_io(nc):
    t = {}

    def din(name, shape, dtype):
        t[name] = nc.dram_tensor(name, list(shape), dtype, kind="ExternalInput").ap()

    din("xin", [D, T], F32R)
    for nm in ["L0_Wq", "L0_Wk", "L0_Wv", "L0_Wout",
               "L1_Wqk_t", "L1_Wv_t", "L1_Wo_t", "L1_Wqk_c", "L1_Wv_c", "L1_Wo_c"]:
        din(nm, [D, D], F32)
    din("L0_g", [P, KC], F32)
    din("L0_b", [P, KC], F32)
    din("L0_router", [D, 8], F32)
    din("L0_W1H", [8, 32, P, KC, P], F32)
    din("L0_W2H", [8, KC, P, 32, P], F32)
    din("L1_tau_t", [1, 16], F32)
    din("L1_tau_c", [1, 8], F32)
    for nm in ["L1_ln1_w", "L1_ln1_b", "L1_ln2_w", "L1_ln2_b", "L1_g1", "L1_g2"]:
        din(nm, [P, KC], F32)
    din("L1_wg", [D, 8], F32)
    din("L1_w1H", [8, 32, P, KC, P], F32)
    din("L1_b1", [8, 4096], F32)
    din("L1_w2H", [8, KC, P, 32, P], F32)
    din("L1_b2", [8, D], F32)
    din("eye_b", [P, P], BF16)
    din("eye_f", [P, P], F32)
    din("eye_r", [P, P], F32R)
    din("slt_r", [P, P], F32R)
    din("ones_sq", [P, P], F32R)
    din("ones_col", [P, 1], F32R)
    din("ones_row", [1, 512], F32)
    din("ones_rowb", [1, 512], BF16)
    din("ropeC", [T, HD], BF16)
    din("ropeS", [T, HD], BF16)
    din("mloc16", [P, NTC], BF16)
    din("mloc", [P, 512], BF16)
    din("mglob", [P, 512], BF16)
    din("pmask", [8, 1], F32R)
    din("idx_kv", [P, 1], U32)
    din("idx_q2", [P, 2], U32)
    din("idx_cd", [P, 1], U32)
    din("idx_qk", [P, 2 * NHEAD], U32)
    din("idx_rhs", [P, NHEAD], U32)
    din("idx_v", [P, NHEAD], U32)
    t["xout"] = nc.dram_tensor("xout", [D, T], F32R, kind="ExternalOutput").ap()
    t["dbg"] = nc.dram_tensor("dbg", [P, 4096], F32, kind="ExternalOutput").ap()

    def dint(name, shape, dtype):
        t[name] = nc.dram_tensor(gs(name), list(shape), dtype).ap()

    dint("xspill", [D, T], F32R)
    dint("ag_kv_in", [65, D], F32)
    dint("ag_kv_out", [8 * 65, D], F32)
    dint("ag_qkv_in", [2 * D, T], BF16)
    dint("ag_qkv_out", [8 * 2 * D, T], BF16)
    dint("ag_q2_in", [P, 256], F32)
    dint("ag_q2_out", [8 * P, 256], F32)
    dint("ag_cd_in", [P, KC * P], F32)
    dint("ag_cd_out", [8 * P, KC * P], F32)
    dint("ag_cnt_in", [1, 16], F32)
    dint("ag_cnt_out", [8, 16], F32)
    return t


CONST_NAMES = ["eye_b", "eye_f", "eye_r", "slt_r", "ones_sq", "ones_col", "ones_row",
               "ones_rowb", "idx_rhs",
               "mloc16", "mloc", "mglob", "pmask", "idx_kv", "idx_q2", "idx_cd",
               "idx_qk", "idx_v", "L0_g", "L0_b", "L1_ln1_w", "L1_ln1_b",
               "L1_ln2_w", "L1_ln2_b", "L1_g1", "L1_g2", "L1_tau_t", "L1_tau_c"]

CONST_SHAPES = {
    "eye_b": ([P, P], BF16), "eye_f": ([P, P], F32), "eye_r": ([P, P], F32R),
    "slt_r": ([P, P], F32R),
    "ones_sq": ([P, P], F32R), "ones_col": ([P, 1], F32R), "ones_row": ([1, 512], F32), "ones_rowb": ([1, 512], BF16),
    "mloc16": ([P, NTC], BF16), "mloc": ([P, 512], BF16), "mglob": ([P, 512], BF16),
    "pmask": ([8, 1], F32R), "idx_kv": ([P, 1], U32), "idx_q2": ([P, 2], U32),
    "idx_cd": ([P, 1], U32), "idx_qk": ([P, 2 * NHEAD], U32), "idx_rhs": ([P, NHEAD], U32), "idx_v": ([P, NHEAD], U32),
    "L0_g": ([P, KC], F32), "L0_b": ([P, KC], F32),
    "L1_ln1_w": ([P, KC], F32), "L1_ln1_b": ([P, KC], F32),
    "L1_ln2_w": ([P, KC], F32), "L1_ln2_b": ([P, KC], F32),
    "L1_g1": ([P, KC], F32), "L1_g2": ([P, KC], F32),
    "L1_tau_t": ([1, 16], F32), "L1_tau_c": ([1, 8], F32),
}


PHASE_CONSTS = {
    "eye_b": ("p1",), "mloc16": ("p1",), "mloc": ("p1",), "mglob": ("p1",),
    "idx_kv": ("p1",),
    "idx_q2": ("p3",), "idx_qk": ("p3",), "idx_rhs": ("p3",), "idx_v": ("p3",),
    "L1_tau_t": ("p3",), "idx_cd": ("p4",), "L1_tau_c": ("p4",),
    "L1_ln1_w": ("p3",), "L1_ln1_b": ("p3",),
    "L1_ln2_w": ("p4",), "L1_ln2_b": ("p4",),
    "L1_g1": ("p3",), "L1_g2": ("p4",),
    "L0_g": ("p2",), "L0_b": ("p2",),
    "pmask": ("p5",), "slt_r": ("p5",), "ones_sq": ("p5",),
}


def setup_ctx(ctx, phases):
    t = ctx.t
    t["X"] = ctx.sbuf("X", [P, KC, T], F32R)
    t["psA"] = ctx.psum("psA", [P, 4, 512], F32)
    t["psB"] = ctx.psum("psB", [P, 4, 512], F32)
    names = [nm for nm in CONST_NAMES
             if not PHASE_CONSTS.get(nm) or
             (any(p in phases for p in PHASE_CONSTS[nm])
              and "p1" not in PHASE_CONSTS.get(nm, ()))]
    rope = False
    for nm in names:
        shape, dt = CONST_SHAPES[nm]
        t["c_" + nm] = ctx.sbuf("c_" + nm, shape, dt)
    if rope:
        t["c_ropeC"] = ctx.sbuf("c_ropeC", [P, NTC, HD], BF16)
        t["c_ropeS"] = ctx.sbuf("c_ropeS", [P, NTC, HD], BF16)
    ldt = ctx.sem["ldt"]

    def f_load(sync):
        sync.dma_start(out=t["X"][:, :, :],
                       in_=ctx.io["xin"].rearrange("(kc p) t -> p kc t", p=P)).then_inc(ldt.h, 16)
        for nm in names:
            sync.dma_start(out=t["c_" + nm][...], in_=ctx.io[nm]).then_inc(ldt.h, 16)
        if rope:
            sync.dma_start(out=t["c_ropeC"][:, :, :],
                           in_=ctx.io["ropeC"].rearrange("(tc p) e -> p tc e", p=P)).then_inc(ldt.h, 16)
            sync.dma_start(out=t["c_ropeS"][:, :, :],
                           in_=ctx.io["ropeS"].rearrange("(tc p) e -> p tc e", p=P)).then_inc(ldt.h, 16)
        sync.wait_ge(ldt.h, ldt.n)

    ldt.n += 16 * (1 + len(names) + (2 if rope else 0))
    blk(ctx, sync=f_load)
    for k in ["eye_b", "eye_f", "eye_r", "slt_r", "ones_sq", "ones_col", "ones_row"]:
        if "c_" + k in t:
            t[k] = t["c_" + k]


def finish(ctx):
    st = ctx.sem["st"]
    target = st.n + 16

    def f_store(sync):
        sync.dma_start(out=ctx.io["xout"].rearrange("(kc p) t -> p kc t", p=P),
                       in_=ctx.t["X"][:, :, :]).then_inc(st.h, 16)
        sync.wait_ge(st.h, target)

    st.n = target
    blk(ctx, sync=f_store)


def allgather(ctx, in_ap, out_ap):
    cc = ctx.sem["cc"]
    cc.n += 1
    target = cc.n

    def f_cc(gpsimd):
        gpsimd.collective_compute(
            "AllGather", ALU.bypass, replica_groups=[[0, 1, 2, 3, 4, 5, 6, 7]],
            ins=[in_ap.opt()], outs=[out_ap.opt()]).then_inc(cc.h, 1)
        gpsimd.wait_ge(cc.h, target)

    blk(ctx, gpsimd=f_cc)




P1_CONSTS = ["eye_b", "mloc16", "mloc", "mglob", "idx_kv"]


def phase1(ctx):
    nc, t = ctx.nc, ctx.t
    pe, ve, ac, ldg = ctx.sem["pe"], ctx.sem["ve"], ctx.sem["ac"], ctx.sem["ldg"]
    with ExitStack() as es:
        for nm in P1_CONSTS:
            shape, dt_ = CONST_SHAPES[nm]
            t["c_" + nm] = ctx.sbuf("c_" + nm, shape, dt_, es)
        t["c_ropeC"] = ctx.sbuf("c_ropeC", [P, NTC, HD], BF16, es)
        t["c_ropeS"] = ctx.sbuf("c_ropeS", [P, NTC, HD], BF16, es)
        t["eye_b"] = t["c_eye_b"]
        ldt1 = ctx.sem["ldt"]

        def f_ld1(sync):
            for nm in P1_CONSTS:
                sync.dma_start(out=t["c_" + nm][...], in_=ctx.io[nm]).then_inc(ldt1.h, 16)
            sync.dma_start(out=t["c_ropeC"][:, :, :],
                           in_=ctx.io["ropeC"].rearrange("(tc p) e -> p tc e", p=P)).then_inc(ldt1.h, 16)
            sync.dma_start(out=t["c_ropeS"][:, :, :],
                           in_=ctx.io["ropeS"].rearrange("(tc p) e -> p tc e", p=P)).then_inc(ldt1.h, 16)
            sync.wait_ge(ldt1.h, ldt1.n)

        ldt1.n += 16 * (len(P1_CONSTS) + 2)
        blk(ctx, sync=f_ld1)
        st = stats(ctx, t["X"], "ln", es)
        q_fm = ctx.sbuf("q_fm", [P, KC, T], BF16, es)
        attn_out = q_fm
        go_sb = ctx.sbuf("go_sb", [P, KC, 416], BF16, es)
        kvden = ctx.sbuf("kvden", [65, NHEAD * HD], F32, es)

        for hg in range(4):
            msl = (hg * 256, (hg + 1) * 256)
            with ExitStack() as eh:
                q_tm = ctx.sbuf("q_tm", [P, NTC, 256], BF16, eh)
                k_tm = ctx.sbuf("k_tm", [P, NTC, 256], BF16, eh)
                v_ext = ctx.sbuf("v_ext", [P, NTC, 260], BF16, eh)

                for wname, kind in [("L0_Wq", "q"), ("L0_Wk", "k"), ("L0_Wv", "v")]:
                    with ExitStack() as ew:
                        wd = load_w(ctx, ctx.io[wname], msl, F32R, ew,
                                    name="w" + kind, want_sum=True)
                        if kind == "v":
                            def evac(vector, ps, tc, j):
                                dst = v_ext[:, tc, :].rearrange(
                                    "p (h c) -> p h c", c=65)[:, :, 0:64]
                                return vector.tensor_scalar(
                                    dst, ps.rearrange("p (h e) -> p h e", e=64),
                                    st["rstd_col"][:, tc:tc + 1], None, ALU.mult)
                        else:
                            dstt = q_tm if kind == "q" else k_tm

                            def evac(vector, ps, tc, j, dstt=dstt):
                                return vector.tensor_scalar(
                                    dstt[:, tc, :], ps,
                                    st["rstd_col"][:, tc:tc + 1], None, ALU.mult)
                        proj_tm(ctx, wd, st, t["X"], 256, evac)

                def f_ones(vector):
                    vector.memset(v_ext[:, :, :].rearrange(
                        "p tc (h c) -> p tc h c", c=65)[:, :, :, 64:65], 1.0)
                blk(ctx, vector=f_ones)

                ra = ctx.sbuf("ra", [P, 256], BF16, eh)
                rb = ctx.sbuf("rb", [P, 256], BF16, eh)

                def f_rope(vector):
                    for dst in (q_tm, k_tm):
                        for tc in range(NTC):
                            x = dst[:, tc, :]
                            xr = x.rearrange("p (h half e) -> p h half e",
                                             half=2, e=32)[:, :, ::-1, :]
                            C = t["c_ropeC"][:, tc, None, :].broadcast_to([P, 4, HD])
                            S = t["c_ropeS"][:, tc, None, :].broadcast_to(
                                [P, 4, HD]).rearrange("p h (half e) -> p h half e",
                                                      half=2)
                            vector.tensor_mul(
                                ra[:, :].rearrange("p (h e) -> p h e", e=64),
                                x.rearrange("p (h e) -> p h e", e=64), C)
                            vb = vector.tensor_mul(
                                rb[:, :].rearrange("p (h half e) -> p h half e",
                                                   half=2, e=32), xr, S)
                            edge(ctx, vector, vb, "edv")
                            edge(ctx, vector,
                                 vector.tensor_add(x, ra[:, :], rb[:, :]), "edv")

                blk(ctx, vector=f_rope)

                def e_qtr(tensor, job, bi):
                    tc, pc = job
                    return tensor.transpose(ctx.bankb(bi)[:, 0:128],
                                            q_tm[:, tc, pc * 128:(pc + 1) * 128],
                                            t["eye_b"][:, :])

                def e_qev(vector, job, bi):
                    tc, pc = job
                    return vector.tensor_copy(q_fm[:, hg * 2 + pc, ts128(tc)],
                                              ctx.bankb(bi)[:, 0:128])

                pipe(ctx, [(tc, pc) for tc in range(NTC) for pc in range(2)],
                     e_qtr, e_qev, banks=(0, 1))

                E_sb = q_tm

                def f_exp(scalar):
                    scalar.activation(E_sb[:, :, :], k_tm[:, :, :], AF.Exp)
                blk(ctx, scalar=f_exp)

                def f_emask(vector):
                    vector.tensor_mul(
                        E_sb[:, :, :], E_sb[:, :, :],
                        t["c_mloc16"][:, :, None].broadcast_to([P, NTC, 256]))
                blk(ctx, vector=f_emask)

                def e_kv(tensor, h, bi):
                    ins = None
                    for tc in range(NTC):
                        ins = tensor.matmul(ctx.bank(bi)[0:65, 0:64],
                                            v_ext[:, tc, h * 65:h * 65 + 65],
                                            E_sb[:, tc, h * 64:(h + 1) * 64],
                                            start=(tc == 0), stop=(tc == NTC - 1))
                    return ins

                def e_kvev(vector, h, bi):
                    hh = hg * 4 + h
                    return vector.tensor_copy(kvden[0:65, hh * 64:(hh + 1) * 64],
                                              ctx.bank(bi)[0:65, 0:64])

                pipe(ctx, list(range(4)), e_kv, e_kvev, banks=(0, 1))

                kg_sb = ctx.sbuf("kg_sb", [P, 2, 512], BF16, eh)
                sg_sb = ctx.sbuf("sg_sb", [P, 2, 4, 416], BF16, eh)
                sgT_sb = ctx.sbuf("sgT_sb", [P, 2, 4, 416], BF16, eh)
                mrow = ctx.sbuf("mrow", [P, 2, 4, 2], F32, eh)
                ic_sizes = [128, 128, 128, 25]
                jc_sizes = [128, 128, 128, 25]

                for rnd in range(2):
                    heads = [rnd * 2, rnd * 2 + 1]

                    def e_kgt2(tensor, job, bi):
                        hh, jc = job
                        return tensor.transpose(
                            ctx.bankb(bi)[0:64, 0:jc_sizes[jc]],
                            k_tm[0:jc_sizes[jc], jc,
                                 heads[hh] * 64:(heads[hh] + 1) * 64],
                            t["eye_b"][0:jc_sizes[jc], 0:jc_sizes[jc]])

                    def e_kgev2(vector, job, bi):
                        hh, jc = job
                        hpar = ((hg * 4 + heads[hh]) % 2) * 64
                        return vector.tensor_copy(
                            kg_sb[hpar:hpar + 64, hh, jc * 128:jc * 128 + jc_sizes[jc]],
                            ctx.bankb(bi)[0:64, 0:jc_sizes[jc]])

                    pipe(ctx, [(hh, jc) for hh in range(2) for jc in range(4)],
                         e_kgt2, e_kgev2, banks=(0, 1))

                    def f_scores(tensor):
                        for hh in range(2):
                            h = heads[hh]
                            hchunk = (hg * 4 + h) // 2
                            hpar = ((hg * 4 + h) % 2) * 64
                            for ic in range(4):
                                icw = ic_sizes[ic]
                                tensor.matmul(
                                    ctx.bank(hh * 4 + ic)[0:icw, 0:G],
                                    q_fm[hpar:hpar + 64, hchunk,
                                         ic * 128:ic * 128 + icw],
                                    kg_sb[hpar:hpar + 64, hh, 0:G],
                                    start=True, stop=True)
                    blk(ctx, tensor=f_scores)

                    def f_smax(vector):
                        for hh in range(2):
                            for ic in range(4):
                                icw = ic_sizes[ic]
                                bi = hh * 4 + ic
                                edge(ctx, vector, vector.reduce_max(
                                    mrow[0:icw, hh, ic, 0:1],
                                    ctx.bank(bi)[0:icw, 0:G], axis=AX), "edv")
                                edge(ctx, vector, vector.tensor_scalar(
                                    mrow[0:icw, hh, ic, 1:2],
                                    mrow[0:icw, hh, ic, 0:1],
                                    -0.125, None, ALU.mult), "edv")
                    blk(ctx, vector=f_smax)

                    def f_sexp(scalar):
                        for hh in range(2):
                            for ic in range(4):
                                icw = ic_sizes[ic]
                                bi = hh * 4 + ic
                                scalar.activation(
                                    sg_sb[0:icw, hh, ic, 0:G],
                                    ctx.bank(bi)[0:icw, 0:G], AF.Exp,
                                    bias=mrow[0:icw, hh, ic, 1:2], scale=0.125)
                    blk(ctx, scalar=f_sexp)

                    def f_snorm(vector):
                        for hh in range(2):
                            for ic in range(4):
                                icw = ic_sizes[ic]
                                sl = sg_sb[0:icw, hh, ic, 0:G]
                                edge(ctx, vector, vector.reduce_sum(
                                    mrow[0:icw, hh, ic, 0:1], sl, axis=AX), "edv")
                                edge(ctx, vector, vector.reciprocal(
                                    mrow[0:icw, hh, ic, 1:2],
                                    mrow[0:icw, hh, ic, 0:1]), "edv")
                                edge(ctx, vector, vector.tensor_scalar(
                                    sl, sl, mrow[0:icw, hh, ic, 1:2],
                                    None, ALU.mult), "edv")
                    blk(ctx, vector=f_snorm)

                    def e_st(tensor, job, bi):
                        hh, jc = job
                        ins = None
                        for ic in range(4):
                            icw = ic_sizes[ic]
                            ins = tensor.transpose(
                                ctx.bankb(bi)[0:jc_sizes[jc],
                                              ic * 128:ic * 128 + icw],
                                sg_sb[0:icw, hh, ic,
                                      jc * 128:jc * 128 + jc_sizes[jc]],
                                t["eye_b"][0:icw, 0:icw])
                        return ins

                    def e_stev(vector, job, bi):
                        hh, jc = job
                        return vector.tensor_copy(
                            sgT_sb[0:jc_sizes[jc], hh, jc, 0:416],
                            ctx.bankb(bi)[0:jc_sizes[jc], 0:416])

                    pipe(ctx, [(hh, jc) for hh in range(2) for jc in range(4)],
                         e_st, e_stev, banks=(0, 1))

                    def e_go(tensor, hh, bi):
                        h = heads[hh]
                        ins = None
                        for jc in range(4):
                            jw = jc_sizes[jc]
                            ins = tensor.matmul(
                                ctx.bank(bi)[0:64, 0:G],
                                v_ext[0:jw, jc, h * 65:h * 65 + 64],
                                sgT_sb[0:jw, hh, jc, 0:G],
                                start=(jc == 0), stop=(jc == 3))
                        return ins

                    def e_goev(vector, hh, bi):
                        h = heads[hh]
                        hid = hg * 4 + h
                        hchunk = hid // 2
                        hpar = (hid % 2) * 64
                        i1 = vector.tensor_copy(
                            sg_sb[0:64, 0, 0, 0:G], ctx.bank(bi)[0:64, 0:G])
                        edge(ctx, vector, i1, "edv")
                        return vector.tensor_mul(
                            go_sb[hpar:hpar + 64, hchunk, 0:G],
                            sg_sb[0:64, 0, 0, 0:G],
                            t["c_mglob"][0:64, 0:G])

                    pipe(ctx, [0, 1], e_go, e_goev, banks=(2, 3))

        stc = ctx.sem["st"]

        kv_tgt = stc.n + 16

        def f_kvout(sync):
            sync.dma_start(out=ctx.io["ag_kv_in"], in_=kvden[:, :]).then_inc(stc.h, 16)
            sync.wait_ge(stc.h, kv_tgt)
        stc.n = kv_tgt
        blk(ctx, sync=f_kvout)

        allgather(ctx, ctx.io["ag_kv_in"], ctx.io["ag_kv_out"])

        kvp = ctx.sbuf("kvp", [65, NHEAD * HD], F32, es)
        ldgn = ldg.n + 16

        def f_kvp(gpsimd):
            gpsimd.indirect_dma_start(
                kvp[:, :], None, ctx.io["ag_kv_out"],
                IndirectOffsetOnAxis(t["c_idx_kv"][0:65, 0:1], 0)).then_inc(ldg.h, 16)
            gpsimd.wait_ge(ldg.h, ldgn)
        ldg.n = ldgn
        blk(ctx, gpsimd=f_kvp)

        def f_kvadd(vector):
            vector.tensor_add(kvden[:, :], kvden[:, :], kvp[:, :])
        blk(ctx, vector=f_kvadd)

        kv_bf = ctx.sbuf("kv_bf", [P, KC, HD], BF16, es)
        rdeni = ctx.sbuf("rdeni", [HD, NHEAD], F32, es)

        def f_dent(tensor):
            for h in range(NHEAD):
                tensor.transpose(ctx.bank(0)[0:64, h:h + 1],
                                 kvden[64:65, h * 64:(h + 1) * 64],
                                 t["eye_f"][64:65, 64:65])
        blk(ctx, tensor=f_dent)

        def f_denev(vector):
            edge(ctx, vector, vector.tensor_copy(rdeni[:, :],
                                                 ctx.bank(0)[0:64, 0:NHEAD]), "edv")
            edge(ctx, vector, vector.reciprocal(rdeni[:, :], rdeni[:, :]), "edv")
            src4 = kvden[0:64, :].rearrange("p (hc par e) -> p hc par e",
                                            par=2, e=64)
            vector.tensor_copy(kv_bf[0:64, :, :], src4[:, :, 0, :])
            vector.tensor_copy(kv_bf[64:128, :, :], src4[:, :, 1, :])
        blk(ctx, vector=f_denev)

        def e_lo(tensor, job, bi):
            h, ib = job
            hchunk = h // 2
            hpar = (h % 2) * 64
            return tensor.matmul(ctx.bank(bi)[0:64, :],
                                 kv_bf[hpar:hpar + 64, hchunk, :],
                                 q_fm[hpar:hpar + 64, hchunk, ts512(ib)],
                                 start=True, stop=True)

        def e_loev(vector, job, bi):
            h, ib = job
            hchunk = h // 2
            hpar = (h % 2) * 64
            dst = attn_out[hpar:hpar + 64, hchunk, ts512(ib)]
            if ib > 0:
                return vector.tensor_scalar(dst, ctx.bank(bi)[0:64, :],
                                            rdeni[0:64, h:h + 1], None, ALU.mult)
            i1 = vector.tensor_scalar(sg_tmp[0:64, :], ctx.bank(bi)[0:64, :],
                                      rdeni[0:64, h:h + 1], None, ALU.mult)
            edge(ctx, vector, i1, "edv")
            i2 = vector.tensor_mul(sg_tmp[0:64, :], sg_tmp[0:64, :],
                                   t["c_mloc"][0:64, :])
            edge(ctx, vector, i2, "edv")
            i3 = vector.tensor_copy(dst, sg_tmp[0:64, :])
            edge(ctx, vector, i3, "edv")
            return vector.tensor_add(dst[:, 0:G], dst[:, 0:G],
                                     go_sb[hpar:hpar + 64, hchunk, 0:G])

        sg_tmp = ctx.sbuf("sg_tmp", [HD, 512], BF16, es)
        pipe(ctx, [(h, ib) for h in range(NHEAD) for ib in range(NTB)],
             e_lo, e_loev, banks=(0, 1))

        with ExitStack() as ew:
            wo = load_w(ctx, ctx.io["L0_Wout"], (0, D), BF16, ew, name="wo")

            def evac_wo(vector, ps, mc, tb):
                return vector.tensor_add(t["X"][:, mc, ts512(tb)],
                                         t["X"][:, mc, ts512(tb)], ps)

            proj_fm(ctx, wo["w"], attn_out, evac_wo)


def spipe(ctx, nslabs, jps, emit_load, emit_mm, emit_ev, ev_eng="vector",
          banks=(0, 1)):
    ldg, pe, ve = ctx.sem["ldg"], ctx.sem["pe"], ctx.sem["ve"]
    ldg0, pe0, ve0 = ldg.n, pe.n, ve.n
    nbuf = len(banks)

    def f_g(gpsimd):
        for s in range(nslabs):
            if s >= 2:
                gpsimd.wait_ge(pe.h, pe0 + (s - 1) * jps)
            emit_load(gpsimd, s, s % 2).then_inc(ldg.h, 16)

    def f_mm(tensor):
        i = 0
        for s in range(nslabs):
            tensor.wait_ge(ldg.h, ldg0 + 16 * (s + 1))
            for j in range(jps):
                if i >= nbuf:
                    tensor.wait_ge(ve.h, ve0 + i - nbuf + 1)
                emit_mm(tensor, s, j, banks[i % nbuf], s % 2).then_inc(pe.h, 1)
                i += 1

    def f_ev(eng):
        i = 0
        for s in range(nslabs):
            for j in range(jps):
                eng.wait_ge(pe.h, pe0 + i + 1)
                emit_ev(eng, s, j, banks[i % nbuf]).then_inc(ve.h, 1)
                i += 1

    ldg.n += 16 * nslabs
    pe.n += nslabs * jps
    ve.n += nslabs * jps
    blk(ctx, gpsimd=f_g, **{"tensor": f_mm, ev_eng: f_ev})


def dump_dbg(ctx, ap, cols):
    stc = ctx.sem["st"]
    tgt = stc.n + 16

    def f_d(gpsimd):
        gpsimd.dma_start(out=ctx.io["dbg"][:, 0:cols], in_=ap).then_inc(stc.h, 16)
        gpsimd.wait_ge(stc.h, tgt)
    stc.n = tgt
    blk(ctx, gpsimd=f_d)


def phase2(ctx):
    nc, t = ctx.nc, ctx.t
    pe, ve, ac, ldg = ctx.sem["pe"], ctx.sem["ve"], ctx.sem["ac"], ctx.sem["ldg"]
    with ExitStack() as es:
        xn = ctx.sbuf("xn", [P, KC, T], BF16, es)
        wrow = ctx.sbuf("wrow", [8, NTC, P], F32, es)

        with ExitStack() as eA:
            st = stats(ctx, t["X"], "lnln", eA, bcast=("mult", "mu", "plain"))
            gcol, bcol = t["c_L0_g"], t["c_L0_b"]

            t1 = ctx.sbuf("p2t1", [P, 512], F32R, eA)
            t2 = ctx.sbuf("p2t2", [P, 512], F32R, eA)

            def f_xn(vector):
                for kc in range(KC):
                    for tb in range(NTB):
                        i1 = vector.tensor_sub(t1[:, :], t["X"][:, kc, ts512(tb)],
                                               st["b_mu"][:, tb, :])
                        edge(ctx, vector, i1, "edv")
                        i2 = vector.tensor_mul(t2[:, :], t1[:, :], st["b_mult"][:, tb, :])
                        edge(ctx, vector, i2, "edv")
                        edge(ctx, vector,
                             vector.tensor_scalar(xn[:, kc, ts512(tb)], t2[:, :],
                                                  gcol[:, kc:kc + 1], bcol[:, kc:kc + 1],
                                                  ALU.mult, ALU.add), "edv")
            blk(ctx, vector=f_xn)

            logits = ctx.sbuf("logits", [P, NTC, 8], F32, eA)
            probs = ctx.sbuf("probs", [P, NTC, 8], F32, eA)
            mx = ctx.sbuf("mx", [P, NTC, 1], F32, eA)
            oh = ctx.sbuf("ohs", [P, NTC, 8], F32, eA)
            wdense = ctx.sbuf("wdense", [P, NTC, 8], F32, eA)

            with ExitStack() as ew:
                wd = load_w(ctx, ctx.io["L0_router"], (0, 8), F32R, ew,
                            name="wrt", want_sum=True, wcol=gcol, bcol=bcol)

                def evac_rt(vector, ps, tc, j):
                    return vector.tensor_scalar(logits[:, tc, :], ps[:, 0:8],
                                                st["rstd_col"][:, tc:tc + 1],
                                                None, ALU.mult)
                proj_tm(ctx, wd, st, t["X"], 8, evac_rt)

            if DEBUG == "logits":
                dump_dbg(ctx, logits[:, :, :].rearrange("p tc e -> p (tc e)"), NTC * 8)

            def f_resid(vector):
                for kc in range(KC):
                    for tb in range(NTB):
                        i1 = vector.tensor_sub(t1[:, :], t["X"][:, kc, ts512(tb)],
                                               st["b_mu"][:, tb, :])
                        edge(ctx, vector, i1, "edv")
                        i2 = vector.tensor_mul(t2[:, :], t1[:, :], st["b_plain"][:, tb, :])
                        edge(ctx, vector, i2, "edv")
                        edge(ctx, vector,
                             vector.tensor_add(t["X"][:, kc, ts512(tb)],
                                               t["X"][:, kc, ts512(tb)], t2[:, :]),
                             "edv")
            blk(ctx, vector=f_resid)

            def f_pexp(scalar):
                scalar.activation(probs[:, :, :], logits[:, :, :], AF.Exp,
                                  scale=1.0 / 1.5)
            blk(ctx, scalar=f_pexp)

            def f_top2(vector):
                edge(ctx, vector, vector.reduce_sum(mx[:, :, :], probs[:, :, :],
                                                    axis=AX), "edv")
                edge(ctx, vector, vector.reciprocal(mx[:, :, :], mx[:, :, :]), "edv")
                edge(ctx, vector, vector.tensor_mul(
                    probs[:, :, :], probs[:, :, :],
                    mx[:, :, :].broadcast_to([P, NTC, 8])), "edv")
                edge(ctx, vector, vector.reduce_max(mx[:, :, :], probs[:, :, :],
                                                    axis=AX), "edv")
                edge(ctx, vector, vector.tensor_tensor(
                    oh[:, :, :], probs[:, :, :],
                    mx[:, :, :].broadcast_to([P, NTC, 8]), ALU.is_equal), "edv")
                edge(ctx, vector, vector.tensor_mul(
                    wdense[:, :, :], oh[:, :, :],
                    mx[:, :, :].broadcast_to([P, NTC, 8])), "edv")
                edge(ctx, vector, vector.tensor_scalar(
                    oh[:, :, :], oh[:, :, :], -1.0, 1.0, ALU.mult, ALU.add), "edv")
                edge(ctx, vector, vector.tensor_mul(
                    probs[:, :, :], probs[:, :, :], oh[:, :, :]), "edv")
                edge(ctx, vector, vector.reduce_max(mx[:, :, :], probs[:, :, :],
                                                    axis=AX), "edv")
                edge(ctx, vector, vector.tensor_tensor(
                    oh[:, :, :], probs[:, :, :],
                    mx[:, :, :].broadcast_to([P, NTC, 8]), ALU.is_equal), "edv")
                edge(ctx, vector, vector.tensor_mul(
                    oh[:, :, :], oh[:, :, :],
                    mx[:, :, :].broadcast_to([P, NTC, 8])), "edv")
                edge(ctx, vector, vector.tensor_add(
                    wdense[:, :, :], wdense[:, :, :], oh[:, :, :]), "edv")
            blk(ctx, vector=f_top2)

            def e_wt(tensor, tc, bi):
                return tensor.transpose(ctx.bank(bi)[0:8, 0:128],
                                        wdense[:, tc, :], t["eye_f"][:, :])

            def e_wtev(vector, tc, bi):
                return vector.tensor_copy(wrow[:, tc, :], ctx.bank(bi)[0:8, 0:128])

            pipe(ctx, list(range(NTC)), e_wt, e_wtev, banks=(0, 1))
            if DEBUG == "wdense":
                dump_dbg(ctx, wdense[:, :, :].rearrange("p tc e -> p (tc e)"), NTC * 8)

        if DEBUG == "noexp":
            return
        brow = ctx.sbuf("brow", [1, T], BF16, es)
        Bcur = ctx.sbuf("Bcur", [P, NTB, 512], BF16, es)
        tmp = ctx.sbuf("p2tmp", [P, 512], F32R, es)
        h = ctx.sbuf("hbuf", [P, 32, 1024], BF16, es)
        wslab = ctx.sbuf("wslab", [P, 64, P], BF16, es)
        w1sb = [wslab[:, 0:KC, :], wslab[:, 32:32 + KC, :]]
        w2sb = [wslab[:, 0:32, :], wslab[:, 32:64, :]]

        for e in range(8):
            stc = ctx.sem["st"]
            br_tgt = stc.n + 16

            def f_brow(gpsimd, e=e, br_tgt=br_tgt):
                gpsimd.dma_start(out=brow[0:1, :],
                                 in_=wrow[e:e + 1, :, :].rearrange("o tc p -> o (tc p)")
                                 ).then_inc(stc.h, 16)
                gpsimd.wait_ge(stc.h, br_tgt)
            stc.n = br_tgt
            blk(ctx, gpsimd=f_brow)

            def e_bmm(tensor, tb, bi):
                return tensor.matmul(ctx.bank(bi), t["c_ones_rowb"][0:1, 0:P],
                                     brow[0:1, ts512(tb)], start=True, stop=True)

            def e_bev(vector, tb, bi):
                return vector.tensor_copy(Bcur[:, tb, :], ctx.bank(bi))

            pipe(ctx, list(range(NTB)), e_bmm, e_bev, banks=(0, 1))

            for tb2 in range(2):
                def el_h(gpsimd, s, buf, e=e):
                    return gpsimd.dma_start(
                        out=w1sb[buf][:, :, :].rearrange("p kc m -> p (kc m)"),
                        in_=ctx.io["L0_W1H"][e, s].rearrange("p kc m -> p (kc m)"))

                def em_h(tensor, s, j, bi, buf, tb2=tb2):
                    ins = None
                    for kc in range(KC):
                        ins = tensor.matmul(
                            ctx.bank(bi), w1sb[buf][:, kc, :],
                            xn[:, kc, tb2 * 1024 + j * 512:tb2 * 1024 + (j + 1) * 512],
                            start=(kc == 0), stop=(kc == KC - 1))
                    return ins

                def ev_h(scalar, s, j, bi):
                    return scalar.activation(h[:, s, ts512(j)], ctx.bank(bi), AF.Silu)

                spipe(ctx, 32, 2, el_h, em_h, ev_h, ev_eng="scalar", banks=(0, 1))

                def el_y(gpsimd, s, buf, e=e):
                    return gpsimd.dma_start(
                        out=w2sb[buf][:, :, :].rearrange("p fc m -> p (fc m)"),
                        in_=ctx.io["L0_W2H"][e, s].rearrange("p fc m -> p (fc m)"),
                        max_dma_last_dim=2048)

                def em_y(tensor, s, j, bi, buf):
                    ins = None
                    for fc in range(32):
                        ins = tensor.matmul(ctx.bank(bi), w2sb[buf][:, fc, :],
                                            h[:, fc, ts512(j)],
                                            start=(fc == 0), stop=(fc == 31))
                    return ins

                def ev_y(vector, s, j, bi, tb2=tb2):
                    tb = tb2 * 2 + j
                    i1 = vector.tensor_mul(tmp[:, :], ctx.bank(bi), Bcur[:, tb, :])
                    edge(ctx, vector, i1, "edv")
                    return vector.tensor_add(t["X"][:, s, ts512(tb)],
                                             t["X"][:, s, ts512(tb)], tmp[:, :])

                spipe(ctx, 8, 2, el_y, em_y, ev_y, ev_eng="vector", banks=(2, 3))



def phase3(ctx):
    nc, t = ctx.nc, ctx.t
    pe, ve, ac, ldg = ctx.sem["pe"], ctx.sem["ve"], ctx.sem["ac"], ctx.sem["ldg"]
    stc, cc = ctx.sem["st"], ctx.sem["cc"]
    psA, psB = t["psA"], t["psB"]
    with ExitStack() as es:
        rows3 = ctx.sbuf("rows3", [1, 8, 16], F32, es)
        with ExitStack() as eA:
            st = stats(ctx, t["X"], "lnln", eA, bcast=("mult", "mu", "plain"))
            qk_sb = ctx.sbuf("qk_sb", [P, KC, T], BF16, eA)
            q2own = ctx.sbuf("q2own", [P, NTC, 16], F32, eA)
            sqq = ctx.sbuf("sqq", [P, T], F32R, eA)
            q2st = st["_rows"][0:1, 0, :]

            for wname in ["L1_Wqk_t", "L1_Wv_t"]:
                isv = wname == "L1_Wv_t"
                for quarter in range(4):
                    m0 = quarter * 2
                    with ExitStack() as ew:
                        wd = load_w(ctx, ctx.io[wname], (m0 * 128, (m0 + 2) * 128),
                                    F32R, ew, name="w3", want_sum=True,
                                    wcol=t["c_L1_ln1_w"], bcol=t["c_L1_ln1_b"])

                        def evac(vector, ps, mc, tb):
                            return vector.tensor_mul(qk_sb[:, mc, ts512(tb)], ps,
                                                     st["b_mult"][:, tb, :])
                        proj_fm(ctx, wd["w"], t["X"], evac, mcs=range(m0, m0 + 2),
                                st=st, wd=wd, m0=m0)

                if not isv:
                    for kc in range(KC):
                        def f_sq(scalar, kc=kc):
                            scalar.activation(sqq[:, :], qk_sb[:, kc, :], AF.Square)
                        blk(ctx, scalar=f_sq)
                        for par in range(2):
                            h = kc * 2 + par
                            base = par * 64

                            def f_rmm(tensor, base=base):
                                for tb in range(NTB):
                                    tensor.matmul(psA[0:1, tb, :],
                                                  t["ones_col"][base:base + 64, 0:1],
                                                  sqq[base:base + 64, ts512(tb)],
                                                  start=True, stop=True)
                            blk(ctx, tensor=f_rmm)

                            def f_rev(vector):
                                for tb in range(NTB):
                                    vector.tensor_copy(q2st[0:1, ts512(tb)],
                                                       psA[0:1, tb, :])
                            blk(ctx, vector=f_rev)

                            def f_rtr(tensor):
                                for tc in range(NTC):
                                    tensor.transpose(psB[:, 0, tc:tc + 1],
                                                     q2st[0:1, ts128(tc)],
                                                     t["eye_f"][0:1, 0:1])
                            blk(ctx, tensor=f_rtr)

                            def f_q2ev(vector, h=h):
                                vector.tensor_copy(q2own[:, :, h],
                                                   psB[:, 0, 0:NTC])
                            blk(ctx, vector=f_q2ev)

                    tg1 = stc.n + 32

                    def f_out1(sync):
                        sync.dma_start(
                            out=ctx.io["ag_q2_in"],
                            in_=q2own[:, :, :].rearrange("p tc h -> p (tc h)")
                        ).then_inc(stc.h, 16)
                        sync.dma_start(
                            out=ctx.io["ag_qkv_in"][0:D, :].rearrange(
                                "(kc p) t -> p kc t", p=P),
                            in_=qk_sb[:, :, :]).then_inc(stc.h, 16)
                        sync.wait_ge(stc.h, tg1)
                    stc.n = tg1
                    blk(ctx, sync=f_out1)

            tg2 = stc.n + 16

            def f_out2(sync):
                sync.dma_start(
                    out=ctx.io["ag_qkv_in"][D:2 * D, :].rearrange(
                        "(kc p) t -> p kc t", p=P),
                    in_=qk_sb[:, :, :]).then_inc(stc.h, 16)
                sync.wait_ge(stc.h, tg2)
            stc.n = tg2
            blk(ctx, sync=f_out2)

            allgather(ctx, ctx.io["ag_qkv_in"], ctx.io["ag_qkv_out"])
            allgather(ctx, ctx.io["ag_q2_in"], ctx.io["ag_q2_out"])

            tg3 = stc.n + 16

            def f_spill(sync):
                sync.dma_start(out=ctx.io["xspill"].rearrange("(kc p) t -> p kc t", p=P),
                               in_=t["X"][:, :, :]).then_inc(stc.h, 16)
                sync.wait_ge(stc.h, tg3)
            stc.n = tg3
            blk(ctx, sync=f_spill)

            def f_xo(vector):
                for kc in range(KC):
                    for tb in range(NTB):
                        sl = t["X"][:, kc, ts512(tb)]
                        i1 = vector.tensor_sub(sqq[:, ts512(tb)], sl,
                                               st["b_mu"][:, tb, :])
                        edge(ctx, vector, i1, "edv")
                        edge(ctx, vector,
                             vector.tensor_mul(sl, sqq[:, ts512(tb)],
                                               st["b_plain"][:, tb, :]), "edv")
            blk(ctx, vector=f_xo)

        q2T = ctx.sbuf("q2T", [P, 2, NTC, 16], F32, es)
        q2m = ctx.sbuf("q2m", [P, 16], F32, es)
        bc3 = ctx.sbuf("bc3", [P, 4, 16], F32, es)
        bias_sb = ctx.sbuf("bias3", [P, 2, NTC, 16], F32, es)
        attn_heads = ctx.sbuf("ath", [P, KC, T], BF16, es)
        v_ext2 = [ctx.sbuf("v_ext3", [P, 32, 130], BF16, es) for _ in range(2)]
        v_fm2 = [ctx.sbuf("vfp", [P, 2, T], BF16, es) for _ in range(2)]
        vtmp2 = [ctx.sbuf("vtmp", [P, 32, P], BF16, es) for _ in range(2)]
        qk_head2 = [ctx.sbuf("qkh", [HD, 2, T], BF16, es) for _ in range(2)]
        rhs_tile2 = [ctx.sbuf("rhst", [HD, T], BF16, es) for _ in range(2)]
        E2 = ctx.sbuf("E2", [P, 2, T], BF16, es)
        denst = ctx.sbuf("denst", [1, 512], F32, es)
        tmp3 = ctx.sbuf("tmp3", [P, 512], F32R, es)

        tg4 = ldg.n + 32

        def f_gq2(gpsimd):
            for half in range(2):
                gpsimd.indirect_dma_start(
                    q2T[:, half, :, :].rearrange("p tc h -> p (tc h)"), None,
                    ctx.io["ag_q2_out"],
                    IndirectOffsetOnAxis(t["c_idx_q2"][:, half:half + 1], 0)
                ).then_inc(ldg.h, 16)
            gpsimd.wait_ge(ldg.h, tg4)
        ldg.n = tg4
        blk(ctx, gpsimd=f_gq2)

        def f_q2m(vector):
            for h in range(NHEAD):
                vector.tensor_reduce(q2m[:, h:h + 1], q2T[:, :, :, h],
                                     mybir.AxisListType.XY, ALU.max)
        blk(ctx, vector=f_q2m)

        def f_q2t(tensor):
            tensor.transpose(psA[0:16, 0, 0:P], q2m[:, :], t["eye_f"][:, :])
        blk(ctx, tensor=f_q2t)

        q2mt = ctx.sbuf("q2mt", [16, P], F32, es)

        def f_q2r(vector):
            edge(ctx, vector, vector.tensor_copy(q2mt[:, :], psA[0:16, 0, 0:P]), "edv")
            edge(ctx, vector, vector.reduce_max(q2mt[:, 0:1], q2mt[:, :],
                                                axis=AX), "edv")
        blk(ctx, vector=f_q2r)

        def f_q2t2(tensor):
            tensor.transpose(psA[0:1, 0, 0:16], q2mt[0:16, 0:1],
                             t["eye_f"][0:16, 0:16])
        blk(ctx, tensor=f_q2t2)

        r3 = lambda i: rows3[0:1, i, :]

        def f_m0(vector):
            edge(ctx, vector, vector.tensor_copy(r3(0), t["c_L1_tau_t"][0:1, :]), "edv")
            edge(ctx, vector, vector.tensor_scalar(r3(1), r3(0), 0.125, None, ALU.mult), "edv")
            edge(ctx, vector, vector.tensor_scalar(r3(2), r3(1), -1.0, None, ALU.mult), "edv")
            edge(ctx, vector, vector.tensor_copy(r3(3), psA[0:1, 0, 0:16]), "edv")
            edge(ctx, vector, vector.tensor_scalar(r3(4), r3(1), -1.0, None, ALU.mult), "edv")
            edge(ctx, vector, vector.tensor_max(r3(4), r3(4), r3(1)), "edv")
            edge(ctx, vector, vector.tensor_scalar(r3(5), r3(1), 0.0, None, ALU.is_gt), "edv")
            edge(ctx, vector, vector.tensor_scalar(r3(5), r3(5), -2.0, 3.0, ALU.mult, ALU.add), "edv")
            edge(ctx, vector, vector.tensor_mul(r3(6), r3(4), r3(3)), "edv")
            edge(ctx, vector, vector.tensor_mul(r3(6), r3(6), r3(5)), "edv")
            edge(ctx, vector, vector.tensor_scalar(r3(7), r3(1), 2.0, None, ALU.mult), "edv")
        blk(ctx, vector=f_m0)

        def e_b3(tensor, i, bi):
            sl = [7, 2, 6][i]
            return tensor.matmul(ctx.bank(bi)[:, 0:16], t["ones_row"][0:1, 0:P],
                                 rows3[0:1, sl, :], start=True, stop=True)

        def e_b3ev(vector, i, bi):
            return vector.tensor_copy(bc3[:, i, :], ctx.bank(bi)[:, 0:16])

        pipe(ctx, [0, 1, 2], e_b3, e_b3ev, banks=(0, 1))

        def f_bias(vector):
            for h in range(NHEAD):
                vector.tensor_scalar(
                    bias_sb[:, :, :, h].rearrange("p a b -> p (a b)"),
                    q2T[:, :, :, h].rearrange("p a b -> p (a b)"),
                    bc3[:, 1, h:h + 1], bc3[:, 2, h:h + 1],
                    ALU.mult, ALU.subtract)
        blk(ctx, vector=f_bias)

        def f_vones(vector):
            for ve_ in v_ext2:
                vector.memset(ve_[:, :, :].rearrange(
                    "p j (g c) -> p j g c", c=65)[:, :, :, 64:65], 1.0)
        blk(ctx, vector=f_vones)

        vg_tgt = [0] * 9
        vt_tgt = [0] * 9
        qk_tgt = [0] * 17

        def emit_gv(gpsimd, p):
            for si in range(2):
                gpsimd.indirect_dma_start(
                    v_fm2[p % 2][:, si, :], None, ctx.io["ag_qkv_out"],
                    IndirectOffsetOnAxis(
                        t["c_idx_v"][:, p * 2 + si:p * 2 + si + 1], 0)
                ).then_inc(ldg.h, 16)
            ldg.n += 32
            vg_tgt[p] = ldg.n

        def emit_gqk(gpsimd, hh):
            for si in range(2):
                gpsimd.indirect_dma_start(
                    qk_head2[hh % 2][:, si, :], None, ctx.io["ag_qkv_out"],
                    IndirectOffsetOnAxis(
                        t["c_idx_qk"][:, 2 * hh + si:2 * hh + si + 1], 0)
                ).then_inc(ldg.h, 16)
            gpsimd.indirect_dma_start(
                rhs_tile2[hh % 2][:, :], None, ctx.io["ag_qkv_out"],
                IndirectOffsetOnAxis(t["c_idx_rhs"][:, hh:hh + 1], 0)
            ).then_inc(ldg.h, 16)
            ldg.n += 48
            qk_tgt[hh] = ldg.n

        def emit_vtr(sync, p):
            sync.wait_ge(ldg.h, vg_tgt[p])
            for half in range(2):
                for tcl in range(NTC):
                    sync.dma_start(out=vtmp2[p % 2][:, half * NTC + tcl, :],
                                   in_=v_fm2[p % 2][:, half, ts128(tcl)],
                                   transpose=True).then_inc(stc.h, 16)
            stc.n += 16 * 32
            vt_tgt[p] = stc.n

        def f_pro(gpsimd):
            emit_gv(gpsimd, 0)
            emit_gqk(gpsimd, 0)
            gpsimd.wait_ge(ldg.h, qk_tgt[0])
        blk(ctx, gpsimd=f_pro)

        def f_pro2(sync):
            emit_vtr(sync, 0)
            sync.wait_ge(stc.h, vt_tgt[0])
        blk(ctx, sync=f_pro2)

        def f_pro3(vector):
            vector.tensor_copy(
                v_ext2[0][:, :, :].rearrange("p j (g c) -> p j g c",
                                             c=65)[:, :, :, 0:64],
                vtmp2[0][:, :, :].rearrange("p j (g c) -> p j g c", c=64))
        blk(ctx, vector=f_pro3)

        for h in range(NHEAD):
            c2 = h // 2
            v_ext = v_ext2[c2 % 2]
            qk_head = qk_head2[h % 2]
            rhs_tile = rhs_tile2[h % 2]

            def f_pref(gpsimd, h=h, c2=c2):
                if h + 1 < NHEAD:
                    emit_gqk(gpsimd, h + 1)
                if h % 2 == 0 and c2 + 1 < 8:
                    emit_gv(gpsimd, c2 + 1)
                gpsimd.wait_ge(ldg.h, max(qk_tgt[h], vg_tgt[c2]))

            def f_pref_t(sync, h=h, c2=c2):
                if h % 2 == 1 and c2 + 1 < 8:
                    emit_vtr(sync, c2 + 1)

            def f_pref_v(vector, h=h, c2=c2):
                pass
            blk(ctx, gpsimd=f_pref, sync=f_pref_t)

            ac0, pe0 = ac.n, pe.n
            gslot = h % 2
            NU = 64

            def dot_count(u):
                return u + 1 if u < 2 else 2 * u

            def f_att_t(tensor, h=h, ac0=ac0, pe0=pe0, gslot=gslot):
                def emit_av(u):
                    jc, hf = u // 2, u % 2
                    ins = None
                    for ib in range(2):
                        ins = tensor.matmul(
                            psB[0:65, hf * 2 + ib, :],
                            v_ext[:, jc, gslot * 65:gslot * 65 + 65],
                            E2[:, u % 2, ts512(ib)],
                            start=(jc == 0), stop=(jc == 31))
                    ins.then_inc(pe.h, 1)

                for u in range(NU):
                    jc, hf = u // 2, u % 2
                    if u >= 2:
                        tensor.wait_ge(ac.h, ac0 + u - 1)
                        emit_av(u - 2)
                    lhsT = qk_head[0:64, jc // NTC, ts128(jc % NTC)]
                    ins = None
                    for ib in range(2):
                        ins = tensor.matmul(psA[:, 2 * (u % 2) + ib, :], lhsT,
                                            rhs_tile[0:64, ts512(hf * 2 + ib)],
                                            start=True, stop=True)
                    ins.then_inc(pe.h, 1)
                for u in (NU - 2, NU - 1):
                    tensor.wait_ge(ac.h, ac0 + u + 1)
                    emit_av(u)

            def f_att_s(scalar, h=h, ac0=ac0, pe0=pe0):
                for u in range(NU):
                    jc = u // 2
                    scalar.wait_ge(pe.h, pe0 + dot_count(u))
                    scalar.activation(E2[:, u % 2, 0:1024],
                                      psA[:, 2 * (u % 2):2 * (u % 2) + 2, :],
                                      AF.Exp,
                                      bias=bias_sb[:, jc // NTC, jc % NTC, h:h + 1],
                                      scale=bc3[:, 0, h:h + 1]).then_inc(ac.h, 1)

            ac.n += NU
            pe.n += 2 * NU
            blk(ctx, tensor=f_att_t, scalar=f_att_s)

            hchunk, hpar = h // 2, (h % 2) * 64

            pe0e, ve0e = pe.n, ve.n

            def f_ev_t(tensor):
                for ib in range(NTB):
                    tensor.wait_ge(ve.h, ve0e + 2 * ib + 1)
                    tensor.matmul(psA[:, 0, :], t["ones_row"][0:1, 0:P],
                                  denst[0:1, :], start=True,
                                  stop=True).then_inc(pe.h, 1)

            def f_ev_v(vector, h=h, c2=c2):
                if h % 2 == 1 and c2 + 1 < 8:
                    vector.wait_ge(stc.h, vt_tgt[c2 + 1])
                    vector.tensor_copy(
                        v_ext2[(c2 + 1) % 2][:, :, :].rearrange(
                            "p j (g c) -> p j g c", c=65)[:, :, :, 0:64],
                        vtmp2[(c2 + 1) % 2][:, :, :].rearrange(
                            "p j (g c) -> p j g c", c=64))
                for ib in range(NTB):
                    i1 = vector.tensor_copy(denst[0:1, :], psB[64:65, ib, :])
                    edge(ctx, vector, i1, "edv")
                    vector.reciprocal(denst[0:1, :],
                                      denst[0:1, :]).then_inc(ve.h, 1)
                    edge(ctx, vector, vector.tensor_copy(
                        attn_heads[hpar:hpar + 64, hchunk, ts512(ib)],
                        psB[0:64, ib, :]), "edv")
                    vector.wait_ge(pe.h, pe0e + ib + 1)
                    vector.tensor_mul(
                        attn_heads[hpar:hpar + 64, hchunk, ts512(ib)],
                        attn_heads[hpar:hpar + 64, hchunk, ts512(ib)],
                        psA[0:64, 0, :]).then_inc(ve.h, 1)

            pe.n += NTB
            ve.n += 2 * NTB
            blk(ctx, tensor=f_ev_t, vector=f_ev_v)

        with ExitStack() as ew:
            wo = load_w(ctx, ctx.io["L1_Wo_t"], (0, D), BF16, ew, name="wo3")

            def evac_wo(vector, ps, mc, tb):
                i1 = vector.tensor_scalar(tmp3[:, :], ps,
                                          t["c_L1_g1"][:, mc:mc + 1], None, ALU.mult)
                edge(ctx, vector, i1, "edv")
                return vector.tensor_add(t["X"][:, mc, ts512(tb)],
                                         t["X"][:, mc, ts512(tb)], tmp3[:, :])

            proj_fm(ctx, wo["w"], attn_heads, evac_wo)



def phase4(ctx):
    nc, t = ctx.nc, ctx.t
    pe, ve, ac, ldg = ctx.sem["pe"], ctx.sem["ve"], ctx.sem["ac"], ctx.sem["ldg"]
    stc = ctx.sem["st"]
    psA, psB = t["psA"], t["psB"]
    with ExitStack() as es:
        qk_tm = ctx.sbuf("qk_tm", [P, NTC, D], BF16, es)
        eB = ExitStack()
        st4 = stats(ctx, t["X"], "ln", eB, bcast=("mult",), nsq=1)
        v_c = ctx.sbuf("v_c", [P, KC, T], BF16, eB)
        dot_sb = ctx.sbuf("dot_sb", [P, KC, P], F32, eB)
        rows4 = ctx.sbuf("rows4", [1, 2, 8], F32, eB)
        sgc = ctx.sbuf("sgc", [P, 8], F32, eB)
        q2c = ctx.sbuf("q2c", [P, 1], F32, eB)
        t4 = ctx.sbuf("t4", [P, P], F32, eB)
        lt = ctx.sbuf("lt", [P, P], F32, eB)
        sm4 = ctx.sbuf("sm4", [P, P], F32, eB)
        attT = ctx.sbuf("attT", [P, P], BF16, eB)
        q2rst = ctx.sbuf("q2rst", [1, P], F32, eB)
        mx4 = ctx.sbuf("mx4", [P, 2], F32, eB)

        for wname, istm in [("L1_Wqk_c", True), ("L1_Wv_c", False)]:
            for m0 in range(KC):
                with ExitStack() as ew:
                    wd = load_w(ctx, ctx.io[wname], (m0 * 128, (m0 + 1) * 128),
                                F32R, ew, name="w4", want_sum=True,
                                wcol=t["c_L1_ln2_w"], bcol=t["c_L1_ln2_b"])
                    if istm:
                        def evac_tm(vector, ps, tc, j, m0=m0):
                            return vector.tensor_scalar(
                                qk_tm[:, tc, m0 * 128:(m0 + 1) * 128], ps,
                                st4["rstd_col"][:, tc:tc + 1], None, ALU.mult)
                        proj_tm(ctx, wd, st4, t["X"], 128, evac_tm)
                    else:
                        def evac_fm(vector, ps, mc, tb):
                            return vector.tensor_mul(v_c[:, mc, ts512(tb)], ps,
                                                     st4["b_mult"][:, tb, :])
                        proj_fm(ctx, wd["w"], t["X"], evac_fm,
                                mcs=[m0], st=st4, wd=wd, m0=m0)

        def e_dot(tensor, g, bi):
            ins = None
            for tc in range(NTC):
                ins = tensor.matmul(ctx.bank(bi)[:, 0:P],
                                    qk_tm[:, tc, ts128(g)],
                                    qk_tm[:, tc, ts128(g)],
                                    start=(tc == 0), stop=(tc == NTC - 1))
            return ins

        def e_dotev(vector, g, bi):
            return vector.tensor_copy(dot_sb[:, g, :], ctx.bank(bi)[:, 0:P])

        pipe(ctx, list(range(KC)), e_dot, e_dotev, banks=(0, 1))

        tg = stc.n + 16

        def f_dout(sync, tg=tg):
            sync.dma_start(out=ctx.io["ag_cd_in"],
                           in_=dot_sb[:, :, :].rearrange("p g c -> p (g c)")
                           ).then_inc(stc.h, 16)
            sync.wait_ge(stc.h, tg)
        stc.n = tg
        blk(ctx, sync=f_dout)

        allgather(ctx, ctx.io["ag_cd_in"], ctx.io["ag_cd_out"])

        dotp = ctx.sbuf("dotp", [P, KC, P], F32, eB)
        tg = ldg.n + 16

        def f_gdot(gpsimd, tg=tg):
            gpsimd.indirect_dma_start(
                dotp[:, :, :].rearrange("p g c -> p (g c)"), None,
                ctx.io["ag_cd_out"],
                IndirectOffsetOnAxis(t["c_idx_cd"][:, 0:1], 0)).then_inc(ldg.h, 16)
            gpsimd.wait_ge(ldg.h, tg)
        ldg.n = tg
        blk(ctx, gpsimd=f_gdot)

        def f_dadd(vector):
            vector.tensor_add(dot_sb[:, :, :], dot_sb[:, :, :], dotp[:, :, :])
        blk(ctx, vector=f_dadd)

        r4 = lambda i: rows4[0:1, i, :]

        def f_tau(vector):
            edge(ctx, vector, vector.tensor_scalar(r4(0), t["c_L1_tau_c"][0:1, :],
                                                   1.0 / 64.0, None, ALU.mult), "edv")
        blk(ctx, vector=f_tau)

        def f_taub(tensor):
            tensor.matmul(psA[:, 0, 0:8], t["ones_row"][0:1, 0:P], r4(0),
                          start=True, stop=True)
        blk(ctx, tensor=f_taub)

        def f_taubv(vector):
            vector.tensor_copy(sgc[:, :], psA[:, 0, 0:8])
        blk(ctx, vector=f_taubv)

        cd_alias = qk_tm[:, :, :].rearrange("p a b -> p (a b)").rearrange(
            "p (kc tt) -> p kc tt", tt=T)

        for g in range(KC):
            def fv1(vector, g=g):
                i1 = vector.tensor_mul(t4[:, :], dot_sb[:, g, :], t["eye_f"][:, :])
                edge(ctx, vector, i1, "edv")
                edge(ctx, vector, vector.reduce_sum(q2c[:, 0:1], t4[:, :],
                                                    axis=AX), "edv")
            blk(ctx, vector=fv1)

            def ft1(tensor):
                tensor.transpose(psA[0:1, 1, 0:P], q2c[:, 0:1], t["eye_f"][:, :])
            blk(ctx, tensor=ft1)

            def fv2(vector):
                vector.tensor_copy(q2rst[0:1, :], psA[0:1, 1, 0:P])
            blk(ctx, vector=fv2)

            def ft2(tensor):
                tensor.matmul(psA[:, 2, 0:P], t["ones_row"][0:1, 0:P],
                              q2rst[0:1, :], start=True, stop=True)
            blk(ctx, tensor=ft2)

            def fv3(vector, g=g):
                i1 = vector.tensor_scalar(lt[:, :], dot_sb[:, g, :], 2.0,
                                          q2c[:, 0:1], ALU.mult, ALU.subtract)
                edge(ctx, vector, i1, "edv")
                i2 = vector.tensor_sub(lt[:, :], lt[:, :], psA[:, 2, 0:P])
                edge(ctx, vector, i2, "edv")
                i3 = vector.tensor_scalar(lt[:, :], lt[:, :], sgc[:, g:g + 1],
                                          None, ALU.mult)
                edge(ctx, vector, i3, "edv")
                edge(ctx, vector, vector.reduce_max(mx4[:, 0:1], lt[:, :],
                                                    axis=AX), "edv")
                edge(ctx, vector, vector.tensor_scalar(mx4[:, 1:2], mx4[:, 0:1],
                                                       -1.0, None, ALU.mult), "edv")
            blk(ctx, vector=fv3)

            def fa1(scalar):
                scalar.activation(sm4[:, :], lt[:, :], AF.Exp,
                                  bias=mx4[:, 1:2])
            blk(ctx, scalar=fa1)

            def fv4(vector):
                edge(ctx, vector, vector.reduce_sum(mx4[:, 0:1], sm4[:, :],
                                                    axis=AX), "edv")
                edge(ctx, vector, vector.reciprocal(mx4[:, 0:1], mx4[:, 0:1]), "edv")
                edge(ctx, vector, vector.tensor_scalar(sm4[:, :], sm4[:, :],
                                                       mx4[:, 0:1], None,
                                                       ALU.mult), "edv")
            blk(ctx, vector=fv4)

            def ft3(tensor):
                tensor.transpose(ctx.bank(0)[:, 0:P], sm4[:, :], t["eye_f"][:, :])
            blk(ctx, tensor=ft3)

            def fv5(vector):
                vector.tensor_copy(attT[:, :], ctx.bank(0)[:, 0:P])
            blk(ctx, vector=fv5)

            def ft4(tensor, g=g):
                for tb in range(NTB):
                    tensor.matmul(psB[:, tb, :], attT[:, :],
                                  v_c[:, g, ts512(tb)], start=True, stop=True)
            blk(ctx, tensor=ft4)

            def fv6(vector, g=g):
                for tb in range(NTB):
                    vector.tensor_copy(cd_alias[:, g, ts512(tb)], psB[:, tb, :])
            blk(ctx, vector=fv6)

        eB.close()
        tmp4 = ctx.sbuf("tmp4", [P, 512], F32R, es)
        with ExitStack() as ew:
            wo = load_w(ctx, ctx.io["L1_Wo_c"], (0, D), BF16, ew, name="wo4")

            def evac_wo(vector, ps, mc, tb):
                i1 = vector.tensor_scalar(tmp4[:, :], ps,
                                          t["c_L1_g2"][:, mc:mc + 1], None, ALU.mult)
                edge(ctx, vector, i1, "edv")
                return vector.tensor_add(t["X"][:, mc, ts512(tb)],
                                         t["X"][:, mc, ts512(tb)], tmp4[:, :])

            proj_fm(ctx, wo["w"], cd_alias, evac_wo)

        xs = ctx.sbuf("xs4", [P, 2, T], F32R, es)
        ldt = ctx.sem["ldt"]
        ldt0 = ldt.n

        ve4 = ctx.sem["aux"]
        ve40 = ve4.n

        def f_back_d2(sync):
            for kc in range(KC):
                if kc >= 2:
                    sync.wait_ge(ve4.h, ve40 + kc - 1)
                sync.dma_start(
                    out=xs[:, kc % 2, :],
                    in_=ctx.io["xspill"].rearrange("(kc p) t -> p kc t", p=P)[:, kc, :]
                ).then_inc(ldt.h, 16)

        def f_back_v(vector):
            for kc in range(KC):
                vector.wait_ge(ldt.h, ldt0 + 16 * (kc + 1))
                vector.tensor_add(t["X"][:, kc, :], t["X"][:, kc, :],
                                  xs[:, kc % 2, :]).then_inc(ve4.h, 1)

        ldt.n += 16 * KC
        ve4.n += KC
        blk(ctx, sync=f_back_d2, vector=f_back_v)



def phase5(ctx):
    nc, t = ctx.nc, ctx.t
    pe, ve, ac, ldg = ctx.sem["pe"], ctx.sem["ve"], ctx.sem["ac"], ctx.sem["ldg"]
    stc, cc = ctx.sem["st"], ctx.sem["cc"]
    psA, psB = t["psA"], t["psB"]
    CAP = 4096.0
    with ExitStack() as es:
        xn = ctx.sbuf("xn5", [P, KC, T], BF16, es)
        wrow = ctx.sbuf("wrow5", [8, NTC, P], BF16, es)

        with ExitStack() as eA:
            st = stats(ctx, t["X"], "ln", eA, bcast=("mult", "mu"))
            t1 = ctx.sbuf("p5t1", [P, 512], F32R, eA)

            def f_xn(vector):
                for kc in range(KC):
                    for tb in range(NTB):
                        i1 = vector.tensor_sub(t1[:, :], t["X"][:, kc, ts512(tb)],
                                               st["b_mu"][:, tb, :])
                        edge(ctx, vector, i1, "edv")
                        edge(ctx, vector,
                             vector.tensor_mul(xn[:, kc, ts512(tb)], t1[:, :],
                                               st["b_mult"][:, tb, :]), "edv")
            blk(ctx, vector=f_xn)

            logits = ctx.sbuf("logits5", [P, NTC, 8], F32, eA)
            probs = ctx.sbuf("probs5", [P, NTC, 8], F32, eA)
            mx = ctx.sbuf("mx5", [P, NTC, 1], F32, eA)
            g1 = ctx.sbuf("g1_5", [P, NTC, 1], F32, eA)
            g2 = ctx.sbuf("g2_5", [P, NTC, 1], F32, eA)
            oh1 = ctx.sbuf("oh1_5", [P, NTC, 8], F32R, eA)
            oh2 = ctx.sbuf("oh2_5", [P, NTC, 8], F32R, eA)
            rank = ctx.sbuf("rank5", [P, NTC, 2, 8], F32, eA)
            G = ctx.sbuf("G5", [P, NTC, 8], F32, eA)
            offs = ctx.sbuf("offs5", [1, 2, 16], F32, eA)
            offb = ctx.sbuf("offb5", [P, 16], F32, eA)
            cnt_r = ctx.sbuf("cntr5", [8, 16], F32R, eA)

            with ExitStack() as ew:
                wd = load_w(ctx, ctx.io["L1_wg"], (0, 8), F32R, ew,
                            name="wg5", want_sum=True)

                def evac_rt(vector, ps, tc, j):
                    return vector.tensor_scalar(logits[:, tc, :], ps[:, 0:8],
                                                st["rstd_col"][:, tc:tc + 1],
                                                None, ALU.mult)
                proj_tm(ctx, wd, st, t["X"], 8, evac_rt)

            def f_pexp(scalar):
                scalar.activation(probs[:, :, :], logits[:, :, :], AF.Exp)
            blk(ctx, scalar=f_pexp)

            def f_top2(vector):
                edge(ctx, vector, vector.reduce_sum(mx[:, :, :], probs[:, :, :],
                                                    axis=AX), "edv")
                edge(ctx, vector, vector.reciprocal(mx[:, :, :], mx[:, :, :]), "edv")
                edge(ctx, vector, vector.tensor_mul(
                    probs[:, :, :], probs[:, :, :],
                    mx[:, :, :].broadcast_to([P, NTC, 8])), "edv")
                edge(ctx, vector, vector.reduce_max(g1[:, :, :], probs[:, :, :],
                                                    axis=AX), "edv")
                edge(ctx, vector, vector.tensor_tensor(
                    oh1[:, :, :], probs[:, :, :],
                    g1[:, :, :].broadcast_to([P, NTC, 8]), ALU.is_equal), "edv")
                edge(ctx, vector, vector.tensor_scalar(
                    G[:, :, :], oh1[:, :, :], -1.0, 1.0, ALU.mult, ALU.add), "edv")
                edge(ctx, vector, vector.tensor_mul(
                    probs[:, :, :], probs[:, :, :], G[:, :, :]), "edv")
                edge(ctx, vector, vector.reduce_max(g2[:, :, :], probs[:, :, :],
                                                    axis=AX), "edv")
                edge(ctx, vector, vector.tensor_tensor(
                    oh2[:, :, :], probs[:, :, :],
                    g2[:, :, :].broadcast_to([P, NTC, 8]), ALU.is_equal), "edv")
                edge(ctx, vector, vector.tensor_add(mx[:, :, :], g1[:, :, :],
                                                    g2[:, :, :]), "edv")
                edge(ctx, vector, vector.tensor_scalar(mx[:, :, :], mx[:, :, :],
                                                       1e-9, None, ALU.max), "edv")
                edge(ctx, vector, vector.reciprocal(mx[:, :, :], mx[:, :, :]), "edv")
                edge(ctx, vector, vector.tensor_mul(g1[:, :, :], g1[:, :, :],
                                                    mx[:, :, :]), "edv")
                edge(ctx, vector, vector.tensor_mul(g2[:, :, :], g2[:, :, :],
                                                    mx[:, :, :]), "edv")
            blk(ctx, vector=f_top2)

            def e_rank(tensor, job, bi):
                tc, k = job
                oh = oh1 if k == 0 else oh2
                ps = ctx.bank(bi)[:, 0:8]
                ins = None
                for tcp in range(tc + 1):
                    lhsT = (t["slt_r"][:, :] if tcp == tc else t["ones_sq"][:, :])
                    ins = tensor.matmul(ps, lhsT, oh[:, tcp, :],
                                        start=(tcp == 0), stop=(tcp == tc))
                return ins

            def e_rankev(vector, job, bi):
                tc, k = job
                return vector.tensor_copy(rank[:, tc, k, :], ctx.bank(bi)[:, 0:8])

            pipe(ctx, [(tc, k) for tc in range(NTC) for k in range(2)],
                 e_rank, e_rankev, banks=(0, 1))

            def f_cnt(tensor):
                for k in range(2):
                    oh = oh1 if k == 0 else oh2
                    for tc in range(NTC):
                        tensor.matmul(psA[0:1, 2 + k, 0:8],
                                      t["ones_col"][:, 0:1], oh[:, tc, :],
                                      start=(tc == 0), stop=(tc == NTC - 1))
            blk(ctx, tensor=f_cnt)

            def f_cntev(vector):
                vector.tensor_copy(offs[0:1, 0, 0:8], psA[0:1, 2, 0:8])
                vector.tensor_copy(offs[0:1, 0, 8:16], psA[0:1, 3, 0:8])
            blk(ctx, vector=f_cntev)

            tg = stc.n + 16

            def f_cout(sync, tg=tg):
                sync.dma_start(out=ctx.io["ag_cnt_in"],
                               in_=offs[0:1, 0, :]).then_inc(stc.h, 16)
                sync.wait_ge(stc.h, tg)
            stc.n = tg
            blk(ctx, sync=f_cout)

            allgather(ctx, ctx.io["ag_cnt_in"], ctx.io["ag_cnt_out"])

            tg = ldg.n + 16

            def f_cin(gpsimd, tg=tg):
                gpsimd.dma_start(out=cnt_r[:, :],
                                 in_=ctx.io["ag_cnt_out"]).then_inc(ldg.h, 16)
                gpsimd.wait_ge(ldg.h, tg)
            ldg.n = tg
            blk(ctx, gpsimd=f_cin)

            def f_off(tensor):
                tensor.matmul(psA[0:1, 2, 0:16], t["c_pmask"][:, 0:1],
                              cnt_r[:, :], start=True, stop=True)
                tensor.matmul(psA[0:1, 3, 0:8], t["ones_col"][0:8, 0:1],
                              cnt_r[:, 0:8], start=True, stop=True)
            blk(ctx, tensor=f_off)

            def f_offv(vector):
                i1 = vector.tensor_copy(offs[0:1, 1, :], psA[0:1, 2, 0:16])
                edge(ctx, vector, i1, "edv")
                edge(ctx, vector, vector.tensor_add(
                    offs[0:1, 1, 8:16], offs[0:1, 1, 8:16], psA[0:1, 3, 0:8]), "edv")
            blk(ctx, vector=f_offv)

            def f_offb_mm(tensor):
                tensor.matmul(psA[:, 2, 0:16], t["ones_row"][0:1, 0:P],
                              offs[0:1, 1, :], start=True, stop=True)
            blk(ctx, tensor=f_offb_mm)

            def f_offb_ev(vector):
                vector.tensor_copy(offb[:, :], psA[:, 2, 0:16])
            blk(ctx, vector=f_offb_ev)

            pos = probs

            def f_G(vector):
                for k in range(2):
                    oh = oh1 if k == 0 else oh2
                    gk = g1 if k == 0 else g2
                    i1 = vector.tensor_add(
                        pos[:, :, :], rank[:, :, k, :],
                        offb[:, None, k * 8:(k + 1) * 8].broadcast_to([P, NTC, 8]))
                    edge(ctx, vector, i1, "edv")
                    i2 = vector.tensor_scalar(pos[:, :, :], pos[:, :, :], CAP,
                                              None, ALU.is_lt)
                    edge(ctx, vector, i2, "edv")
                    i3 = vector.tensor_mul(pos[:, :, :], pos[:, :, :], oh[:, :, :])
                    edge(ctx, vector, i3, "edv")
                    i4 = vector.tensor_mul(pos[:, :, :], pos[:, :, :],
                                           gk[:, :, :].broadcast_to([P, NTC, 8]))
                    edge(ctx, vector, i4, "edv")
                    if k == 0:
                        i5 = vector.tensor_copy(G[:, :, :], pos[:, :, :])
                    else:
                        i5 = vector.tensor_add(G[:, :, :], G[:, :, :], pos[:, :, :])
                    edge(ctx, vector, i5, "edv")
            blk(ctx, vector=f_G)

            def e_wt(tensor, tc, bi):
                return tensor.transpose(ctx.bank(bi)[0:8, 0:128],
                                        G[:, tc, :], t["eye_f"][:, :])

            def e_wtev(vector, tc, bi):
                return vector.tensor_copy(wrow[:, tc, :], ctx.bank(bi)[0:8, 0:128])

            pipe(ctx, list(range(NTC)), e_wt, e_wtev, banks=(0, 1))
            if DEBUG == "G5":
                dump_dbg(ctx, G[:, :, :].rearrange("p tc e -> p (tc e)"), NTC * 8)
            if DEBUG == "logits5":
                dump_dbg(ctx, logits[:, :, :].rearrange("p tc e -> p (tc e)"), NTC * 8)
            if DEBUG == "rank5":
                dump_dbg(ctx, rank[:, :, 0, :].rearrange("p tc e -> p (tc e)"), NTC * 8)
            if DEBUG == "offs5":
                dump_dbg(ctx, offb[:, :], 16)

        if DEBUG == "xn5":
            dump_dbg(ctx, xn[:, 0, :], T)
        if DEBUG == "noexp5":
            return
        brow = ctx.sbuf("brow5", [1, T], BF16, es)
        Bcur = ctx.sbuf("Bcur5", [P, NTB, 512], BF16, es)
        tmp = ctx.sbuf("p5tmp", [P, 512], F32R, es)
        b1r = ctx.sbuf("b1r", [1, 4096], BF16, es)
        b2r = ctx.sbuf("b2r", [1, D], BF16, es)
        h = ctx.sbuf("hbuf5", [P, 32, 1024], BF16, es)
        wslab = ctx.sbuf("wslab5", [P, 64, P], BF16, es)
        w1sb = [wslab[:, 0:KC, :], wslab[:, 32:32 + KC, :]]
        w2sb = [wslab[:, 0:32, :], wslab[:, 32:64, :]]

        for e in range(8):
            tg = ldg.n + 48

            def f_brow(gpsimd, e=e, tg=tg):
                gpsimd.dma_start(out=brow[0:1, :],
                                 in_=wrow[e:e + 1, :, :].rearrange("o tc p -> o (tc p)")
                                 ).then_inc(ldg.h, 16)
                gpsimd.dma_start(out=b1r[0:1, :],
                                 in_=ctx.io["L1_b1"][e:e + 1, :]).then_inc(ldg.h, 16)
                gpsimd.dma_start(out=b2r[0:1, :],
                                 in_=ctx.io["L1_b2"][e:e + 1, :]).then_inc(ldg.h, 16)
                gpsimd.wait_ge(ldg.h, tg)
            ldg.n = tg
            blk(ctx, gpsimd=f_brow)

            def e_bmm(tensor, tb, bi):
                return tensor.matmul(ctx.bank(bi), t["c_ones_rowb"][0:1, 0:P],
                                     brow[0:1, ts512(tb)], start=True, stop=True)

            def e_bev(vector, tb, bi):
                return vector.tensor_copy(Bcur[:, tb, :], ctx.bank(bi))

            pipe(ctx, list(range(NTB)), e_bmm, e_bev, banks=(0, 1))

            for tb2 in range(2):
                def el_h(gpsimd, s, buf, e=e):
                    return gpsimd.dma_start(
                        out=w1sb[buf][:, :, :].rearrange("p kc m -> p (kc m)"),
                        in_=ctx.io["L1_w1H"][e, s].rearrange("p kc m -> p (kc m)"))

                def em_h(tensor, s, j, bi, buf, tb2=tb2):
                    for kc in range(KC):
                        tensor.matmul(
                            ctx.bank(bi), w1sb[buf][:, kc, :],
                            xn[:, kc, tb2 * 1024 + j * 512:tb2 * 1024 + (j + 1) * 512],
                            start=(kc == 0), stop=False)
                    return tensor.matmul(
                        ctx.bank(bi), b1r[0:1, s * 128:(s + 1) * 128],
                        t["c_ones_rowb"][0:1, 0:512], start=False, stop=True)

                def ev_h(scalar, s, j, bi):
                    return scalar.activation(h[:, s, ts512(j)], ctx.bank(bi), AF.Gelu)

                spipe(ctx, 32, 2, el_h, em_h, ev_h, ev_eng="scalar", banks=(0, 1))

                def el_y(gpsimd, s, buf, e=e):
                    return gpsimd.dma_start(
                        out=w2sb[buf][:, :, :].rearrange("p fc m -> p (fc m)"),
                        in_=ctx.io["L1_w2H"][e, s].rearrange("p fc m -> p (fc m)"),
                        max_dma_last_dim=2048)

                def em_y(tensor, s, j, bi, buf):
                    for fc in range(32):
                        tensor.matmul(ctx.bank(bi), w2sb[buf][:, fc, :],
                                      h[:, fc, ts512(j)],
                                      start=(fc == 0), stop=False)
                    return tensor.matmul(
                        ctx.bank(bi), b2r[0:1, s * 128:(s + 1) * 128],
                        t["c_ones_rowb"][0:1, 0:512], start=False, stop=True)

                def ev_y(vector, s, j, bi, tb2=tb2):
                    tb = tb2 * 2 + j
                    i1 = vector.tensor_mul(tmp[:, :], ctx.bank(bi), Bcur[:, tb, :])
                    edge(ctx, vector, i1, "edv")
                    return vector.tensor_add(t["X"][:, s, ts512(tb)],
                                             t["X"][:, s, ts512(tb)], tmp[:, :])

                spipe(ctx, 8, 2, el_y, em_y, ev_y, ev_eng="vector", banks=(2, 3))


PHASE_FNS = {"p1": phase1, "p2": phase2, "p3": phase3, "p4": phase4, "p5": phase5}

_BUILT = {}


def build_launch(phases):
    key = tuple(phases)
    if key in _BUILT:
        return _BUILT[key]
    nc = bass.Bass("TRN2", target_bir_lowering=False, debug=False)
    io = declare_io(nc)
    with ExitStack() as es:
        ctx = Ctx(nc, es, io)
        setup_ctx(ctx, phases)
        for ph in phases:
            PHASE_FNS[ph](ctx)
        finish(ctx)
    _BUILT[key] = nc
    return nc


def _host_consts():
    import ml_dtypes
    bf = ml_dtypes.bfloat16
    c = {}
    c["eye_b"] = np.eye(P, dtype=np.float32).astype(bf)
    c["eye_f"] = np.eye(P, dtype=np.float32)
    c["eye_r"] = np.eye(P, dtype=np.float32)
    c["slt_r"] = np.triu(np.ones((P, P), dtype=np.float32), 1)
    c["ones_sq"] = np.ones((P, P), dtype=np.float32)
    c["ones_col"] = np.ones((P, 1), dtype=np.float32)
    c["ones_row"] = np.ones((1, 512), dtype=np.float32)
    c["ones_rowb"] = np.ones((1, 512), dtype=np.float32).astype(bf)
    return c


def _core_consts(core):
    import ml_dtypes
    bf = ml_dtypes.bfloat16
    b, s = core // 2, core % 2
    partner = core ^ 1
    cc = {}
    inv_freq = (1.0 / 10000.0 ** (np.arange(0, HD, 2, dtype=np.float32) / HD))
    pos = (s * T + np.arange(T, dtype=np.float32))[:, None]
    th = pos * inv_freq[None, :]
    cos, sin = np.cos(th).astype(np.float32), np.sin(th).astype(np.float32)
    cc["ropeC"] = np.concatenate([cos, cos], axis=1).astype(bf)
    cc["ropeS"] = np.concatenate([-sin, sin], axis=1).astype(bf)
    tok = s * T + np.arange(T)
    is_local = (tok >= G).astype(np.float32)
    m16 = is_local.reshape(NTC, P).T
    cc["mloc16"] = m16.astype(bf)
    cc["mloc"] = np.broadcast_to(is_local[None, 0:512], (P, 512)).astype(bf).copy()
    is_glob = ((tok < G) & (s == 0)).astype(np.float32)
    cc["mglob"] = np.broadcast_to(is_glob[None, 0:512], (P, 512)).astype(bf).copy()
    cc["pmask"] = (np.arange(8) < core).astype(np.float32).reshape(8, 1)
    p = np.arange(P, dtype=np.uint32)
    cc["idx_kv"] = (partner * 65 + np.minimum(p, 64)).astype(np.uint32).reshape(P, 1)
    iq2 = np.zeros((P, 2), dtype=np.uint32)
    for si in range(2):
        iq2[:, si] = (2 * b + si) * P + p
    cc["idx_q2"] = iq2
    cc["idx_cd"] = (partner * P + p).astype(np.uint32).reshape(P, 1)
    iq = np.zeros((P, 2 * NHEAD), dtype=np.uint32)
    for h in range(NHEAD):
        for si in range(2):
            iq[0:64, 2 * h + si] = (2 * b + si) * 2048 + h * 64 + np.arange(64)
    cc["idx_qk"] = iq
    ir = np.zeros((P, NHEAD), dtype=np.uint32)
    for h in range(NHEAD):
        ir[0:64, h] = (2 * b + s) * 2048 + h * 64 + np.arange(64)
    cc["idx_rhs"] = ir
    iv = np.zeros((P, NHEAD), dtype=np.uint32)
    for c2 in range(8):
        for si in range(2):
            iv[:, c2 * 2 + si] = (2 * b + si) * 2048 + 1024 + c2 * 128 + p
    cc["idx_v"] = iv
    return cc


def _packw1(w):
    return np.ascontiguousarray(
        np.asarray(w, np.float32).reshape(8, 8, P, 32, P).transpose(0, 3, 2, 1, 4))


def _packw2(w):
    return np.ascontiguousarray(
        np.asarray(w, np.float32).reshape(8, 32, P, 8, P).transpose(0, 3, 2, 1, 4))


def _vec8(v):
    return np.ascontiguousarray(np.asarray(v, dtype=np.float32).reshape(KC, P).T)


def kernel(**inp):
    inp = {k: np.asarray(v) for k, v in inp.items()}
    x = inp["L0_attn_x"].astype(np.float32, copy=False)
    shared = {
        "L0_Wq": inp["L0_attn_Wq"], "L0_Wk": inp["L0_attn_Wk"],
        "L0_Wv": inp["L0_attn_Wv"], "L0_Wout": inp["L0_attn_Wout"],
        "L0_g": _vec8(inp["L0_ffn_gamma"]), "L0_b": _vec8(inp["L0_ffn_beta"]),
        "L0_router": np.ascontiguousarray(inp["L0_ffn_router_w"].T),
        "L0_W1H": _packw1(inp["L0_ffn_W1"]), "L0_W2H": _packw2(inp["L0_ffn_W2"]),
        "L1_Wqk_t": inp["L1_attn_Wqk_t"], "L1_Wv_t": inp["L1_attn_Wv_t"],
        "L1_Wo_t": inp["L1_attn_Wo_t"], "L1_Wqk_c": inp["L1_attn_Wqk_c"],
        "L1_Wv_c": inp["L1_attn_Wv_c"], "L1_Wo_c": inp["L1_attn_Wo_c"],
        "L1_tau_t": np.ascontiguousarray(inp["L1_attn_tau_t"].reshape(1, 16)),
        "L1_tau_c": np.ascontiguousarray(inp["L1_attn_tau_c"].reshape(1, 8)),
        "L1_ln1_w": _vec8(inp["L1_attn_ln1_w"]), "L1_ln1_b": _vec8(inp["L1_attn_ln1_b"]),
        "L1_ln2_w": _vec8(inp["L1_attn_ln2_w"]), "L1_ln2_b": _vec8(inp["L1_attn_ln2_b"]),
        "L1_g1": _vec8(inp["L1_attn_gamma1"]), "L1_g2": _vec8(inp["L1_attn_gamma2"]),
        "L1_wg": inp["L1_ffn_wg"], "L1_w1H": _packw1(inp["L1_ffn_w1"]),
        "L1_b1": inp["L1_ffn_b1"], "L1_w2H": _packw2(inp["L1_ffn_w2"]),
        "L1_b2": inp["L1_ffn_b2"],
    }
    shared = {k: np.ascontiguousarray(v.astype(np.float32, copy=False))
              for k, v in shared.items()}
    shared.update(_host_consts())

    in_maps = []
    for c in range(8):
        b, s = c // 2, c % 2
        m = dict(shared)
        m.update(_core_consts(c))
        m["xin"] = np.ascontiguousarray(x[b, s * T:(s + 1) * T, :].T)
        in_maps.append(m)

    for phases in LAUNCHES:
        nc = build_launch(phases)
        res = run_bass_kernel_spmd(nc, in_maps, core_ids=list(range(8)))
        for c in range(8):
            in_maps[c]["xin"] = np.ascontiguousarray(res.results[c]["xout"])

    out = np.empty((4, 4096, D), dtype=np.float32)
    for c in range(8):
        b, s = c // 2, c % 2
        out[b, s * T:(s + 1) * T, :] = in_maps[c]["xin"].T
    return out


# version 11
# speedup vs baseline: 1.1596x; 1.1596x over previous
import numpy as np
from contextlib import ExitStack

import concourse.bass as bass
import concourse.mybir as mybir
from concourse.bass import IndirectOffsetOnAxis
from concourse.bass_utils import run_bass_kernel_spmd
from concourse.bass_isa import ReduceOp

F32 = mybir.dt.float32
F32R = mybir.dt.float32r
BF16 = mybir.dt.bfloat16
U32 = mybir.dt.uint32
AF = mybir.ActivationFunctionType
ALU = mybir.AluOpType
AX = mybir.AxisListType.X

P = 128
T = 2048
KC = 8
NTB = 4
NTC = 16
D = 1024
NHEAD = 16
HD = 64
G = 409
EPS = 1e-5

LAUNCHES = [["p1", "p2", "p3", "p4", "p5"]]
DEBUG = None

_GS = [0]


def gs(p):
    _GS[0] += 1
    return f"{p}_{_GS[0]}"


def ts512(i):
    return slice(i * 512, (i + 1) * 512)


def ts128(i):
    return slice(i * 128, (i + 1) * 128)


class Sem:
    def __init__(self, h):
        self.h = h
        self.n = 0


class Ctx:
    def __init__(self, nc, es, io):
        self.nc = nc
        self.es = es
        self.io = io
        self.sem = {}
        for name in ["ldw", "ldg", "ldt", "st", "pe", "ve", "ac", "gp", "cc",
                     "edv", "eac", "egp", "aux"]:
            self.sem[name] = Sem(es.enter_context(nc.semaphore(gs("sm_" + name))))
        self.t = {}

    def sbuf(self, name, shape, dtype, es=None):
        return (es or self.es).enter_context(
            self.nc.sbuf_tensor(gs(name), list(shape), dtype))

    def psum(self, name, shape, dtype=F32, es=None):
        return (es or self.es).enter_context(
            self.nc.psum_tensor(gs(name), list(shape), dtype))

    def bank(self, i):
        return (self.t["psA"] if i < 4 else self.t["psB"])[:, i % 4, :]

    def bankb(self, i):
        return self.bank(i).bitcast(BF16)


def blk(ctx, **fns):
    with ctx.nc.Block() as b:
        for k, f in fns.items():
            getattr(b, k)(f)


def edge(ctx, eng, instr, which):
    s = ctx.sem[which]
    s.n += 1
    instr.then_inc(s.h, 1)
    eng.wait_ge(s.h, s.n)


def pipe(ctx, jobs, emit_mm, emit_ev, banks=(0, 1), ev_eng="vector"):
    pe, ve = ctx.sem["pe"], ctx.sem["ve"]
    pe0, ve0 = pe.n, ve.n
    nbuf = len(banks)

    def f_mm(tensor):
        for i, job in enumerate(jobs):
            if i >= nbuf:
                tensor.wait_ge(ve.h, ve0 + i - nbuf + 1)
            emit_mm(tensor, job, banks[i % nbuf]).then_inc(pe.h, 1)

    def f_ev(eng):
        for i, job in enumerate(jobs):
            eng.wait_ge(pe.h, pe0 + i + 1)
            emit_ev(eng, job, banks[i % nbuf]).then_inc(ve.h, 1)

    pe.n += len(jobs)
    ve.n += len(jobs)
    blk(ctx, **{"tensor": f_mm, ev_eng: f_ev})


def stats(ctx, src, flavor, es, bcast=(), nsq=2):
    nc = ctx.nc
    rows = ctx.sbuf("st_rows", [1, 6, T], F32, es)
    rc = ctx.sbuf("st_rcol", [P, NTC], F32, es)
    bts = {k: ctx.sbuf("st_b_" + k, [P, NTB, 512], F32R, es) for k in bcast}
    es_t = ExitStack()
    sq = [ctx.sbuf("st_sq", [P, T], F32R, es_t) for _ in range(nsq)]
    sq = [sq[i % nsq] for i in range(2)]
    ac, pe = ctx.sem["ac"], ctx.sem["pe"]
    ac0, pe0 = ac.n, pe.n
    psA, psB = ctx.t["psA"], ctx.t["psB"]
    ones_col = ctx.t["ones_col"]

    def f_scalar(scalar):
        for kc in range(KC):
            if kc >= nsq:
                scalar.wait_ge(pe.h, pe0 + kc - nsq + 1)
            scalar.activation(sq[kc % 2][:, :], src[:, kc, :], AF.Square).then_inc(ac.h, 1)

    def f_tensor(tensor):
        for kc in range(KC):
            for tb in range(NTB):
                tensor.matmul(psA[0:1, tb, :], ones_col[:, 0:1], src[:, kc, ts512(tb)],
                              start=(kc == 0), stop=(kc == KC - 1))
            tensor.wait_ge(ac.h, ac0 + kc + 1)
            ins = None
            for tb in range(NTB):
                ins = tensor.matmul(psB[0:1, tb, :], ones_col[:, 0:1],
                                    sq[kc % 2][:, ts512(tb)],
                                    start=(kc == 0), stop=(kc == KC - 1))
            ins.then_inc(pe.h, 1)

    ac.n += KC
    pe.n += KC
    blk(ctx, scalar=f_scalar, tensor=f_tensor)

    inv_d = 1.0 / D
    r = lambda i: rows[0:1, i, :]

    def f_rowmath(vector):
        for tb in range(NTB):
            vector.tensor_copy(rows[0:1, 0, ts512(tb)], psA[0:1, tb, :])
            ins = vector.tensor_copy(rows[0:1, 1, ts512(tb)], psB[0:1, tb, :])
            edge(ctx, vector, ins, "edv")
        edge(ctx, vector, vector.tensor_scalar(r(2), r(0), inv_d, None, ALU.mult), "edv")
        edge(ctx, vector, vector.tensor_scalar(r(3), r(2), -1.0, None, ALU.mult), "edv")
        edge(ctx, vector, vector.tensor_mul(r(5), r(2), r(2)), "edv")
        edge(ctx, vector, vector.tensor_scalar(r(4), r(1), inv_d, None, ALU.mult), "edv")
        edge(ctx, vector, vector.tensor_sub(r(4), r(4), r(5)), "edv")
        edge(ctx, vector, vector.tensor_scalar(r(5), r(4), EPS, None, ALU.add), "edv")
        edge(ctx, vector, vector.reciprocal(r(0), r(5)), "edv")

    blk(ctx, vector=f_rowmath)

    if flavor == "ln":
        def f_sqrt(scalar):
            edge(ctx, scalar, scalar.activation(r(1), r(0), AF.Sqrt), "eac")
            edge(ctx, scalar, scalar.activation(r(5), r(5), AF.Sqrt), "eac")
        blk(ctx, scalar=f_sqrt)
        mult_slot, inv_slot, plain_slot = 1, 5, 1
    else:
        def f_v2(vector):
            edge(ctx, vector, vector.tensor_mul(r(5), r(4), r(0)), "edv")
            edge(ctx, vector, vector.tensor_scalar(r(5), r(5), EPS, None, ALU.add), "edv")
        blk(ctx, vector=f_v2)

        def f_sq2(scalar):
            edge(ctx, scalar, scalar.activation(r(1), r(0), AF.Sqrt), "eac")
            edge(ctx, scalar, scalar.activation(r(5), r(5), AF.Sqrt), "eac")
        blk(ctx, scalar=f_sq2)

        def f_rr(vector):
            edge(ctx, vector, vector.reciprocal(r(0), r(5)), "edv")
            edge(ctx, vector, vector.tensor_mul(r(4), r(1), r(0)), "edv")
            edge(ctx, vector, vector.reciprocal(r(5), r(4)), "edv")
        blk(ctx, vector=f_rr)
        mult_slot, inv_slot, plain_slot = 4, 5, 1

    out = {"_rows": rows,
           "mu": r(2), "negmu": r(3),
           "rstd": r(mult_slot), "sig": r(inv_slot), "plain": r(plain_slot)}

    def f_tr(tensor):
        for tc in range(NTC):
            tensor.transpose(psA[:, 0, tc:tc + 1], rows[0:1, mult_slot, ts128(tc)],
                             ctx.t["eye_f"][0:1, 0:1])

    blk(ctx, tensor=f_tr)

    def f_trc(vector):
        vector.tensor_copy(rc[:, :], psA[:, 0, 0:NTC])

    blk(ctx, vector=f_trc)
    out["rstd_col"] = rc

    slot_of = {"mult": mult_slot, "mu": 2, "plain": plain_slot}
    for bk in bcast:
        bc = bts[bk]
        out["b_" + bk] = bc
        sl = slot_of[bk]

        def e_mm(tensor, tb, bi, sl=sl):
            return tensor.matmul(ctx.bank(bi), ctx.t["ones_row"][0:1, 0:P],
                                 rows[0:1, sl, ts512(tb)], start=True, stop=True)

        def e_ev(vector, tb, bi, bc=bc):
            return vector.tensor_copy(bc[:, tb, :], ctx.bank(bi))

        pipe(ctx, list(range(NTB)), e_mm, e_ev, banks=(0, 1))
    es_t.close()
    return out


def load_w(ctx, wap, mslice, dtype, es, name="w", want_sum=False, wcol=None,
           bcol=None):
    nc = ctx.nc
    m0, m1 = mslice
    M = m1 - m0
    wsb = ctx.sbuf(name, [P, KC, M], dtype, es)
    ldg = ctx.sem["ldg"]

    def f_load(gpsimd):
        src = wap.rearrange("(kc p) m -> p kc m", p=P)[:, :, m0:m1]
        gpsimd.dma_start(out=wsb[:, :, :], in_=src).then_inc(ldg.h, 16)

    ldg.n += 16
    blk(ctx, gpsimd=f_load)
    wait_n = ldg.n

    out = {"w": wsb}
    nmm = (M + 511) // 512
    need_rows = want_sum or (bcol is not None)
    rows = ctx.sbuf(name + "_r", [1, 3, max(M, 16)], F32, es) if need_rows else None

    if bcol is not None:
        bcol_r = ctx.sbuf(name + "_bc", [P, KC], F32R, es)

        def f_bcr(vector):
            vector.tensor_copy(bcol_r[:, :], bcol[:, :])
        blk(ctx, vector=f_bcr)

        def f_qb_mm(tensor):
            tensor.wait_ge(ldg.h, wait_n)
            for j in range(nmm):
                sl = slice(j * 512, min(M, (j + 1) * 512))
                w = sl.stop - sl.start
                for kc in range(KC):
                    tensor.matmul(ctx.bank(j)[0:1, 0:w], bcol_r[:, kc:kc + 1],
                                  wsb[:, kc, sl], start=(kc == 0), stop=(kc == KC - 1))
        blk(ctx, tensor=f_qb_mm)

        def f_qb_ev(vector):
            for j in range(nmm):
                sl = slice(j * 512, min(M, (j + 1) * 512))
                vector.tensor_copy(rows[0:1, 2, sl], ctx.bank(j)[0:1, 0:sl.stop - sl.start])
        blk(ctx, vector=f_qb_ev)
        out["qb"] = rows[0:1, 2, 0:M]

    if wcol is not None:
        def f_scale(vector):
            vector.wait_ge(ldg.h, wait_n)
            for kc in range(KC):
                vector.tensor_scalar(wsb[:, kc, :], wsb[:, kc, :], wcol[:, kc:kc + 1],
                                     None, ALU.mult)
        blk(ctx, vector=f_scale)

    if want_sum:
        def f_sum(tensor):
            tensor.wait_ge(ldg.h, wait_n)
            for j in range(nmm):
                sl = slice(j * 512, min(M, (j + 1) * 512))
                w = sl.stop - sl.start
                for kc in range(KC):
                    tensor.matmul(ctx.bank(j)[0:1, 0:w], ctx.t["ones_col"][:, 0:1],
                                  wsb[:, kc, sl], start=(kc == 0), stop=(kc == KC - 1))
        blk(ctx, tensor=f_sum)

        def f_ev(vector):
            for j in range(nmm):
                sl = slice(j * 512, min(M, (j + 1) * 512))
                w = sl.stop - sl.start
                vector.tensor_copy(rows[0:1, 0, sl], ctx.bank(j)[0:1, 0:w])
                vector.tensor_scalar(rows[0:1, 1, sl], ctx.bank(j)[0:1, 0:w], -1.0,
                                     None, ALU.mult)
        blk(ctx, vector=f_ev)
        out["wsum"] = rows[0:1, 0, 0:M]
        out["negwsum"] = rows[0:1, 1, 0:M]

    if not need_rows and wcol is None:
        def f_w(vector):
            vector.wait_ge(ldg.h, wait_n)
        blk(ctx, vector=f_w)
    return out


def proj_tm(ctx, wd, st, src, M, evac):
    nj = (M + 511) // 512
    jobs = [(tc, j) for tc in range(NTC) for j in range(nj)]

    def e_mm(tensor, job, bi):
        tc, j = job
        sl = slice(j * 512, min(M, (j + 1) * 512))
        w = sl.stop - sl.start
        ps = ctx.bank(bi)[:, 0:w]
        for kc in range(KC):
            ins = tensor.matmul(ps, src[:, kc, ts128(tc)], wd["w"][:, kc, sl],
                                start=(kc == 0), stop=False)
        ins = tensor.matmul(ps, st["negmu"][0:1, ts128(tc)], wd["wsum"][0:1, sl],
                            start=False, stop=("qb" not in wd))
        if "qb" in wd:
            ins = tensor.matmul(ps, st["sig"][0:1, ts128(tc)], wd["qb"][0:1, sl],
                                start=False, stop=True)
        return ins

    def e_ev(vector, job, bi):
        tc, j = job
        sl = slice(j * 512, min(M, (j + 1) * 512))
        return evac(vector, ctx.bank(bi)[:, 0:sl.stop - sl.start], tc, j)

    pipe(ctx, jobs, e_mm, e_ev, banks=(0, 1))


def proj_fm(ctx, wsb, src, evac, mcs=range(KC), st=None, wd=None, m0=0):
    jobs = [(mc, tb) for mc in mcs for tb in range(NTB)]

    def e_mm(tensor, job, bi):
        mc, tb = job
        ml = mc - m0
        ps = ctx.bank(bi)
        for kc in range(KC):
            ins = tensor.matmul(ps, wsb[:, kc, ts128(ml)], src[:, kc, ts512(tb)],
                                start=(kc == 0),
                                stop=(kc == KC - 1 and st is None))
        if st is not None:
            last = ("negwsum" not in wd, "qb" not in wd)
            if "negwsum" in wd:
                ins = tensor.matmul(ps, wd["negwsum"][0:1, ts128(ml)],
                                    st["mu"][0:1, ts512(tb)], start=False,
                                    stop=("qb" not in wd))
            if "qb" in wd:
                ins = tensor.matmul(ps, wd["qb"][0:1, ts128(ml)],
                                    st["sig"][0:1, ts512(tb)], start=False, stop=True)
        return ins

    def e_ev(vector, job, bi):
        mc, tb = job
        return evac(vector, ctx.bank(bi), mc, tb)

    pipe(ctx, jobs, e_mm, e_ev, banks=(0, 1))


def declare_io(nc):
    t = {}

    def din(name, shape, dtype):
        t[name] = nc.dram_tensor(name, list(shape), dtype, kind="ExternalInput").ap()

    din("xin", [D, T], F32R)
    for nm in ["L0_Wq", "L0_Wk", "L0_Wv", "L0_Wout",
               "L1_Wqk_t", "L1_Wv_t", "L1_Wo_t", "L1_Wqk_c", "L1_Wv_c", "L1_Wo_c"]:
        din(nm, [D, D], F32)
    din("L0_g", [P, KC], F32)
    din("L0_b", [P, KC], F32)
    din("L0_router", [D, 8], F32)
    din("L0_W1H", [8, 32, P, KC, P], F32)
    din("L0_W2H", [8, KC, P, 32, P], F32)
    din("L1_tau_t", [1, 16], F32)
    din("L1_tau_c", [1, 8], F32)
    for nm in ["L1_ln1_w", "L1_ln1_b", "L1_ln2_w", "L1_ln2_b", "L1_g1", "L1_g2"]:
        din(nm, [P, KC], F32)
    din("L1_wg", [D, 8], F32)
    din("L1_w1H", [8, 32, P, KC, P], F32)
    din("L1_b1", [8, 4096], F32)
    din("L1_w2H", [8, KC, P, 32, P], F32)
    din("L1_b2", [8, D], F32)
    din("eye_b", [P, P], BF16)
    din("eye_f", [P, P], F32)
    din("eye_r", [P, P], F32R)
    din("slt_r", [P, P], F32R)
    din("ones_sq", [P, P], F32R)
    din("ones_col", [P, 1], F32R)
    din("ones_row", [1, 512], F32)
    din("ones_rowb", [1, 512], BF16)
    din("ropeC", [T, HD], BF16)
    din("ropeS", [T, HD], BF16)
    din("mloc16", [P, NTC], BF16)
    din("mloc", [P, 512], BF16)
    din("mglob", [P, 512], BF16)
    din("pmask", [8, 1], F32R)
    din("idx_kv", [P, 1], U32)
    din("idx_q2", [P, 2], U32)
    din("idx_cd", [P, 1], U32)
    din("idx_qk", [P, 2 * NHEAD], U32)
    din("idx_rhs", [P, NHEAD], U32)
    din("idx_v", [P, NHEAD], U32)
    t["xout"] = nc.dram_tensor("xout", [D, T], F32R, kind="ExternalOutput").ap()
    t["dbg"] = nc.dram_tensor("dbg", [P, 4096], F32, kind="ExternalOutput").ap()

    def dint(name, shape, dtype):
        t[name] = nc.dram_tensor(gs(name), list(shape), dtype).ap()

    dint("xspill", [D, T], F32R)
    dint("ag_kv_in", [65, D], F32)
    dint("ag_kv_out", [8 * 65, D], F32)
    dint("ag_qkv_in", [2 * D, T], BF16)
    dint("ag_qkv_out", [8 * 2 * D, T], BF16)
    dint("ag_q2_in", [P, 256], F32)
    dint("ag_q2_out", [8 * P, 256], F32)
    dint("ag_cd_in", [P, KC * P], F32)
    dint("ag_cd_out", [8 * P, KC * P], F32)
    dint("ag_cnt_in", [1, 16], F32)
    dint("ag_cnt_out", [8, 16], F32)
    return t


CONST_NAMES = ["eye_b", "eye_f", "eye_r", "slt_r", "ones_sq", "ones_col", "ones_row",
               "ones_rowb", "idx_rhs",
               "mloc16", "mloc", "mglob", "pmask", "idx_kv", "idx_q2", "idx_cd",
               "idx_qk", "idx_v", "L0_g", "L0_b", "L1_ln1_w", "L1_ln1_b",
               "L1_ln2_w", "L1_ln2_b", "L1_g1", "L1_g2", "L1_tau_t", "L1_tau_c"]

CONST_SHAPES = {
    "eye_b": ([P, P], BF16), "eye_f": ([P, P], F32), "eye_r": ([P, P], F32R),
    "slt_r": ([P, P], F32R),
    "ones_sq": ([P, P], F32R), "ones_col": ([P, 1], F32R), "ones_row": ([1, 512], F32), "ones_rowb": ([1, 512], BF16),
    "mloc16": ([P, NTC], BF16), "mloc": ([P, 512], BF16), "mglob": ([P, 512], BF16),
    "pmask": ([8, 1], F32R), "idx_kv": ([P, 1], U32), "idx_q2": ([P, 2], U32),
    "idx_cd": ([P, 1], U32), "idx_qk": ([P, 2 * NHEAD], U32), "idx_rhs": ([P, NHEAD], U32), "idx_v": ([P, NHEAD], U32),
    "L0_g": ([P, KC], F32), "L0_b": ([P, KC], F32),
    "L1_ln1_w": ([P, KC], F32), "L1_ln1_b": ([P, KC], F32),
    "L1_ln2_w": ([P, KC], F32), "L1_ln2_b": ([P, KC], F32),
    "L1_g1": ([P, KC], F32), "L1_g2": ([P, KC], F32),
    "L1_tau_t": ([1, 16], F32), "L1_tau_c": ([1, 8], F32),
}


PHASE_CONSTS = {
    "eye_b": ("p1",), "mloc16": ("p1",), "mloc": ("p1",), "mglob": ("p1",),
    "idx_kv": ("p1",),
    "idx_q2": ("p3",), "idx_qk": ("p3",), "idx_rhs": ("p3",), "idx_v": ("p3",),
    "L1_tau_t": ("p3",), "idx_cd": ("p4",), "L1_tau_c": ("p4",),
    "L1_ln1_w": ("p3",), "L1_ln1_b": ("p3",),
    "L1_ln2_w": ("p4",), "L1_ln2_b": ("p4",),
    "L1_g1": ("p3",), "L1_g2": ("p4",),
    "L0_g": ("p2",), "L0_b": ("p2",),
    "pmask": ("p5",), "slt_r": ("p5",), "ones_sq": ("p5",),
}


def setup_ctx(ctx, phases):
    t = ctx.t
    t["X"] = ctx.sbuf("X", [P, KC, T], F32R)
    t["psA"] = ctx.psum("psA", [P, 4, 512], F32)
    t["psB"] = ctx.psum("psB", [P, 4, 512], F32)
    names = [nm for nm in CONST_NAMES
             if not PHASE_CONSTS.get(nm) or
             (any(p in phases for p in PHASE_CONSTS[nm])
              and "p1" not in PHASE_CONSTS.get(nm, ()))]
    rope = False
    for nm in names:
        shape, dt = CONST_SHAPES[nm]
        t["c_" + nm] = ctx.sbuf("c_" + nm, shape, dt)
    if rope:
        t["c_ropeC"] = ctx.sbuf("c_ropeC", [P, NTC, HD], BF16)
        t["c_ropeS"] = ctx.sbuf("c_ropeS", [P, NTC, HD], BF16)
    ldt = ctx.sem["ldt"]

    def f_load(sync):
        sync.dma_start(out=t["X"][:, :, :],
                       in_=ctx.io["xin"].rearrange("(kc p) t -> p kc t", p=P)).then_inc(ldt.h, 16)
        for nm in names:
            sync.dma_start(out=t["c_" + nm][...], in_=ctx.io[nm]).then_inc(ldt.h, 16)
        if rope:
            sync.dma_start(out=t["c_ropeC"][:, :, :],
                           in_=ctx.io["ropeC"].rearrange("(tc p) e -> p tc e", p=P)).then_inc(ldt.h, 16)
            sync.dma_start(out=t["c_ropeS"][:, :, :],
                           in_=ctx.io["ropeS"].rearrange("(tc p) e -> p tc e", p=P)).then_inc(ldt.h, 16)
        sync.wait_ge(ldt.h, ldt.n)

    ldt.n += 16 * (1 + len(names) + (2 if rope else 0))
    blk(ctx, sync=f_load)
    for k in ["eye_b", "eye_f", "eye_r", "slt_r", "ones_sq", "ones_col", "ones_row"]:
        if "c_" + k in t:
            t[k] = t["c_" + k]


def finish(ctx):
    st = ctx.sem["st"]
    target = st.n + 16

    def f_store(sync):
        sync.dma_start(out=ctx.io["xout"].rearrange("(kc p) t -> p kc t", p=P),
                       in_=ctx.t["X"][:, :, :]).then_inc(st.h, 16)
        sync.wait_ge(st.h, target)

    st.n = target
    blk(ctx, sync=f_store)


def allgather(ctx, in_ap, out_ap):
    cc = ctx.sem["cc"]
    cc.n += 1
    target = cc.n

    def f_cc(gpsimd):
        gpsimd.collective_compute(
            "AllGather", ALU.bypass, replica_groups=[[0, 1, 2, 3, 4, 5, 6, 7]],
            ins=[in_ap.opt()], outs=[out_ap.opt()]).then_inc(cc.h, 1)
        gpsimd.wait_ge(cc.h, target)

    blk(ctx, gpsimd=f_cc)




P1_CONSTS = ["eye_b", "mloc16", "mloc", "mglob", "idx_kv"]


def phase1(ctx):
    nc, t = ctx.nc, ctx.t
    pe, ve, ac, ldg = ctx.sem["pe"], ctx.sem["ve"], ctx.sem["ac"], ctx.sem["ldg"]
    with ExitStack() as es:
        for nm in P1_CONSTS:
            shape, dt_ = CONST_SHAPES[nm]
            t["c_" + nm] = ctx.sbuf("c_" + nm, shape, dt_, es)
        t["c_ropeC"] = ctx.sbuf("c_ropeC", [P, NTC, HD], BF16, es)
        t["c_ropeS"] = ctx.sbuf("c_ropeS", [P, NTC, HD], BF16, es)
        t["eye_b"] = t["c_eye_b"]
        ldt1 = ctx.sem["ldt"]

        def f_ld1(sync):
            for nm in P1_CONSTS:
                sync.dma_start(out=t["c_" + nm][...], in_=ctx.io[nm]).then_inc(ldt1.h, 16)
            sync.dma_start(out=t["c_ropeC"][:, :, :],
                           in_=ctx.io["ropeC"].rearrange("(tc p) e -> p tc e", p=P)).then_inc(ldt1.h, 16)
            sync.dma_start(out=t["c_ropeS"][:, :, :],
                           in_=ctx.io["ropeS"].rearrange("(tc p) e -> p tc e", p=P)).then_inc(ldt1.h, 16)
            sync.wait_ge(ldt1.h, ldt1.n)

        ldt1.n += 16 * (len(P1_CONSTS) + 2)
        blk(ctx, sync=f_ld1)
        st = stats(ctx, t["X"], "ln", es)
        q_fm = ctx.sbuf("q_fm", [P, KC, T], BF16, es)
        attn_out = q_fm
        go_sb = ctx.sbuf("go_sb", [P, KC, 416], BF16, es)
        kvden = ctx.sbuf("kvden", [65, NHEAD * HD], F32, es)

        for hg in range(4):
            msl = (hg * 256, (hg + 1) * 256)
            with ExitStack() as eh:
                q_tm = ctx.sbuf("q_tm", [P, NTC, 256], BF16, eh)
                k_tm = ctx.sbuf("k_tm", [P, NTC, 256], BF16, eh)
                v_ext = ctx.sbuf("v_ext", [P, NTC, 260], BF16, eh)

                for wname, kind in [("L0_Wq", "q"), ("L0_Wk", "k"), ("L0_Wv", "v")]:
                    with ExitStack() as ew:
                        wd = load_w(ctx, ctx.io[wname], msl, F32R, ew,
                                    name="w" + kind, want_sum=True)
                        if kind == "v":
                            def evac(vector, ps, tc, j):
                                dst = v_ext[:, tc, :].rearrange(
                                    "p (h c) -> p h c", c=65)[:, :, 0:64]
                                return vector.tensor_scalar(
                                    dst, ps.rearrange("p (h e) -> p h e", e=64),
                                    st["rstd_col"][:, tc:tc + 1], None, ALU.mult)
                        else:
                            dstt = q_tm if kind == "q" else k_tm

                            def evac(vector, ps, tc, j, dstt=dstt):
                                return vector.tensor_scalar(
                                    dstt[:, tc, :], ps,
                                    st["rstd_col"][:, tc:tc + 1], None, ALU.mult)
                        proj_tm(ctx, wd, st, t["X"], 256, evac)

                def f_ones(vector):
                    vector.memset(v_ext[:, :, :].rearrange(
                        "p tc (h c) -> p tc h c", c=65)[:, :, :, 64:65], 1.0)
                blk(ctx, vector=f_ones)

                ra = ctx.sbuf("ra", [P, 256], BF16, eh)
                rb = ctx.sbuf("rb", [P, 256], BF16, eh)

                def f_rope(vector):
                    for dst in (q_tm, k_tm):
                        for tc in range(NTC):
                            x = dst[:, tc, :]
                            xr = x.rearrange("p (h half e) -> p h half e",
                                             half=2, e=32)[:, :, ::-1, :]
                            C = t["c_ropeC"][:, tc, None, :].broadcast_to([P, 4, HD])
                            S = t["c_ropeS"][:, tc, None, :].broadcast_to(
                                [P, 4, HD]).rearrange("p h (half e) -> p h half e",
                                                      half=2)
                            vector.tensor_mul(
                                ra[:, :].rearrange("p (h e) -> p h e", e=64),
                                x.rearrange("p (h e) -> p h e", e=64), C)
                            vb = vector.tensor_mul(
                                rb[:, :].rearrange("p (h half e) -> p h half e",
                                                   half=2, e=32), xr, S)
                            edge(ctx, vector, vb, "edv")
                            edge(ctx, vector,
                                 vector.tensor_add(x, ra[:, :], rb[:, :]), "edv")

                blk(ctx, vector=f_rope)

                def e_qtr(tensor, job, bi):
                    tc, pc = job
                    return tensor.transpose(ctx.bankb(bi)[:, 0:128],
                                            q_tm[:, tc, pc * 128:(pc + 1) * 128],
                                            t["eye_b"][:, :])

                def e_qev(vector, job, bi):
                    tc, pc = job
                    return vector.tensor_copy(q_fm[:, hg * 2 + pc, ts128(tc)],
                                              ctx.bankb(bi)[:, 0:128])

                pipe(ctx, [(tc, pc) for tc in range(NTC) for pc in range(2)],
                     e_qtr, e_qev, banks=(0, 1))

                E_sb = q_tm

                def f_exp(scalar):
                    scalar.activation(E_sb[:, :, :], k_tm[:, :, :], AF.Exp)
                blk(ctx, scalar=f_exp)

                def f_emask(vector):
                    vector.tensor_mul(
                        E_sb[:, :, :], E_sb[:, :, :],
                        t["c_mloc16"][:, :, None].broadcast_to([P, NTC, 256]))
                blk(ctx, vector=f_emask)

                def e_kv(tensor, h, bi):
                    ins = None
                    for tc in range(NTC):
                        ins = tensor.matmul(ctx.bank(bi)[0:65, 0:64],
                                            v_ext[:, tc, h * 65:h * 65 + 65],
                                            E_sb[:, tc, h * 64:(h + 1) * 64],
                                            start=(tc == 0), stop=(tc == NTC - 1))
                    return ins

                def e_kvev(vector, h, bi):
                    hh = hg * 4 + h
                    return vector.tensor_copy(kvden[0:65, hh * 64:(hh + 1) * 64],
                                              ctx.bank(bi)[0:65, 0:64])

                pipe(ctx, list(range(4)), e_kv, e_kvev, banks=(0, 1))

                kg_sb = ctx.sbuf("kg_sb", [P, 2, 512], BF16, eh)
                sg_sb = ctx.sbuf("sg_sb", [P, 2, 4, 416], BF16, eh)
                sgT_sb = ctx.sbuf("sgT_sb", [P, 2, 4, 416], BF16, eh)
                mrow = ctx.sbuf("mrow", [P, 2, 4, 2], F32, eh)
                ic_sizes = [128, 128, 128, 25]
                jc_sizes = [128, 128, 128, 25]

                for rnd in range(2):
                    heads = [rnd * 2, rnd * 2 + 1]

                    def e_kgt2(tensor, job, bi):
                        hh, jc = job
                        return tensor.transpose(
                            ctx.bankb(bi)[0:64, 0:jc_sizes[jc]],
                            k_tm[0:jc_sizes[jc], jc,
                                 heads[hh] * 64:(heads[hh] + 1) * 64],
                            t["eye_b"][0:jc_sizes[jc], 0:jc_sizes[jc]])

                    def e_kgev2(vector, job, bi):
                        hh, jc = job
                        hpar = ((hg * 4 + heads[hh]) % 2) * 64
                        return vector.tensor_copy(
                            kg_sb[hpar:hpar + 64, hh, jc * 128:jc * 128 + jc_sizes[jc]],
                            ctx.bankb(bi)[0:64, 0:jc_sizes[jc]])

                    pipe(ctx, [(hh, jc) for hh in range(2) for jc in range(4)],
                         e_kgt2, e_kgev2, banks=(0, 1))

                    def f_scores(tensor):
                        for hh in range(2):
                            h = heads[hh]
                            hchunk = (hg * 4 + h) // 2
                            hpar = ((hg * 4 + h) % 2) * 64
                            for ic in range(4):
                                icw = ic_sizes[ic]
                                tensor.matmul(
                                    ctx.bank(hh * 4 + ic)[0:icw, 0:G],
                                    q_fm[hpar:hpar + 64, hchunk,
                                         ic * 128:ic * 128 + icw],
                                    kg_sb[hpar:hpar + 64, hh, 0:G],
                                    start=True, stop=True)
                    blk(ctx, tensor=f_scores)

                    def f_smax(vector):
                        for hh in range(2):
                            for ic in range(4):
                                icw = ic_sizes[ic]
                                bi = hh * 4 + ic
                                edge(ctx, vector, vector.reduce_max(
                                    mrow[0:icw, hh, ic, 0:1],
                                    ctx.bank(bi)[0:icw, 0:G], axis=AX), "edv")
                                edge(ctx, vector, vector.tensor_scalar(
                                    mrow[0:icw, hh, ic, 1:2],
                                    mrow[0:icw, hh, ic, 0:1],
                                    -0.125, None, ALU.mult), "edv")
                    blk(ctx, vector=f_smax)

                    def f_sexp(scalar):
                        for hh in range(2):
                            for ic in range(4):
                                icw = ic_sizes[ic]
                                bi = hh * 4 + ic
                                scalar.activation(
                                    sg_sb[0:icw, hh, ic, 0:G],
                                    ctx.bank(bi)[0:icw, 0:G], AF.Exp,
                                    bias=mrow[0:icw, hh, ic, 1:2], scale=0.125)
                    blk(ctx, scalar=f_sexp)

                    def f_snorm(vector):
                        for hh in range(2):
                            for ic in range(4):
                                icw = ic_sizes[ic]
                                sl = sg_sb[0:icw, hh, ic, 0:G]
                                edge(ctx, vector, vector.reduce_sum(
                                    mrow[0:icw, hh, ic, 0:1], sl, axis=AX), "edv")
                                edge(ctx, vector, vector.reciprocal(
                                    mrow[0:icw, hh, ic, 1:2],
                                    mrow[0:icw, hh, ic, 0:1]), "edv")
                                edge(ctx, vector, vector.tensor_scalar(
                                    sl, sl, mrow[0:icw, hh, ic, 1:2],
                                    None, ALU.mult), "edv")
                    blk(ctx, vector=f_snorm)

                    def e_st(tensor, job, bi):
                        hh, jc = job
                        ins = None
                        for ic in range(4):
                            icw = ic_sizes[ic]
                            ins = tensor.transpose(
                                ctx.bankb(bi)[0:jc_sizes[jc],
                                              ic * 128:ic * 128 + icw],
                                sg_sb[0:icw, hh, ic,
                                      jc * 128:jc * 128 + jc_sizes[jc]],
                                t["eye_b"][0:icw, 0:icw])
                        return ins

                    def e_stev(vector, job, bi):
                        hh, jc = job
                        return vector.tensor_copy(
                            sgT_sb[0:jc_sizes[jc], hh, jc, 0:G],
                            ctx.bankb(bi)[0:jc_sizes[jc], 0:G])

                    pipe(ctx, [(hh, jc) for hh in range(2) for jc in range(4)],
                         e_st, e_stev, banks=(0, 1))

                    def e_go(tensor, hh, bi):
                        h = heads[hh]
                        ins = None
                        for jc in range(4):
                            jw = jc_sizes[jc]
                            ins = tensor.matmul(
                                ctx.bank(bi)[0:64, 0:G],
                                v_ext[0:jw, jc, h * 65:h * 65 + 64],
                                sgT_sb[0:jw, hh, jc, 0:G],
                                start=(jc == 0), stop=(jc == 3))
                        return ins

                    def e_goev(vector, hh, bi):
                        h = heads[hh]
                        hid = hg * 4 + h
                        hchunk = hid // 2
                        hpar = (hid % 2) * 64
                        i1 = vector.tensor_copy(
                            sg_sb[0:64, 0, 0, 0:G], ctx.bank(bi)[0:64, 0:G])
                        edge(ctx, vector, i1, "edv")
                        return vector.tensor_mul(
                            go_sb[hpar:hpar + 64, hchunk, 0:G],
                            sg_sb[0:64, 0, 0, 0:G],
                            t["c_mglob"][0:64, 0:G])

                    pipe(ctx, [0, 1], e_go, e_goev, banks=(2, 3))

        stc = ctx.sem["st"]

        kv_tgt = stc.n + 16

        def f_kvout(sync):
            sync.dma_start(out=ctx.io["ag_kv_in"], in_=kvden[:, :]).then_inc(stc.h, 16)
            sync.wait_ge(stc.h, kv_tgt)
        stc.n = kv_tgt
        blk(ctx, sync=f_kvout)

        allgather(ctx, ctx.io["ag_kv_in"], ctx.io["ag_kv_out"])

        kvp = ctx.sbuf("kvp", [65, NHEAD * HD], F32, es)
        ldgn = ldg.n + 16

        def f_kvp(gpsimd):
            gpsimd.indirect_dma_start(
                kvp[:, :], None, ctx.io["ag_kv_out"],
                IndirectOffsetOnAxis(t["c_idx_kv"][0:65, 0:1], 0)).then_inc(ldg.h, 16)
            gpsimd.wait_ge(ldg.h, ldgn)
        ldg.n = ldgn
        blk(ctx, gpsimd=f_kvp)

        def f_kvadd(vector):
            vector.tensor_add(kvden[:, :], kvden[:, :], kvp[:, :])
        blk(ctx, vector=f_kvadd)

        kv_bf = ctx.sbuf("kv_bf", [P, KC, HD], BF16, es)
        rdeni = ctx.sbuf("rdeni", [HD, NHEAD], F32, es)

        def f_dent(tensor):
            for h in range(NHEAD):
                tensor.transpose(ctx.bank(0)[0:64, h:h + 1],
                                 kvden[64:65, h * 64:(h + 1) * 64],
                                 t["eye_f"][64:65, 64:65])
        blk(ctx, tensor=f_dent)

        def f_denev(vector):
            edge(ctx, vector, vector.tensor_copy(rdeni[:, :],
                                                 ctx.bank(0)[0:64, 0:NHEAD]), "edv")
            edge(ctx, vector, vector.reciprocal(rdeni[:, :], rdeni[:, :]), "edv")
            src4 = kvden[0:64, :].rearrange("p (hc par e) -> p hc par e",
                                            par=2, e=64)
            vector.tensor_copy(kv_bf[0:64, :, :], src4[:, :, 0, :])
            vector.tensor_copy(kv_bf[64:128, :, :], src4[:, :, 1, :])
        blk(ctx, vector=f_denev)

        def e_lo(tensor, job, bi):
            h, ib = job
            hchunk = h // 2
            hpar = (h % 2) * 64
            return tensor.matmul(ctx.bank(bi)[0:64, :],
                                 kv_bf[hpar:hpar + 64, hchunk, :],
                                 q_fm[hpar:hpar + 64, hchunk, ts512(ib)],
                                 start=True, stop=True)

        def e_loev(vector, job, bi):
            h, ib = job
            hchunk = h // 2
            hpar = (h % 2) * 64
            dst = attn_out[hpar:hpar + 64, hchunk, ts512(ib)]
            if ib > 0:
                return vector.tensor_scalar(dst, ctx.bank(bi)[0:64, :],
                                            rdeni[0:64, h:h + 1], None, ALU.mult)
            i1 = vector.tensor_scalar(sg_tmp[0:64, :], ctx.bank(bi)[0:64, :],
                                      rdeni[0:64, h:h + 1], None, ALU.mult)
            edge(ctx, vector, i1, "edv")
            i2 = vector.tensor_mul(sg_tmp[0:64, :], sg_tmp[0:64, :],
                                   t["c_mloc"][0:64, :])
            edge(ctx, vector, i2, "edv")
            i3 = vector.tensor_copy(dst, sg_tmp[0:64, :])
            edge(ctx, vector, i3, "edv")
            return vector.tensor_add(dst[:, 0:G], dst[:, 0:G],
                                     go_sb[hpar:hpar + 64, hchunk, 0:G])

        sg_tmp = ctx.sbuf("sg_tmp", [HD, 512], BF16, es)
        pipe(ctx, [(h, ib) for h in range(NHEAD) for ib in range(NTB)],
             e_lo, e_loev, banks=(0, 1))

        with ExitStack() as ew:
            wo = load_w(ctx, ctx.io["L0_Wout"], (0, D), BF16, ew, name="wo")

            def evac_wo(vector, ps, mc, tb):
                return vector.tensor_add(t["X"][:, mc, ts512(tb)],
                                         t["X"][:, mc, ts512(tb)], ps)

            proj_fm(ctx, wo["w"], attn_out, evac_wo)


def spipe(ctx, nslabs, jps, emit_load, emit_mm, emit_ev, ev_eng="vector",
          banks=(0, 1)):
    ldg, pe, ve = ctx.sem["ldg"], ctx.sem["pe"], ctx.sem["ve"]
    ldg0, pe0, ve0 = ldg.n, pe.n, ve.n
    nbuf = len(banks)

    def f_g(gpsimd):
        for s in range(nslabs):
            if s >= 2:
                gpsimd.wait_ge(pe.h, pe0 + (s - 1) * jps)
            emit_load(gpsimd, s, s % 2).then_inc(ldg.h, 16)

    def f_mm(tensor):
        i = 0
        for s in range(nslabs):
            tensor.wait_ge(ldg.h, ldg0 + 16 * (s + 1))
            for j in range(jps):
                if i >= nbuf:
                    tensor.wait_ge(ve.h, ve0 + i - nbuf + 1)
                emit_mm(tensor, s, j, banks[i % nbuf], s % 2).then_inc(pe.h, 1)
                i += 1

    def f_ev(eng):
        i = 0
        for s in range(nslabs):
            for j in range(jps):
                eng.wait_ge(pe.h, pe0 + i + 1)
                emit_ev(eng, s, j, banks[i % nbuf]).then_inc(ve.h, 1)
                i += 1

    ldg.n += 16 * nslabs
    pe.n += nslabs * jps
    ve.n += nslabs * jps
    blk(ctx, gpsimd=f_g, **{"tensor": f_mm, ev_eng: f_ev})


def dump_dbg(ctx, ap, cols):
    stc = ctx.sem["st"]
    tgt = stc.n + 16

    def f_d(gpsimd):
        gpsimd.dma_start(out=ctx.io["dbg"][:, 0:cols], in_=ap).then_inc(stc.h, 16)
        gpsimd.wait_ge(stc.h, tgt)
    stc.n = tgt
    blk(ctx, gpsimd=f_d)


def phase2(ctx):
    nc, t = ctx.nc, ctx.t
    pe, ve, ac, ldg = ctx.sem["pe"], ctx.sem["ve"], ctx.sem["ac"], ctx.sem["ldg"]
    with ExitStack() as es:
        xn = ctx.sbuf("xn", [P, KC, T], BF16, es)
        wrow = ctx.sbuf("wrow", [8, NTC, P], F32, es)

        with ExitStack() as eA:
            st = stats(ctx, t["X"], "lnln", eA, bcast=("mult", "mu", "plain"))
            gcol, bcol = t["c_L0_g"], t["c_L0_b"]

            t1 = ctx.sbuf("p2t1", [P, 512], F32R, eA)
            t2 = ctx.sbuf("p2t2", [P, 512], F32R, eA)

            def f_xn(vector):
                for kc in range(KC):
                    for tb in range(NTB):
                        i1 = vector.tensor_sub(t1[:, :], t["X"][:, kc, ts512(tb)],
                                               st["b_mu"][:, tb, :])
                        edge(ctx, vector, i1, "edv")
                        i2 = vector.tensor_mul(t2[:, :], t1[:, :], st["b_mult"][:, tb, :])
                        edge(ctx, vector, i2, "edv")
                        edge(ctx, vector,
                             vector.tensor_scalar(xn[:, kc, ts512(tb)], t2[:, :],
                                                  gcol[:, kc:kc + 1], bcol[:, kc:kc + 1],
                                                  ALU.mult, ALU.add), "edv")
            blk(ctx, vector=f_xn)

            logits = ctx.sbuf("logits", [P, NTC, 8], F32, eA)
            probs = ctx.sbuf("probs", [P, NTC, 8], F32, eA)
            mx = ctx.sbuf("mx", [P, NTC, 1], F32, eA)
            oh = ctx.sbuf("ohs", [P, NTC, 8], F32, eA)
            wdense = ctx.sbuf("wdense", [P, NTC, 8], F32, eA)

            with ExitStack() as ew:
                wd = load_w(ctx, ctx.io["L0_router"], (0, 8), F32R, ew,
                            name="wrt", want_sum=True, wcol=gcol, bcol=bcol)

                def evac_rt(vector, ps, tc, j):
                    return vector.tensor_scalar(logits[:, tc, :], ps[:, 0:8],
                                                st["rstd_col"][:, tc:tc + 1],
                                                None, ALU.mult)
                proj_tm(ctx, wd, st, t["X"], 8, evac_rt)

            if DEBUG == "logits":
                dump_dbg(ctx, logits[:, :, :].rearrange("p tc e -> p (tc e)"), NTC * 8)

            def f_resid(vector):
                for kc in range(KC):
                    for tb in range(NTB):
                        i1 = vector.tensor_sub(t1[:, :], t["X"][:, kc, ts512(tb)],
                                               st["b_mu"][:, tb, :])
                        edge(ctx, vector, i1, "edv")
                        i2 = vector.tensor_mul(t2[:, :], t1[:, :], st["b_plain"][:, tb, :])
                        edge(ctx, vector, i2, "edv")
                        edge(ctx, vector,
                             vector.tensor_add(t["X"][:, kc, ts512(tb)],
                                               t["X"][:, kc, ts512(tb)], t2[:, :]),
                             "edv")
            blk(ctx, vector=f_resid)

            def f_pexp(scalar):
                scalar.activation(probs[:, :, :], logits[:, :, :], AF.Exp,
                                  scale=1.0 / 1.5)
            blk(ctx, scalar=f_pexp)

            def f_top2(vector):
                edge(ctx, vector, vector.reduce_sum(mx[:, :, :], probs[:, :, :],
                                                    axis=AX), "edv")
                edge(ctx, vector, vector.reciprocal(mx[:, :, :], mx[:, :, :]), "edv")
                edge(ctx, vector, vector.tensor_mul(
                    probs[:, :, :], probs[:, :, :],
                    mx[:, :, :].broadcast_to([P, NTC, 8])), "edv")
                edge(ctx, vector, vector.reduce_max(mx[:, :, :], probs[:, :, :],
                                                    axis=AX), "edv")
                edge(ctx, vector, vector.tensor_tensor(
                    oh[:, :, :], probs[:, :, :],
                    mx[:, :, :].broadcast_to([P, NTC, 8]), ALU.is_equal), "edv")
                edge(ctx, vector, vector.tensor_mul(
                    wdense[:, :, :], oh[:, :, :],
                    mx[:, :, :].broadcast_to([P, NTC, 8])), "edv")
                edge(ctx, vector, vector.tensor_scalar(
                    oh[:, :, :], oh[:, :, :], -1.0, 1.0, ALU.mult, ALU.add), "edv")
                edge(ctx, vector, vector.tensor_mul(
                    probs[:, :, :], probs[:, :, :], oh[:, :, :]), "edv")
                edge(ctx, vector, vector.reduce_max(mx[:, :, :], probs[:, :, :],
                                                    axis=AX), "edv")
                edge(ctx, vector, vector.tensor_tensor(
                    oh[:, :, :], probs[:, :, :],
                    mx[:, :, :].broadcast_to([P, NTC, 8]), ALU.is_equal), "edv")
                edge(ctx, vector, vector.tensor_mul(
                    oh[:, :, :], oh[:, :, :],
                    mx[:, :, :].broadcast_to([P, NTC, 8])), "edv")
                edge(ctx, vector, vector.tensor_add(
                    wdense[:, :, :], wdense[:, :, :], oh[:, :, :]), "edv")
            blk(ctx, vector=f_top2)

            def e_wt(tensor, tc, bi):
                return tensor.transpose(ctx.bank(bi)[0:8, 0:128],
                                        wdense[:, tc, :], t["eye_f"][:, :])

            def e_wtev(vector, tc, bi):
                return vector.tensor_copy(wrow[:, tc, :], ctx.bank(bi)[0:8, 0:128])

            pipe(ctx, list(range(NTC)), e_wt, e_wtev, banks=(0, 1))
            if DEBUG == "wdense":
                dump_dbg(ctx, wdense[:, :, :].rearrange("p tc e -> p (tc e)"), NTC * 8)

        if DEBUG == "noexp":
            return
        brow = ctx.sbuf("brow", [1, T], BF16, es)
        Bcur = ctx.sbuf("Bcur", [P, NTB, 512], BF16, es)
        tmp = ctx.sbuf("p2tmp", [P, 512], F32R, es)
        h = ctx.sbuf("hbuf", [P, 32, 1024], BF16, es)
        wslab = ctx.sbuf("wslab", [P, 64, P], BF16, es)
        w1sb = [wslab[:, 0:KC, :], wslab[:, 32:32 + KC, :]]
        w2sb = [wslab[:, 0:32, :], wslab[:, 32:64, :]]

        for e in range(8):
            stc = ctx.sem["st"]
            br_tgt = stc.n + 16

            def f_brow(gpsimd, e=e, br_tgt=br_tgt):
                gpsimd.dma_start(out=brow[0:1, :],
                                 in_=wrow[e:e + 1, :, :].rearrange("o tc p -> o (tc p)")
                                 ).then_inc(stc.h, 16)
                gpsimd.wait_ge(stc.h, br_tgt)
            stc.n = br_tgt
            blk(ctx, gpsimd=f_brow)

            def e_bmm(tensor, tb, bi):
                return tensor.matmul(ctx.bank(bi), t["c_ones_rowb"][0:1, 0:P],
                                     brow[0:1, ts512(tb)], start=True, stop=True)

            def e_bev(vector, tb, bi):
                return vector.tensor_copy(Bcur[:, tb, :], ctx.bank(bi))

            pipe(ctx, list(range(NTB)), e_bmm, e_bev, banks=(0, 1))

            for tb2 in range(2):
                def el_h(gpsimd, s, buf, e=e):
                    return gpsimd.dma_start(
                        out=w1sb[buf][:, :, :].rearrange("p kc m -> p (kc m)"),
                        in_=ctx.io["L0_W1H"][e, s].rearrange("p kc m -> p (kc m)"))

                def em_h(tensor, s, j, bi, buf, tb2=tb2):
                    ins = None
                    for kc in range(KC):
                        ins = tensor.matmul(
                            ctx.bank(bi), w1sb[buf][:, kc, :],
                            xn[:, kc, tb2 * 1024 + j * 512:tb2 * 1024 + (j + 1) * 512],
                            start=(kc == 0), stop=(kc == KC - 1))
                    return ins

                def ev_h(scalar, s, j, bi):
                    return scalar.activation(h[:, s, ts512(j)], ctx.bank(bi), AF.Silu)

                spipe(ctx, 32, 2, el_h, em_h, ev_h, ev_eng="scalar", banks=(0, 1))

                def el_y(gpsimd, s, buf, e=e):
                    return gpsimd.dma_start(
                        out=w2sb[buf][:, :, :].rearrange("p fc m -> p (fc m)"),
                        in_=ctx.io["L0_W2H"][e, s].rearrange("p fc m -> p (fc m)"),
                        max_dma_last_dim=2048)

                def em_y(tensor, s, j, bi, buf):
                    ins = None
                    for fc in range(32):
                        ins = tensor.matmul(ctx.bank(bi), w2sb[buf][:, fc, :],
                                            h[:, fc, ts512(j)],
                                            start=(fc == 0), stop=(fc == 31))
                    return ins

                def ev_y(vector, s, j, bi, tb2=tb2):
                    tb = tb2 * 2 + j
                    i1 = vector.tensor_mul(tmp[:, :], ctx.bank(bi), Bcur[:, tb, :])
                    edge(ctx, vector, i1, "edv")
                    return vector.tensor_add(t["X"][:, s, ts512(tb)],
                                             t["X"][:, s, ts512(tb)], tmp[:, :])

                spipe(ctx, 8, 2, el_y, em_y, ev_y, ev_eng="vector", banks=(2, 3))



def phase3(ctx):
    nc, t = ctx.nc, ctx.t
    pe, ve, ac, ldg = ctx.sem["pe"], ctx.sem["ve"], ctx.sem["ac"], ctx.sem["ldg"]
    stc, cc = ctx.sem["st"], ctx.sem["cc"]
    psA, psB = t["psA"], t["psB"]
    with ExitStack() as es:
        rows3 = ctx.sbuf("rows3", [1, 8, 16], F32, es)
        with ExitStack() as eA:
            st = stats(ctx, t["X"], "lnln", eA, bcast=("mult", "mu", "plain"))
            qk_sb = ctx.sbuf("qk_sb", [P, KC, T], BF16, eA)
            q2own = ctx.sbuf("q2own", [P, NTC, 16], F32, eA)
            sqq = ctx.sbuf("sqq", [P, T], F32R, eA)
            q2st = st["_rows"][0:1, 0, :]

            for wname in ["L1_Wqk_t", "L1_Wv_t"]:
                isv = wname == "L1_Wv_t"
                for quarter in range(4):
                    m0 = quarter * 2
                    with ExitStack() as ew:
                        wd = load_w(ctx, ctx.io[wname], (m0 * 128, (m0 + 2) * 128),
                                    F32R, ew, name="w3", want_sum=True,
                                    wcol=t["c_L1_ln1_w"], bcol=t["c_L1_ln1_b"])

                        def evac(vector, ps, mc, tb):
                            return vector.tensor_mul(qk_sb[:, mc, ts512(tb)], ps,
                                                     st["b_mult"][:, tb, :])
                        proj_fm(ctx, wd["w"], t["X"], evac, mcs=range(m0, m0 + 2),
                                st=st, wd=wd, m0=m0)

                if not isv:
                    for kc in range(KC):
                        def f_sq(scalar, kc=kc):
                            scalar.activation(sqq[:, :], qk_sb[:, kc, :], AF.Square)
                        blk(ctx, scalar=f_sq)
                        for par in range(2):
                            h = kc * 2 + par
                            base = par * 64

                            def f_rmm(tensor, base=base):
                                for tb in range(NTB):
                                    tensor.matmul(psA[0:1, tb, :],
                                                  t["ones_col"][base:base + 64, 0:1],
                                                  sqq[base:base + 64, ts512(tb)],
                                                  start=True, stop=True)
                            blk(ctx, tensor=f_rmm)

                            def f_rev(vector):
                                for tb in range(NTB):
                                    vector.tensor_copy(q2st[0:1, ts512(tb)],
                                                       psA[0:1, tb, :])
                            blk(ctx, vector=f_rev)

                            def f_rtr(tensor):
                                for tc in range(NTC):
                                    tensor.transpose(psB[:, 0, tc:tc + 1],
                                                     q2st[0:1, ts128(tc)],
                                                     t["eye_f"][0:1, 0:1])
                            blk(ctx, tensor=f_rtr)

                            def f_q2ev(vector, h=h):
                                vector.tensor_copy(q2own[:, :, h],
                                                   psB[:, 0, 0:NTC])
                            blk(ctx, vector=f_q2ev)

                    tg1 = stc.n + 32

                    def f_out1(sync):
                        sync.dma_start(
                            out=ctx.io["ag_q2_in"],
                            in_=q2own[:, :, :].rearrange("p tc h -> p (tc h)")
                        ).then_inc(stc.h, 16)
                        sync.dma_start(
                            out=ctx.io["ag_qkv_in"][0:D, :].rearrange(
                                "(kc p) t -> p kc t", p=P),
                            in_=qk_sb[:, :, :]).then_inc(stc.h, 16)
                        sync.wait_ge(stc.h, tg1)
                    stc.n = tg1
                    blk(ctx, sync=f_out1)

            tg2 = stc.n + 16

            def f_out2(sync):
                sync.dma_start(
                    out=ctx.io["ag_qkv_in"][D:2 * D, :].rearrange(
                        "(kc p) t -> p kc t", p=P),
                    in_=qk_sb[:, :, :]).then_inc(stc.h, 16)
                sync.wait_ge(stc.h, tg2)
            stc.n = tg2
            blk(ctx, sync=f_out2)

            allgather(ctx, ctx.io["ag_qkv_in"], ctx.io["ag_qkv_out"])
            allgather(ctx, ctx.io["ag_q2_in"], ctx.io["ag_q2_out"])

            tg3 = stc.n + 16

            def f_spill(sync):
                sync.dma_start(out=ctx.io["xspill"].rearrange("(kc p) t -> p kc t", p=P),
                               in_=t["X"][:, :, :]).then_inc(stc.h, 16)
                sync.wait_ge(stc.h, tg3)
            stc.n = tg3
            blk(ctx, sync=f_spill)

            def f_xo(vector):
                for kc in range(KC):
                    for tb in range(NTB):
                        sl = t["X"][:, kc, ts512(tb)]
                        i1 = vector.tensor_sub(sqq[:, ts512(tb)], sl,
                                               st["b_mu"][:, tb, :])
                        edge(ctx, vector, i1, "edv")
                        edge(ctx, vector,
                             vector.tensor_mul(sl, sqq[:, ts512(tb)],
                                               st["b_plain"][:, tb, :]), "edv")
            blk(ctx, vector=f_xo)

        q2T = ctx.sbuf("q2T", [P, 2, NTC, 16], F32, es)
        q2m = ctx.sbuf("q2m", [P, 16], F32, es)
        bc3 = ctx.sbuf("bc3", [P, 4, 16], F32, es)
        bias_sb = ctx.sbuf("bias3", [P, 2, NTC, 16], F32, es)
        attn_heads = ctx.sbuf("ath", [P, KC, T], BF16, es)
        v_ext2 = [ctx.sbuf("v_ext3", [P, 32, 130], BF16, es) for _ in range(2)]
        v_fm2 = [ctx.sbuf("vfp", [P, 2, T], BF16, es) for _ in range(2)]
        vtmp2 = [ctx.sbuf("vtmp", [P, 32, P], BF16, es) for _ in range(2)]
        qk_head2 = [ctx.sbuf("qkh", [HD, 2, T], BF16, es) for _ in range(2)]
        rhs_tile2 = [ctx.sbuf("rhst", [HD, T], BF16, es) for _ in range(2)]
        E2 = ctx.sbuf("E2", [P, 2, T], BF16, es)
        denst = ctx.sbuf("denst", [1, 512], F32, es)
        tmp3 = ctx.sbuf("tmp3", [P, 512], F32R, es)

        tg4 = ldg.n + 32

        def f_gq2(gpsimd):
            for half in range(2):
                gpsimd.indirect_dma_start(
                    q2T[:, half, :, :].rearrange("p tc h -> p (tc h)"), None,
                    ctx.io["ag_q2_out"],
                    IndirectOffsetOnAxis(t["c_idx_q2"][:, half:half + 1], 0)
                ).then_inc(ldg.h, 16)
            gpsimd.wait_ge(ldg.h, tg4)
        ldg.n = tg4
        blk(ctx, gpsimd=f_gq2)

        def f_q2m(vector):
            for h in range(NHEAD):
                vector.tensor_reduce(q2m[:, h:h + 1], q2T[:, :, :, h],
                                     mybir.AxisListType.XY, ALU.max)
        blk(ctx, vector=f_q2m)

        def f_q2t(tensor):
            tensor.transpose(psA[0:16, 0, 0:P], q2m[:, :], t["eye_f"][:, :])
        blk(ctx, tensor=f_q2t)

        q2mt = ctx.sbuf("q2mt", [16, P], F32, es)

        def f_q2r(vector):
            edge(ctx, vector, vector.tensor_copy(q2mt[:, :], psA[0:16, 0, 0:P]), "edv")
            edge(ctx, vector, vector.reduce_max(q2mt[:, 0:1], q2mt[:, :],
                                                axis=AX), "edv")
        blk(ctx, vector=f_q2r)

        def f_q2t2(tensor):
            tensor.transpose(psA[0:1, 0, 0:16], q2mt[0:16, 0:1],
                             t["eye_f"][0:16, 0:16])
        blk(ctx, tensor=f_q2t2)

        r3 = lambda i: rows3[0:1, i, :]

        def f_m0(vector):
            edge(ctx, vector, vector.tensor_copy(r3(0), t["c_L1_tau_t"][0:1, :]), "edv")
            edge(ctx, vector, vector.tensor_scalar(r3(1), r3(0), 0.125, None, ALU.mult), "edv")
            edge(ctx, vector, vector.tensor_scalar(r3(2), r3(1), -1.0, None, ALU.mult), "edv")
            edge(ctx, vector, vector.tensor_copy(r3(3), psA[0:1, 0, 0:16]), "edv")
            edge(ctx, vector, vector.tensor_scalar(r3(4), r3(1), -1.0, None, ALU.mult), "edv")
            edge(ctx, vector, vector.tensor_max(r3(4), r3(4), r3(1)), "edv")
            edge(ctx, vector, vector.tensor_scalar(r3(5), r3(1), 0.0, None, ALU.is_gt), "edv")
            edge(ctx, vector, vector.tensor_scalar(r3(5), r3(5), -2.0, 3.0, ALU.mult, ALU.add), "edv")
            edge(ctx, vector, vector.tensor_mul(r3(6), r3(4), r3(3)), "edv")
            edge(ctx, vector, vector.tensor_mul(r3(6), r3(6), r3(5)), "edv")
            edge(ctx, vector, vector.tensor_scalar(r3(7), r3(1), 2.0, None, ALU.mult), "edv")
        blk(ctx, vector=f_m0)

        def e_b3(tensor, i, bi):
            sl = [7, 2, 6][i]
            return tensor.matmul(ctx.bank(bi)[:, 0:16], t["ones_row"][0:1, 0:P],
                                 rows3[0:1, sl, :], start=True, stop=True)

        def e_b3ev(vector, i, bi):
            return vector.tensor_copy(bc3[:, i, :], ctx.bank(bi)[:, 0:16])

        pipe(ctx, [0, 1, 2], e_b3, e_b3ev, banks=(0, 1))

        def f_bias(vector):
            for h in range(NHEAD):
                vector.tensor_scalar(
                    bias_sb[:, :, :, h].rearrange("p a b -> p (a b)"),
                    q2T[:, :, :, h].rearrange("p a b -> p (a b)"),
                    bc3[:, 1, h:h + 1], bc3[:, 2, h:h + 1],
                    ALU.mult, ALU.subtract)
        blk(ctx, vector=f_bias)

        def f_vones(vector):
            for ve_ in v_ext2:
                vector.memset(ve_[:, :, :].rearrange(
                    "p j (g c) -> p j g c", c=65)[:, :, :, 64:65], 1.0)
        blk(ctx, vector=f_vones)

        vg_tgt = [0] * 9
        vt_tgt = [0] * 9
        qk_tgt = [0] * 17

        def emit_gv(gpsimd, p):
            for si in range(2):
                gpsimd.indirect_dma_start(
                    v_fm2[p % 2][:, si, :], None, ctx.io["ag_qkv_out"],
                    IndirectOffsetOnAxis(
                        t["c_idx_v"][:, p * 2 + si:p * 2 + si + 1], 0)
                ).then_inc(ldg.h, 16)
            ldg.n += 32
            vg_tgt[p] = ldg.n

        def emit_gqk(gpsimd, hh):
            for si in range(2):
                gpsimd.indirect_dma_start(
                    qk_head2[hh % 2][:, si, :], None, ctx.io["ag_qkv_out"],
                    IndirectOffsetOnAxis(
                        t["c_idx_qk"][:, 2 * hh + si:2 * hh + si + 1], 0)
                ).then_inc(ldg.h, 16)
            gpsimd.indirect_dma_start(
                rhs_tile2[hh % 2][:, :], None, ctx.io["ag_qkv_out"],
                IndirectOffsetOnAxis(t["c_idx_rhs"][:, hh:hh + 1], 0)
            ).then_inc(ldg.h, 16)
            ldg.n += 48
            qk_tgt[hh] = ldg.n

        def emit_vtr(sync, p):
            sync.wait_ge(ldg.h, vg_tgt[p])
            for half in range(2):
                for tcl in range(NTC):
                    sync.dma_start(out=vtmp2[p % 2][:, half * NTC + tcl, :],
                                   in_=v_fm2[p % 2][:, half, ts128(tcl)],
                                   transpose=True).then_inc(stc.h, 16)
            stc.n += 16 * 32
            vt_tgt[p] = stc.n

        def f_pro(gpsimd):
            emit_gv(gpsimd, 0)
            emit_gqk(gpsimd, 0)
            gpsimd.wait_ge(ldg.h, qk_tgt[0])
        blk(ctx, gpsimd=f_pro)

        def f_pro2(sync):
            emit_vtr(sync, 0)
            sync.wait_ge(stc.h, vt_tgt[0])
        blk(ctx, sync=f_pro2)

        def f_pro3(vector):
            vector.tensor_copy(
                v_ext2[0][:, :, :].rearrange("p j (g c) -> p j g c",
                                             c=65)[:, :, :, 0:64],
                vtmp2[0][:, :, :].rearrange("p j (g c) -> p j g c", c=64))
        blk(ctx, vector=f_pro3)

        for h in range(NHEAD):
            c2 = h // 2
            v_ext = v_ext2[c2 % 2]
            qk_head = qk_head2[h % 2]
            rhs_tile = rhs_tile2[h % 2]

            def f_pref(gpsimd, h=h, c2=c2):
                if h + 1 < NHEAD:
                    emit_gqk(gpsimd, h + 1)
                if h % 2 == 0 and c2 + 1 < 8:
                    emit_gv(gpsimd, c2 + 1)
                gpsimd.wait_ge(ldg.h, max(qk_tgt[h], vg_tgt[c2]))

            def f_pref_t(sync, h=h, c2=c2):
                if h % 2 == 1 and c2 + 1 < 8:
                    emit_vtr(sync, c2 + 1)

            def f_pref_v(vector, h=h, c2=c2):
                pass
            blk(ctx, gpsimd=f_pref, sync=f_pref_t)

            ac0, pe0 = ac.n, pe.n
            gslot = h % 2
            NU = 64

            def dot_count(u):
                return u + 1 if u < 2 else 2 * u

            def f_att_t(tensor, h=h, ac0=ac0, pe0=pe0, gslot=gslot):
                def emit_av(u):
                    jc, hf = u // 2, u % 2
                    ins = None
                    for ib in range(2):
                        ins = tensor.matmul(
                            psB[0:65, hf * 2 + ib, :],
                            v_ext[:, jc, gslot * 65:gslot * 65 + 65],
                            E2[:, u % 2, ts512(ib)],
                            start=(jc == 0), stop=(jc == 31))
                    ins.then_inc(pe.h, 1)

                for u in range(NU):
                    jc, hf = u // 2, u % 2
                    if u >= 2:
                        tensor.wait_ge(ac.h, ac0 + u - 1)
                        emit_av(u - 2)
                    lhsT = qk_head[0:64, jc // NTC, ts128(jc % NTC)]
                    ins = None
                    for ib in range(2):
                        ins = tensor.matmul(psA[:, 2 * (u % 2) + ib, :], lhsT,
                                            rhs_tile[0:64, ts512(hf * 2 + ib)],
                                            start=True, stop=True)
                    ins.then_inc(pe.h, 1)
                for u in (NU - 2, NU - 1):
                    tensor.wait_ge(ac.h, ac0 + u + 1)
                    emit_av(u)

            def f_att_s(scalar, h=h, ac0=ac0, pe0=pe0):
                for u in range(NU):
                    jc = u // 2
                    scalar.wait_ge(pe.h, pe0 + dot_count(u))
                    scalar.activation(E2[:, u % 2, 0:1024],
                                      psA[:, 2 * (u % 2):2 * (u % 2) + 2, :],
                                      AF.Exp,
                                      bias=bias_sb[:, jc // NTC, jc % NTC, h:h + 1],
                                      scale=bc3[:, 0, h:h + 1]).then_inc(ac.h, 1)

            ac.n += NU
            pe.n += 2 * NU
            blk(ctx, tensor=f_att_t, scalar=f_att_s)

            hchunk, hpar = h // 2, (h % 2) * 64

            pe0e, ve0e = pe.n, ve.n

            def f_ev_t(tensor):
                for ib in range(NTB):
                    tensor.wait_ge(ve.h, ve0e + 2 * ib + 1)
                    tensor.matmul(psA[:, 0, :], t["ones_row"][0:1, 0:P],
                                  denst[0:1, :], start=True,
                                  stop=True).then_inc(pe.h, 1)

            def f_ev_v(vector, h=h, c2=c2):
                if h % 2 == 1 and c2 + 1 < 8:
                    vector.wait_ge(stc.h, vt_tgt[c2 + 1])
                    vector.tensor_copy(
                        v_ext2[(c2 + 1) % 2][:, :, :].rearrange(
                            "p j (g c) -> p j g c", c=65)[:, :, :, 0:64],
                        vtmp2[(c2 + 1) % 2][:, :, :].rearrange(
                            "p j (g c) -> p j g c", c=64))
                for ib in range(NTB):
                    i1 = vector.tensor_copy(denst[0:1, :], psB[64:65, ib, :])
                    edge(ctx, vector, i1, "edv")
                    vector.reciprocal(denst[0:1, :],
                                      denst[0:1, :]).then_inc(ve.h, 1)
                    edge(ctx, vector, vector.tensor_copy(
                        attn_heads[hpar:hpar + 64, hchunk, ts512(ib)],
                        psB[0:64, ib, :]), "edv")
                    vector.wait_ge(pe.h, pe0e + ib + 1)
                    vector.tensor_mul(
                        attn_heads[hpar:hpar + 64, hchunk, ts512(ib)],
                        attn_heads[hpar:hpar + 64, hchunk, ts512(ib)],
                        psA[0:64, 0, :]).then_inc(ve.h, 1)

            pe.n += NTB
            ve.n += 2 * NTB
            blk(ctx, tensor=f_ev_t, vector=f_ev_v)

        with ExitStack() as ew:
            wo = load_w(ctx, ctx.io["L1_Wo_t"], (0, D), BF16, ew, name="wo3")

            def evac_wo(vector, ps, mc, tb):
                i1 = vector.tensor_scalar(tmp3[:, :], ps,
                                          t["c_L1_g1"][:, mc:mc + 1], None, ALU.mult)
                edge(ctx, vector, i1, "edv")
                return vector.tensor_add(t["X"][:, mc, ts512(tb)],
                                         t["X"][:, mc, ts512(tb)], tmp3[:, :])

            proj_fm(ctx, wo["w"], attn_heads, evac_wo)



def phase4(ctx):
    nc, t = ctx.nc, ctx.t
    pe, ve, ac, ldg = ctx.sem["pe"], ctx.sem["ve"], ctx.sem["ac"], ctx.sem["ldg"]
    stc = ctx.sem["st"]
    psA, psB = t["psA"], t["psB"]
    with ExitStack() as es:
        qk_tm = ctx.sbuf("qk_tm", [P, NTC, D], BF16, es)
        eB = ExitStack()
        st4 = stats(ctx, t["X"], "ln", eB, bcast=("mult",), nsq=1)
        v_c = ctx.sbuf("v_c", [P, KC, T], BF16, eB)
        dot_sb = ctx.sbuf("dot_sb", [P, KC, P], F32, eB)
        rows4 = ctx.sbuf("rows4", [1, 2, 8], F32, eB)
        sgc = ctx.sbuf("sgc", [P, 8], F32, eB)
        q2c = ctx.sbuf("q2c", [P, 1], F32, eB)
        t4 = ctx.sbuf("t4", [P, P], F32, eB)
        lt = ctx.sbuf("lt", [P, P], F32, eB)
        sm4 = ctx.sbuf("sm4", [P, P], F32, eB)
        attT = ctx.sbuf("attT", [P, P], BF16, eB)
        q2rst = ctx.sbuf("q2rst", [1, P], F32, eB)
        mx4 = ctx.sbuf("mx4", [P, 2], F32, eB)

        for wname, istm in [("L1_Wqk_c", True), ("L1_Wv_c", False)]:
            for m0 in range(KC):
                with ExitStack() as ew:
                    wd = load_w(ctx, ctx.io[wname], (m0 * 128, (m0 + 1) * 128),
                                F32R, ew, name="w4", want_sum=True,
                                wcol=t["c_L1_ln2_w"], bcol=t["c_L1_ln2_b"])
                    if istm:
                        def evac_tm(vector, ps, tc, j, m0=m0):
                            return vector.tensor_scalar(
                                qk_tm[:, tc, m0 * 128:(m0 + 1) * 128], ps,
                                st4["rstd_col"][:, tc:tc + 1], None, ALU.mult)
                        proj_tm(ctx, wd, st4, t["X"], 128, evac_tm)
                    else:
                        def evac_fm(vector, ps, mc, tb):
                            return vector.tensor_mul(v_c[:, mc, ts512(tb)], ps,
                                                     st4["b_mult"][:, tb, :])
                        proj_fm(ctx, wd["w"], t["X"], evac_fm,
                                mcs=[m0], st=st4, wd=wd, m0=m0)

        def e_dot(tensor, g, bi):
            ins = None
            for tc in range(NTC):
                ins = tensor.matmul(ctx.bank(bi)[:, 0:P],
                                    qk_tm[:, tc, ts128(g)],
                                    qk_tm[:, tc, ts128(g)],
                                    start=(tc == 0), stop=(tc == NTC - 1))
            return ins

        def e_dotev(vector, g, bi):
            return vector.tensor_copy(dot_sb[:, g, :], ctx.bank(bi)[:, 0:P])

        pipe(ctx, list(range(KC)), e_dot, e_dotev, banks=(0, 1))

        tg = stc.n + 16

        def f_dout(sync, tg=tg):
            sync.dma_start(out=ctx.io["ag_cd_in"],
                           in_=dot_sb[:, :, :].rearrange("p g c -> p (g c)")
                           ).then_inc(stc.h, 16)
            sync.wait_ge(stc.h, tg)
        stc.n = tg
        blk(ctx, sync=f_dout)

        allgather(ctx, ctx.io["ag_cd_in"], ctx.io["ag_cd_out"])

        dotp = ctx.sbuf("dotp", [P, KC, P], F32, eB)
        tg = ldg.n + 16

        def f_gdot(gpsimd, tg=tg):
            gpsimd.indirect_dma_start(
                dotp[:, :, :].rearrange("p g c -> p (g c)"), None,
                ctx.io["ag_cd_out"],
                IndirectOffsetOnAxis(t["c_idx_cd"][:, 0:1], 0)).then_inc(ldg.h, 16)
            gpsimd.wait_ge(ldg.h, tg)
        ldg.n = tg
        blk(ctx, gpsimd=f_gdot)

        def f_dadd(vector):
            vector.tensor_add(dot_sb[:, :, :], dot_sb[:, :, :], dotp[:, :, :])
        blk(ctx, vector=f_dadd)

        r4 = lambda i: rows4[0:1, i, :]

        def f_tau(vector):
            edge(ctx, vector, vector.tensor_scalar(r4(0), t["c_L1_tau_c"][0:1, :],
                                                   1.0 / 64.0, None, ALU.mult), "edv")
        blk(ctx, vector=f_tau)

        def f_taub(tensor):
            tensor.matmul(psA[:, 0, 0:8], t["ones_row"][0:1, 0:P], r4(0),
                          start=True, stop=True)
        blk(ctx, tensor=f_taub)

        def f_taubv(vector):
            vector.tensor_copy(sgc[:, :], psA[:, 0, 0:8])
        blk(ctx, vector=f_taubv)

        cd_alias = qk_tm[:, :, :].rearrange("p a b -> p (a b)").rearrange(
            "p (kc tt) -> p kc tt", tt=T)

        for g in range(KC):
            def fv1(vector, g=g):
                i1 = vector.tensor_mul(t4[:, :], dot_sb[:, g, :], t["eye_f"][:, :])
                edge(ctx, vector, i1, "edv")
                edge(ctx, vector, vector.reduce_sum(q2c[:, 0:1], t4[:, :],
                                                    axis=AX), "edv")
            blk(ctx, vector=fv1)

            def ft1(tensor):
                tensor.transpose(psA[0:1, 1, 0:P], q2c[:, 0:1], t["eye_f"][:, :])
            blk(ctx, tensor=ft1)

            def fv2(vector):
                vector.tensor_copy(q2rst[0:1, :], psA[0:1, 1, 0:P])
            blk(ctx, vector=fv2)

            def ft2(tensor):
                tensor.matmul(psA[:, 2, 0:P], t["ones_row"][0:1, 0:P],
                              q2rst[0:1, :], start=True, stop=True)
            blk(ctx, tensor=ft2)

            def fv3(vector, g=g):
                i1 = vector.tensor_scalar(lt[:, :], dot_sb[:, g, :], 2.0,
                                          q2c[:, 0:1], ALU.mult, ALU.subtract)
                edge(ctx, vector, i1, "edv")
                i2 = vector.tensor_sub(lt[:, :], lt[:, :], psA[:, 2, 0:P])
                edge(ctx, vector, i2, "edv")
                i3 = vector.tensor_scalar(lt[:, :], lt[:, :], sgc[:, g:g + 1],
                                          None, ALU.mult)
                edge(ctx, vector, i3, "edv")
                edge(ctx, vector, vector.reduce_max(mx4[:, 0:1], lt[:, :],
                                                    axis=AX), "edv")
                edge(ctx, vector, vector.tensor_scalar(mx4[:, 1:2], mx4[:, 0:1],
                                                       -1.0, None, ALU.mult), "edv")
            blk(ctx, vector=fv3)

            def fa1(scalar):
                scalar.activation(sm4[:, :], lt[:, :], AF.Exp,
                                  bias=mx4[:, 1:2])
            blk(ctx, scalar=fa1)

            def fv4(vector):
                edge(ctx, vector, vector.reduce_sum(mx4[:, 0:1], sm4[:, :],
                                                    axis=AX), "edv")
                edge(ctx, vector, vector.reciprocal(mx4[:, 0:1], mx4[:, 0:1]), "edv")
                edge(ctx, vector, vector.tensor_scalar(sm4[:, :], sm4[:, :],
                                                       mx4[:, 0:1], None,
                                                       ALU.mult), "edv")
            blk(ctx, vector=fv4)

            def ft3(tensor):
                tensor.transpose(ctx.bank(0)[:, 0:P], sm4[:, :], t["eye_f"][:, :])
            blk(ctx, tensor=ft3)

            def fv5(vector):
                vector.tensor_copy(attT[:, :], ctx.bank(0)[:, 0:P])
            blk(ctx, vector=fv5)

            def ft4(tensor, g=g):
                for tb in range(NTB):
                    tensor.matmul(psB[:, tb, :], attT[:, :],
                                  v_c[:, g, ts512(tb)], start=True, stop=True)
            blk(ctx, tensor=ft4)

            def fv6(vector, g=g):
                for tb in range(NTB):
                    vector.tensor_copy(cd_alias[:, g, ts512(tb)], psB[:, tb, :])
            blk(ctx, vector=fv6)

        eB.close()
        tmp4 = ctx.sbuf("tmp4", [P, 512], F32R, es)
        with ExitStack() as ew:
            wo = load_w(ctx, ctx.io["L1_Wo_c"], (0, D), BF16, ew, name="wo4")

            def evac_wo(vector, ps, mc, tb):
                i1 = vector.tensor_scalar(tmp4[:, :], ps,
                                          t["c_L1_g2"][:, mc:mc + 1], None, ALU.mult)
                edge(ctx, vector, i1, "edv")
                return vector.tensor_add(t["X"][:, mc, ts512(tb)],
                                         t["X"][:, mc, ts512(tb)], tmp4[:, :])

            proj_fm(ctx, wo["w"], cd_alias, evac_wo)

        xs = ctx.sbuf("xs4", [P, 2, T], F32R, es)
        ldt = ctx.sem["ldt"]
        ldt0 = ldt.n

        ve4 = ctx.sem["aux"]
        ve40 = ve4.n

        def f_back_d2(sync):
            for kc in range(KC):
                if kc >= 2:
                    sync.wait_ge(ve4.h, ve40 + kc - 1)
                sync.dma_start(
                    out=xs[:, kc % 2, :],
                    in_=ctx.io["xspill"].rearrange("(kc p) t -> p kc t", p=P)[:, kc, :]
                ).then_inc(ldt.h, 16)

        def f_back_v(vector):
            for kc in range(KC):
                vector.wait_ge(ldt.h, ldt0 + 16 * (kc + 1))
                vector.tensor_add(t["X"][:, kc, :], t["X"][:, kc, :],
                                  xs[:, kc % 2, :]).then_inc(ve4.h, 1)

        ldt.n += 16 * KC
        ve4.n += KC
        blk(ctx, sync=f_back_d2, vector=f_back_v)



def phase5(ctx):
    nc, t = ctx.nc, ctx.t
    pe, ve, ac, ldg = ctx.sem["pe"], ctx.sem["ve"], ctx.sem["ac"], ctx.sem["ldg"]
    stc, cc = ctx.sem["st"], ctx.sem["cc"]
    psA, psB = t["psA"], t["psB"]
    CAP = 4096.0
    with ExitStack() as es:
        xn = ctx.sbuf("xn5", [P, KC, T], BF16, es)
        wrow = ctx.sbuf("wrow5", [8, NTC, P], BF16, es)

        with ExitStack() as eA:
            st = stats(ctx, t["X"], "ln", eA, bcast=("mult", "mu"))
            t1 = ctx.sbuf("p5t1", [P, 512], F32R, eA)

            def f_xn(vector):
                for kc in range(KC):
                    for tb in range(NTB):
                        i1 = vector.tensor_sub(t1[:, :], t["X"][:, kc, ts512(tb)],
                                               st["b_mu"][:, tb, :])
                        edge(ctx, vector, i1, "edv")
                        edge(ctx, vector,
                             vector.tensor_mul(xn[:, kc, ts512(tb)], t1[:, :],
                                               st["b_mult"][:, tb, :]), "edv")
            blk(ctx, vector=f_xn)

            logits = ctx.sbuf("logits5", [P, NTC, 8], F32, eA)
            probs = ctx.sbuf("probs5", [P, NTC, 8], F32, eA)
            mx = ctx.sbuf("mx5", [P, NTC, 1], F32, eA)
            g1 = ctx.sbuf("g1_5", [P, NTC, 1], F32, eA)
            g2 = ctx.sbuf("g2_5", [P, NTC, 1], F32, eA)
            oh1 = ctx.sbuf("oh1_5", [P, NTC, 8], F32R, eA)
            oh2 = ctx.sbuf("oh2_5", [P, NTC, 8], F32R, eA)
            rank = ctx.sbuf("rank5", [P, NTC, 2, 8], F32, eA)
            G = ctx.sbuf("G5", [P, NTC, 8], F32, eA)
            offs = ctx.sbuf("offs5", [1, 2, 16], F32, eA)
            offb = ctx.sbuf("offb5", [P, 16], F32, eA)
            cnt_r = ctx.sbuf("cntr5", [8, 16], F32R, eA)

            with ExitStack() as ew:
                wd = load_w(ctx, ctx.io["L1_wg"], (0, 8), F32R, ew,
                            name="wg5", want_sum=True)

                def evac_rt(vector, ps, tc, j):
                    return vector.tensor_scalar(logits[:, tc, :], ps[:, 0:8],
                                                st["rstd_col"][:, tc:tc + 1],
                                                None, ALU.mult)
                proj_tm(ctx, wd, st, t["X"], 8, evac_rt)

            def f_pexp(scalar):
                scalar.activation(probs[:, :, :], logits[:, :, :], AF.Exp)
            blk(ctx, scalar=f_pexp)

            def f_top2(vector):
                edge(ctx, vector, vector.reduce_sum(mx[:, :, :], probs[:, :, :],
                                                    axis=AX), "edv")
                edge(ctx, vector, vector.reciprocal(mx[:, :, :], mx[:, :, :]), "edv")
                edge(ctx, vector, vector.tensor_mul(
                    probs[:, :, :], probs[:, :, :],
                    mx[:, :, :].broadcast_to([P, NTC, 8])), "edv")
                edge(ctx, vector, vector.reduce_max(g1[:, :, :], probs[:, :, :],
                                                    axis=AX), "edv")
                edge(ctx, vector, vector.tensor_tensor(
                    oh1[:, :, :], probs[:, :, :],
                    g1[:, :, :].broadcast_to([P, NTC, 8]), ALU.is_equal), "edv")
                edge(ctx, vector, vector.tensor_scalar(
                    G[:, :, :], oh1[:, :, :], -1.0, 1.0, ALU.mult, ALU.add), "edv")
                edge(ctx, vector, vector.tensor_mul(
                    probs[:, :, :], probs[:, :, :], G[:, :, :]), "edv")
                edge(ctx, vector, vector.reduce_max(g2[:, :, :], probs[:, :, :],
                                                    axis=AX), "edv")
                edge(ctx, vector, vector.tensor_tensor(
                    oh2[:, :, :], probs[:, :, :],
                    g2[:, :, :].broadcast_to([P, NTC, 8]), ALU.is_equal), "edv")
                edge(ctx, vector, vector.tensor_add(mx[:, :, :], g1[:, :, :],
                                                    g2[:, :, :]), "edv")
                edge(ctx, vector, vector.tensor_scalar(mx[:, :, :], mx[:, :, :],
                                                       1e-9, None, ALU.max), "edv")
                edge(ctx, vector, vector.reciprocal(mx[:, :, :], mx[:, :, :]), "edv")
                edge(ctx, vector, vector.tensor_mul(g1[:, :, :], g1[:, :, :],
                                                    mx[:, :, :]), "edv")
                edge(ctx, vector, vector.tensor_mul(g2[:, :, :], g2[:, :, :],
                                                    mx[:, :, :]), "edv")
            blk(ctx, vector=f_top2)

            def e_rank(tensor, job, bi):
                tc, k = job
                oh = oh1 if k == 0 else oh2
                ps = ctx.bank(bi)[:, 0:8]
                ins = None
                for tcp in range(tc + 1):
                    lhsT = (t["slt_r"][:, :] if tcp == tc else t["ones_sq"][:, :])
                    ins = tensor.matmul(ps, lhsT, oh[:, tcp, :],
                                        start=(tcp == 0), stop=(tcp == tc))
                return ins

            def e_rankev(vector, job, bi):
                tc, k = job
                return vector.tensor_copy(rank[:, tc, k, :], ctx.bank(bi)[:, 0:8])

            pipe(ctx, [(tc, k) for tc in range(NTC) for k in range(2)],
                 e_rank, e_rankev, banks=(0, 1))

            def f_cnt(tensor):
                for k in range(2):
                    oh = oh1 if k == 0 else oh2
                    for tc in range(NTC):
                        tensor.matmul(psA[0:1, 2 + k, 0:8],
                                      t["ones_col"][:, 0:1], oh[:, tc, :],
                                      start=(tc == 0), stop=(tc == NTC - 1))
            blk(ctx, tensor=f_cnt)

            def f_cntev(vector):
                vector.tensor_copy(offs[0:1, 0, 0:8], psA[0:1, 2, 0:8])
                vector.tensor_copy(offs[0:1, 0, 8:16], psA[0:1, 3, 0:8])
            blk(ctx, vector=f_cntev)

            tg = stc.n + 16

            def f_cout(sync, tg=tg):
                sync.dma_start(out=ctx.io["ag_cnt_in"],
                               in_=offs[0:1, 0, :]).then_inc(stc.h, 16)
                sync.wait_ge(stc.h, tg)
            stc.n = tg
            blk(ctx, sync=f_cout)

            allgather(ctx, ctx.io["ag_cnt_in"], ctx.io["ag_cnt_out"])

            tg = ldg.n + 16

            def f_cin(gpsimd, tg=tg):
                gpsimd.dma_start(out=cnt_r[:, :],
                                 in_=ctx.io["ag_cnt_out"]).then_inc(ldg.h, 16)
                gpsimd.wait_ge(ldg.h, tg)
            ldg.n = tg
            blk(ctx, gpsimd=f_cin)

            def f_off(tensor):
                tensor.matmul(psA[0:1, 2, 0:16], t["c_pmask"][:, 0:1],
                              cnt_r[:, :], start=True, stop=True)
                tensor.matmul(psA[0:1, 3, 0:8], t["ones_col"][0:8, 0:1],
                              cnt_r[:, 0:8], start=True, stop=True)
            blk(ctx, tensor=f_off)

            def f_offv(vector):
                i1 = vector.tensor_copy(offs[0:1, 1, :], psA[0:1, 2, 0:16])
                edge(ctx, vector, i1, "edv")
                edge(ctx, vector, vector.tensor_add(
                    offs[0:1, 1, 8:16], offs[0:1, 1, 8:16], psA[0:1, 3, 0:8]), "edv")
            blk(ctx, vector=f_offv)

            def f_offb_mm(tensor):
                tensor.matmul(psA[:, 2, 0:16], t["ones_row"][0:1, 0:P],
                              offs[0:1, 1, :], start=True, stop=True)
            blk(ctx, tensor=f_offb_mm)

            def f_offb_ev(vector):
                vector.tensor_copy(offb[:, :], psA[:, 2, 0:16])
            blk(ctx, vector=f_offb_ev)

            pos = probs

            def f_G(vector):
                for k in range(2):
                    oh = oh1 if k == 0 else oh2
                    gk = g1 if k == 0 else g2
                    i1 = vector.tensor_add(
                        pos[:, :, :], rank[:, :, k, :],
                        offb[:, None, k * 8:(k + 1) * 8].broadcast_to([P, NTC, 8]))
                    edge(ctx, vector, i1, "edv")
                    i2 = vector.tensor_scalar(pos[:, :, :], pos[:, :, :], CAP,
                                              None, ALU.is_lt)
                    edge(ctx, vector, i2, "edv")
                    i3 = vector.tensor_mul(pos[:, :, :], pos[:, :, :], oh[:, :, :])
                    edge(ctx, vector, i3, "edv")
                    i4 = vector.tensor_mul(pos[:, :, :], pos[:, :, :],
                                           gk[:, :, :].broadcast_to([P, NTC, 8]))
                    edge(ctx, vector, i4, "edv")
                    if k == 0:
                        i5 = vector.tensor_copy(G[:, :, :], pos[:, :, :])
                    else:
                        i5 = vector.tensor_add(G[:, :, :], G[:, :, :], pos[:, :, :])
                    edge(ctx, vector, i5, "edv")
            blk(ctx, vector=f_G)

            def e_wt(tensor, tc, bi):
                return tensor.transpose(ctx.bank(bi)[0:8, 0:128],
                                        G[:, tc, :], t["eye_f"][:, :])

            def e_wtev(vector, tc, bi):
                return vector.tensor_copy(wrow[:, tc, :], ctx.bank(bi)[0:8, 0:128])

            pipe(ctx, list(range(NTC)), e_wt, e_wtev, banks=(0, 1))
            if DEBUG == "G5":
                dump_dbg(ctx, G[:, :, :].rearrange("p tc e -> p (tc e)"), NTC * 8)
            if DEBUG == "logits5":
                dump_dbg(ctx, logits[:, :, :].rearrange("p tc e -> p (tc e)"), NTC * 8)
            if DEBUG == "rank5":
                dump_dbg(ctx, rank[:, :, 0, :].rearrange("p tc e -> p (tc e)"), NTC * 8)
            if DEBUG == "offs5":
                dump_dbg(ctx, offb[:, :], 16)

        if DEBUG == "xn5":
            dump_dbg(ctx, xn[:, 0, :], T)
        if DEBUG == "noexp5":
            return
        brow = ctx.sbuf("brow5", [1, T], BF16, es)
        Bcur = ctx.sbuf("Bcur5", [P, NTB, 512], BF16, es)
        tmp = ctx.sbuf("p5tmp", [P, 512], F32R, es)
        b1r = ctx.sbuf("b1r", [1, 4096], BF16, es)
        b2r = ctx.sbuf("b2r", [1, D], BF16, es)
        h = ctx.sbuf("hbuf5", [P, 32, 1024], BF16, es)
        wslab = ctx.sbuf("wslab5", [P, 64, P], BF16, es)
        w1sb = [wslab[:, 0:KC, :], wslab[:, 32:32 + KC, :]]
        w2sb = [wslab[:, 0:32, :], wslab[:, 32:64, :]]

        for e in range(8):
            tg = ldg.n + 48

            def f_brow(gpsimd, e=e, tg=tg):
                gpsimd.dma_start(out=brow[0:1, :],
                                 in_=wrow[e:e + 1, :, :].rearrange("o tc p -> o (tc p)")
                                 ).then_inc(ldg.h, 16)
                gpsimd.dma_start(out=b1r[0:1, :],
                                 in_=ctx.io["L1_b1"][e:e + 1, :]).then_inc(ldg.h, 16)
                gpsimd.dma_start(out=b2r[0:1, :],
                                 in_=ctx.io["L1_b2"][e:e + 1, :]).then_inc(ldg.h, 16)
                gpsimd.wait_ge(ldg.h, tg)
            ldg.n = tg
            blk(ctx, gpsimd=f_brow)

            def e_bmm(tensor, tb, bi):
                return tensor.matmul(ctx.bank(bi), t["c_ones_rowb"][0:1, 0:P],
                                     brow[0:1, ts512(tb)], start=True, stop=True)

            def e_bev(vector, tb, bi):
                return vector.tensor_copy(Bcur[:, tb, :], ctx.bank(bi))

            pipe(ctx, list(range(NTB)), e_bmm, e_bev, banks=(0, 1))

            for tb2 in range(2):
                def el_h(gpsimd, s, buf, e=e):
                    return gpsimd.dma_start(
                        out=w1sb[buf][:, :, :].rearrange("p kc m -> p (kc m)"),
                        in_=ctx.io["L1_w1H"][e, s].rearrange("p kc m -> p (kc m)"))

                def em_h(tensor, s, j, bi, buf, tb2=tb2):
                    for kc in range(KC):
                        tensor.matmul(
                            ctx.bank(bi), w1sb[buf][:, kc, :],
                            xn[:, kc, tb2 * 1024 + j * 512:tb2 * 1024 + (j + 1) * 512],
                            start=(kc == 0), stop=False)
                    return tensor.matmul(
                        ctx.bank(bi), b1r[0:1, s * 128:(s + 1) * 128],
                        t["c_ones_rowb"][0:1, 0:512], start=False, stop=True)

                def ev_h(scalar, s, j, bi):
                    return scalar.activation(h[:, s, ts512(j)], ctx.bank(bi), AF.Gelu)

                spipe(ctx, 32, 2, el_h, em_h, ev_h, ev_eng="scalar", banks=(0, 1))

                def el_y(gpsimd, s, buf, e=e):
                    return gpsimd.dma_start(
                        out=w2sb[buf][:, :, :].rearrange("p fc m -> p (fc m)"),
                        in_=ctx.io["L1_w2H"][e, s].rearrange("p fc m -> p (fc m)"),
                        max_dma_last_dim=2048)

                def em_y(tensor, s, j, bi, buf):
                    for fc in range(32):
                        tensor.matmul(ctx.bank(bi), w2sb[buf][:, fc, :],
                                      h[:, fc, ts512(j)],
                                      start=(fc == 0), stop=False)
                    return tensor.matmul(
                        ctx.bank(bi), b2r[0:1, s * 128:(s + 1) * 128],
                        t["c_ones_rowb"][0:1, 0:512], start=False, stop=True)

                def ev_y(vector, s, j, bi, tb2=tb2):
                    tb = tb2 * 2 + j
                    i1 = vector.tensor_mul(tmp[:, :], ctx.bank(bi), Bcur[:, tb, :])
                    edge(ctx, vector, i1, "edv")
                    return vector.tensor_add(t["X"][:, s, ts512(tb)],
                                             t["X"][:, s, ts512(tb)], tmp[:, :])

                spipe(ctx, 8, 2, el_y, em_y, ev_y, ev_eng="vector", banks=(2, 3))


PHASE_FNS = {"p1": phase1, "p2": phase2, "p3": phase3, "p4": phase4, "p5": phase5}

_BUILT = {}


def build_launch(phases):
    key = tuple(phases)
    if key in _BUILT:
        return _BUILT[key]
    nc = bass.Bass("TRN2", target_bir_lowering=False, debug=False)
    io = declare_io(nc)
    with ExitStack() as es:
        ctx = Ctx(nc, es, io)
        setup_ctx(ctx, phases)
        for ph in phases:
            PHASE_FNS[ph](ctx)
        finish(ctx)
    _BUILT[key] = nc
    return nc


def _host_consts():
    import ml_dtypes
    bf = ml_dtypes.bfloat16
    c = {}
    c["eye_b"] = np.eye(P, dtype=np.float32).astype(bf)
    c["eye_f"] = np.eye(P, dtype=np.float32)
    c["eye_r"] = np.eye(P, dtype=np.float32)
    c["slt_r"] = np.triu(np.ones((P, P), dtype=np.float32), 1)
    c["ones_sq"] = np.ones((P, P), dtype=np.float32)
    c["ones_col"] = np.ones((P, 1), dtype=np.float32)
    c["ones_row"] = np.ones((1, 512), dtype=np.float32)
    c["ones_rowb"] = np.ones((1, 512), dtype=np.float32).astype(bf)
    return c


def _core_consts(core):
    import ml_dtypes
    bf = ml_dtypes.bfloat16
    b, s = core // 2, core % 2
    partner = core ^ 1
    cc = {}
    inv_freq = (1.0 / 10000.0 ** (np.arange(0, HD, 2, dtype=np.float32) / HD))
    pos = (s * T + np.arange(T, dtype=np.float32))[:, None]
    th = pos * inv_freq[None, :]
    cos, sin = np.cos(th).astype(np.float32), np.sin(th).astype(np.float32)
    cc["ropeC"] = np.concatenate([cos, cos], axis=1).astype(bf)
    cc["ropeS"] = np.concatenate([-sin, sin], axis=1).astype(bf)
    tok = s * T + np.arange(T)
    is_local = (tok >= G).astype(np.float32)
    m16 = is_local.reshape(NTC, P).T
    cc["mloc16"] = m16.astype(bf)
    cc["mloc"] = np.broadcast_to(is_local[None, 0:512], (P, 512)).astype(bf).copy()
    is_glob = ((tok < G) & (s == 0)).astype(np.float32)
    cc["mglob"] = np.broadcast_to(is_glob[None, 0:512], (P, 512)).astype(bf).copy()
    cc["pmask"] = (np.arange(8) < core).astype(np.float32).reshape(8, 1)
    p = np.arange(P, dtype=np.uint32)
    cc["idx_kv"] = (partner * 65 + np.minimum(p, 64)).astype(np.uint32).reshape(P, 1)
    iq2 = np.zeros((P, 2), dtype=np.uint32)
    for si in range(2):
        iq2[:, si] = (2 * b + si) * P + p
    cc["idx_q2"] = iq2
    cc["idx_cd"] = (partner * P + p).astype(np.uint32).reshape(P, 1)
    iq = np.zeros((P, 2 * NHEAD), dtype=np.uint32)
    for h in range(NHEAD):
        for si in range(2):
            iq[0:64, 2 * h + si] = (2 * b + si) * 2048 + h * 64 + np.arange(64)
    cc["idx_qk"] = iq
    ir = np.zeros((P, NHEAD), dtype=np.uint32)
    for h in range(NHEAD):
        ir[0:64, h] = (2 * b + s) * 2048 + h * 64 + np.arange(64)
    cc["idx_rhs"] = ir
    iv = np.zeros((P, NHEAD), dtype=np.uint32)
    for c2 in range(8):
        for si in range(2):
            iv[:, c2 * 2 + si] = (2 * b + si) * 2048 + 1024 + c2 * 128 + p
    cc["idx_v"] = iv
    return cc


def _packw1(w):
    return np.ascontiguousarray(
        np.asarray(w, np.float32).reshape(8, 8, P, 32, P).transpose(0, 3, 2, 1, 4))


def _packw2(w):
    return np.ascontiguousarray(
        np.asarray(w, np.float32).reshape(8, 32, P, 8, P).transpose(0, 3, 2, 1, 4))


def _vec8(v):
    return np.ascontiguousarray(np.asarray(v, dtype=np.float32).reshape(KC, P).T)


def kernel(**inp):
    inp = {k: np.asarray(v) for k, v in inp.items()}
    x = inp["L0_attn_x"].astype(np.float32, copy=False)
    shared = {
        "L0_Wq": inp["L0_attn_Wq"], "L0_Wk": inp["L0_attn_Wk"],
        "L0_Wv": inp["L0_attn_Wv"], "L0_Wout": inp["L0_attn_Wout"],
        "L0_g": _vec8(inp["L0_ffn_gamma"]), "L0_b": _vec8(inp["L0_ffn_beta"]),
        "L0_router": np.ascontiguousarray(inp["L0_ffn_router_w"].T),
        "L0_W1H": _packw1(inp["L0_ffn_W1"]), "L0_W2H": _packw2(inp["L0_ffn_W2"]),
        "L1_Wqk_t": inp["L1_attn_Wqk_t"], "L1_Wv_t": inp["L1_attn_Wv_t"],
        "L1_Wo_t": inp["L1_attn_Wo_t"], "L1_Wqk_c": inp["L1_attn_Wqk_c"],
        "L1_Wv_c": inp["L1_attn_Wv_c"], "L1_Wo_c": inp["L1_attn_Wo_c"],
        "L1_tau_t": np.ascontiguousarray(inp["L1_attn_tau_t"].reshape(1, 16)),
        "L1_tau_c": np.ascontiguousarray(inp["L1_attn_tau_c"].reshape(1, 8)),
        "L1_ln1_w": _vec8(inp["L1_attn_ln1_w"]), "L1_ln1_b": _vec8(inp["L1_attn_ln1_b"]),
        "L1_ln2_w": _vec8(inp["L1_attn_ln2_w"]), "L1_ln2_b": _vec8(inp["L1_attn_ln2_b"]),
        "L1_g1": _vec8(inp["L1_attn_gamma1"]), "L1_g2": _vec8(inp["L1_attn_gamma2"]),
        "L1_wg": inp["L1_ffn_wg"], "L1_w1H": _packw1(inp["L1_ffn_w1"]),
        "L1_b1": inp["L1_ffn_b1"], "L1_w2H": _packw2(inp["L1_ffn_w2"]),
        "L1_b2": inp["L1_ffn_b2"],
    }
    shared = {k: np.ascontiguousarray(v.astype(np.float32, copy=False))
              for k, v in shared.items()}
    shared.update(_host_consts())

    in_maps = []
    for c in range(8):
        b, s = c // 2, c % 2
        m = dict(shared)
        m.update(_core_consts(c))
        m["xin"] = np.ascontiguousarray(x[b, s * T:(s + 1) * T, :].T)
        in_maps.append(m)

    for phases in LAUNCHES:
        nc = build_launch(phases)
        res = run_bass_kernel_spmd(nc, in_maps, core_ids=list(range(8)))
        for c in range(8):
            in_maps[c]["xin"] = np.ascontiguousarray(res.results[c]["xout"])

    out = np.empty((4, 4096, D), dtype=np.float32)
    for c in range(8):
        b, s = c // 2, c % 2
        out[b, s * T:(s + 1) * T, :] = in_maps[c]["xin"].T
    return out


# version 13
# speedup vs baseline: 1.1726x; 1.0112x over previous
import numpy as np
from contextlib import ExitStack

import concourse.bass as bass
import concourse.mybir as mybir
from concourse.bass import IndirectOffsetOnAxis
from concourse.bass_utils import run_bass_kernel_spmd
from concourse.bass_isa import ReduceOp

F32 = mybir.dt.float32
F32R = mybir.dt.float32r
BF16 = mybir.dt.bfloat16
U32 = mybir.dt.uint32
AF = mybir.ActivationFunctionType
ALU = mybir.AluOpType
AX = mybir.AxisListType.X

P = 128
T = 2048
KC = 8
NTB = 4
NTC = 16
D = 1024
NHEAD = 16
HD = 64
G = 409
EPS = 1e-5

LAUNCHES = [["p1", "p2", "p3", "p4", "p5"]]
DEBUG = None

_GS = [0]


def gs(p):
    _GS[0] += 1
    return f"{p}_{_GS[0]}"


def ts512(i):
    return slice(i * 512, (i + 1) * 512)


def ts128(i):
    return slice(i * 128, (i + 1) * 128)


class Sem:
    def __init__(self, h):
        self.h = h
        self.n = 0


class Ctx:
    def __init__(self, nc, es, io):
        self.nc = nc
        self.es = es
        self.io = io
        self.sem = {}
        for name in ["ldw", "ldg", "ldt", "st", "pe", "ve", "ac", "gp", "cc",
                     "edv", "eac", "egp", "aux"]:
            self.sem[name] = Sem(es.enter_context(nc.semaphore(gs("sm_" + name))))
        self.t = {}

    def sbuf(self, name, shape, dtype, es=None):
        return (es or self.es).enter_context(
            self.nc.sbuf_tensor(gs(name), list(shape), dtype))

    def psum(self, name, shape, dtype=F32, es=None):
        return (es or self.es).enter_context(
            self.nc.psum_tensor(gs(name), list(shape), dtype))

    def bank(self, i):
        return (self.t["psA"] if i < 4 else self.t["psB"])[:, i % 4, :]

    def bankb(self, i):
        return self.bank(i).bitcast(BF16)


def blk(ctx, **fns):
    with ctx.nc.Block() as b:
        for k, f in fns.items():
            getattr(b, k)(f)


def edge(ctx, eng, instr, which):
    s = ctx.sem[which]
    s.n += 1
    instr.then_inc(s.h, 1)
    eng.wait_ge(s.h, s.n)


def pipe(ctx, jobs, emit_mm, emit_ev, banks=(0, 1), ev_eng="vector"):
    pe, ve = ctx.sem["pe"], ctx.sem["ve"]
    pe0, ve0 = pe.n, ve.n
    nbuf = len(banks)

    def f_mm(tensor):
        for i, job in enumerate(jobs):
            if i >= nbuf:
                tensor.wait_ge(ve.h, ve0 + i - nbuf + 1)
            emit_mm(tensor, job, banks[i % nbuf]).then_inc(pe.h, 1)

    def f_ev(eng):
        for i, job in enumerate(jobs):
            eng.wait_ge(pe.h, pe0 + i + 1)
            emit_ev(eng, job, banks[i % nbuf]).then_inc(ve.h, 1)

    pe.n += len(jobs)
    ve.n += len(jobs)
    blk(ctx, **{"tensor": f_mm, ev_eng: f_ev})


def stats(ctx, src, flavor, es, bcast=(), nsq=2):
    nc = ctx.nc
    rows = ctx.sbuf("st_rows", [1, 6, T], F32, es)
    rc = ctx.sbuf("st_rcol", [P, NTC], F32, es)
    bts = {k: ctx.sbuf("st_b_" + k, [P, NTB, 512], F32R, es) for k in bcast}
    es_t = ExitStack()
    sq = [ctx.sbuf("st_sq", [P, T], F32R, es_t) for _ in range(nsq)]
    sq = [sq[i % nsq] for i in range(2)]
    ac, pe = ctx.sem["ac"], ctx.sem["pe"]
    ac0, pe0 = ac.n, pe.n
    psA, psB = ctx.t["psA"], ctx.t["psB"]
    ones_col = ctx.t["ones_col"]

    def f_scalar(scalar):
        for kc in range(KC):
            if kc >= nsq:
                scalar.wait_ge(pe.h, pe0 + kc - nsq + 1)
            scalar.activation(sq[kc % 2][:, :], src[:, kc, :], AF.Square).then_inc(ac.h, 1)

    def f_tensor(tensor):
        for kc in range(KC):
            for tb in range(NTB):
                tensor.matmul(psA[0:1, tb, :], ones_col[:, 0:1], src[:, kc, ts512(tb)],
                              start=(kc == 0), stop=(kc == KC - 1))
            tensor.wait_ge(ac.h, ac0 + kc + 1)
            ins = None
            for tb in range(NTB):
                ins = tensor.matmul(psB[0:1, tb, :], ones_col[:, 0:1],
                                    sq[kc % 2][:, ts512(tb)],
                                    start=(kc == 0), stop=(kc == KC - 1))
            ins.then_inc(pe.h, 1)

    ac.n += KC
    pe.n += KC
    blk(ctx, scalar=f_scalar, tensor=f_tensor)

    inv_d = 1.0 / D
    r = lambda i: rows[0:1, i, :]

    def f_rowmath(vector):
        for tb in range(NTB):
            vector.tensor_copy(rows[0:1, 0, ts512(tb)], psA[0:1, tb, :])
            ins = vector.tensor_copy(rows[0:1, 1, ts512(tb)], psB[0:1, tb, :])
            edge(ctx, vector, ins, "edv")
        edge(ctx, vector, vector.tensor_scalar(r(2), r(0), inv_d, None, ALU.mult), "edv")
        edge(ctx, vector, vector.tensor_scalar(r(3), r(2), -1.0, None, ALU.mult), "edv")
        edge(ctx, vector, vector.tensor_mul(r(5), r(2), r(2)), "edv")
        edge(ctx, vector, vector.tensor_scalar(r(4), r(1), inv_d, None, ALU.mult), "edv")
        edge(ctx, vector, vector.tensor_sub(r(4), r(4), r(5)), "edv")
        edge(ctx, vector, vector.tensor_scalar(r(5), r(4), EPS, None, ALU.add), "edv")
        edge(ctx, vector, vector.reciprocal(r(0), r(5)), "edv")

    blk(ctx, vector=f_rowmath)

    if flavor == "ln":
        def f_sqrt(scalar):
            edge(ctx, scalar, scalar.activation(r(1), r(0), AF.Sqrt), "eac")
            edge(ctx, scalar, scalar.activation(r(5), r(5), AF.Sqrt), "eac")
        blk(ctx, scalar=f_sqrt)
        mult_slot, inv_slot, plain_slot = 1, 5, 1
    else:
        def f_v2(vector):
            edge(ctx, vector, vector.tensor_mul(r(5), r(4), r(0)), "edv")
            edge(ctx, vector, vector.tensor_scalar(r(5), r(5), EPS, None, ALU.add), "edv")
        blk(ctx, vector=f_v2)

        def f_sq2(scalar):
            edge(ctx, scalar, scalar.activation(r(1), r(0), AF.Sqrt), "eac")
            edge(ctx, scalar, scalar.activation(r(5), r(5), AF.Sqrt), "eac")
        blk(ctx, scalar=f_sq2)

        def f_rr(vector):
            edge(ctx, vector, vector.reciprocal(r(0), r(5)), "edv")
            edge(ctx, vector, vector.tensor_mul(r(4), r(1), r(0)), "edv")
            edge(ctx, vector, vector.reciprocal(r(5), r(4)), "edv")
        blk(ctx, vector=f_rr)
        mult_slot, inv_slot, plain_slot = 4, 5, 1

    out = {"_rows": rows,
           "mu": r(2), "negmu": r(3),
           "rstd": r(mult_slot), "sig": r(inv_slot), "plain": r(plain_slot)}

    def f_tr(tensor):
        for tc in range(NTC):
            tensor.transpose(psA[:, 0, tc:tc + 1], rows[0:1, mult_slot, ts128(tc)],
                             ctx.t["eye_f"][0:1, 0:1])

    blk(ctx, tensor=f_tr)

    def f_trc(vector):
        vector.tensor_copy(rc[:, :], psA[:, 0, 0:NTC])

    blk(ctx, vector=f_trc)
    out["rstd_col"] = rc

    slot_of = {"mult": mult_slot, "mu": 2, "plain": plain_slot}
    for bk in bcast:
        bc = bts[bk]
        out["b_" + bk] = bc
        sl = slot_of[bk]

        def e_mm(tensor, tb, bi, sl=sl):
            return tensor.matmul(ctx.bank(bi), ctx.t["ones_row"][0:1, 0:P],
                                 rows[0:1, sl, ts512(tb)], start=True, stop=True)

        def e_ev(vector, tb, bi, bc=bc):
            return vector.tensor_copy(bc[:, tb, :], ctx.bank(bi))

        pipe(ctx, list(range(NTB)), e_mm, e_ev, banks=(0, 1))
    es_t.close()
    return out


def load_w(ctx, wap, mslice, dtype, es, name="w", want_sum=False, wcol=None,
           bcol=None):
    nc = ctx.nc
    m0, m1 = mslice
    M = m1 - m0
    wsb = ctx.sbuf(name, [P, KC, M], dtype, es)
    ldg = ctx.sem["ldg"]

    def f_load(gpsimd):
        src = wap.rearrange("(kc p) m -> p kc m", p=P)[:, :, m0:m1]
        gpsimd.dma_start(out=wsb[:, :, :], in_=src).then_inc(ldg.h, 16)

    ldg.n += 16
    blk(ctx, gpsimd=f_load)
    wait_n = ldg.n

    out = {"w": wsb}
    nmm = (M + 511) // 512
    need_rows = want_sum or (bcol is not None)
    rows = ctx.sbuf(name + "_r", [1, 3, max(M, 16)], F32, es) if need_rows else None

    if bcol is not None:
        bcol_r = ctx.sbuf(name + "_bc", [P, KC], F32R, es)

        def f_bcr(vector):
            vector.tensor_copy(bcol_r[:, :], bcol[:, :])
        blk(ctx, vector=f_bcr)

        def f_qb_mm(tensor):
            tensor.wait_ge(ldg.h, wait_n)
            for j in range(nmm):
                sl = slice(j * 512, min(M, (j + 1) * 512))
                w = sl.stop - sl.start
                for kc in range(KC):
                    tensor.matmul(ctx.bank(j)[0:1, 0:w], bcol_r[:, kc:kc + 1],
                                  wsb[:, kc, sl], start=(kc == 0), stop=(kc == KC - 1))
        blk(ctx, tensor=f_qb_mm)

        def f_qb_ev(vector):
            for j in range(nmm):
                sl = slice(j * 512, min(M, (j + 1) * 512))
                vector.tensor_copy(rows[0:1, 2, sl], ctx.bank(j)[0:1, 0:sl.stop - sl.start])
        blk(ctx, vector=f_qb_ev)
        out["qb"] = rows[0:1, 2, 0:M]

    if wcol is not None:
        def f_scale(vector):
            vector.wait_ge(ldg.h, wait_n)
            for kc in range(KC):
                vector.tensor_scalar(wsb[:, kc, :], wsb[:, kc, :], wcol[:, kc:kc + 1],
                                     None, ALU.mult)
        blk(ctx, vector=f_scale)

    if want_sum:
        def f_sum(tensor):
            tensor.wait_ge(ldg.h, wait_n)
            for j in range(nmm):
                sl = slice(j * 512, min(M, (j + 1) * 512))
                w = sl.stop - sl.start
                for kc in range(KC):
                    tensor.matmul(ctx.bank(j)[0:1, 0:w], ctx.t["ones_col"][:, 0:1],
                                  wsb[:, kc, sl], start=(kc == 0), stop=(kc == KC - 1))
        blk(ctx, tensor=f_sum)

        def f_ev(vector):
            for j in range(nmm):
                sl = slice(j * 512, min(M, (j + 1) * 512))
                w = sl.stop - sl.start
                vector.tensor_copy(rows[0:1, 0, sl], ctx.bank(j)[0:1, 0:w])
                vector.tensor_scalar(rows[0:1, 1, sl], ctx.bank(j)[0:1, 0:w], -1.0,
                                     None, ALU.mult)
        blk(ctx, vector=f_ev)
        out["wsum"] = rows[0:1, 0, 0:M]
        out["negwsum"] = rows[0:1, 1, 0:M]

    if not need_rows and wcol is None:
        def f_w(vector):
            vector.wait_ge(ldg.h, wait_n)
        blk(ctx, vector=f_w)
    return out


def proj_tm(ctx, wd, st, src, M, evac):
    nj = (M + 511) // 512
    jobs = [(tc, j) for tc in range(NTC) for j in range(nj)]

    def e_mm(tensor, job, bi):
        tc, j = job
        sl = slice(j * 512, min(M, (j + 1) * 512))
        w = sl.stop - sl.start
        ps = ctx.bank(bi)[:, 0:w]
        for kc in range(KC):
            ins = tensor.matmul(ps, src[:, kc, ts128(tc)], wd["w"][:, kc, sl],
                                start=(kc == 0), stop=False)
        ins = tensor.matmul(ps, st["negmu"][0:1, ts128(tc)], wd["wsum"][0:1, sl],
                            start=False, stop=("qb" not in wd))
        if "qb" in wd:
            ins = tensor.matmul(ps, st["sig"][0:1, ts128(tc)], wd["qb"][0:1, sl],
                                start=False, stop=True)
        return ins

    def e_ev(vector, job, bi):
        tc, j = job
        sl = slice(j * 512, min(M, (j + 1) * 512))
        return evac(vector, ctx.bank(bi)[:, 0:sl.stop - sl.start], tc, j)

    pipe(ctx, jobs, e_mm, e_ev, banks=(0, 1, 2, 3))


def proj_fm(ctx, wsb, src, evac, mcs=range(KC), st=None, wd=None, m0=0):
    jobs = [(mc, tb) for mc in mcs for tb in range(NTB)]

    def e_mm(tensor, job, bi):
        mc, tb = job
        ml = mc - m0
        ps = ctx.bank(bi)
        for kc in range(KC):
            ins = tensor.matmul(ps, wsb[:, kc, ts128(ml)], src[:, kc, ts512(tb)],
                                start=(kc == 0),
                                stop=(kc == KC - 1 and st is None))
        if st is not None:
            last = ("negwsum" not in wd, "qb" not in wd)
            if "negwsum" in wd:
                ins = tensor.matmul(ps, wd["negwsum"][0:1, ts128(ml)],
                                    st["mu"][0:1, ts512(tb)], start=False,
                                    stop=("qb" not in wd))
            if "qb" in wd:
                ins = tensor.matmul(ps, wd["qb"][0:1, ts128(ml)],
                                    st["sig"][0:1, ts512(tb)], start=False, stop=True)
        return ins

    def e_ev(vector, job, bi):
        mc, tb = job
        return evac(vector, ctx.bank(bi), mc, tb)

    pipe(ctx, jobs, e_mm, e_ev, banks=(0, 1, 2, 3))


def declare_io(nc):
    t = {}

    def din(name, shape, dtype):
        t[name] = nc.dram_tensor(name, list(shape), dtype, kind="ExternalInput").ap()

    din("xin", [D, T], F32R)
    for nm in ["L0_Wq", "L0_Wk", "L0_Wv", "L0_Wout",
               "L1_Wqk_t", "L1_Wv_t", "L1_Wo_t", "L1_Wqk_c", "L1_Wv_c", "L1_Wo_c"]:
        din(nm, [D, D], F32)
    din("L0_g", [P, KC], F32)
    din("L0_b", [P, KC], F32)
    din("L0_router", [D, 8], F32)
    din("L0_W1H", [8, 32, P, KC, P], F32)
    din("L0_W2H", [8, KC, P, 32, P], F32)
    din("L1_tau_t", [1, 16], F32)
    din("L1_tau_c", [1, 8], F32)
    for nm in ["L1_ln1_w", "L1_ln1_b", "L1_ln2_w", "L1_ln2_b", "L1_g1", "L1_g2"]:
        din(nm, [P, KC], F32)
    din("L1_wg", [D, 8], F32)
    din("L1_w1H", [8, 32, P, KC, P], F32)
    din("L1_b1", [8, 4096], F32)
    din("L1_w2H", [8, KC, P, 32, P], F32)
    din("L1_b2", [8, D], F32)
    din("eye_b", [P, P], BF16)
    din("eye_f", [P, P], F32)
    din("eye_r", [P, P], F32R)
    din("slt_r", [P, P], F32R)
    din("ones_sq", [P, P], F32R)
    din("ones_col", [P, 1], F32R)
    din("ones_row", [1, 512], F32)
    din("ones_rowb", [1, 512], BF16)
    din("ropeC", [T, HD], BF16)
    din("ropeS", [T, HD], BF16)
    din("mloc16", [P, NTC], BF16)
    din("mloc", [P, 512], BF16)
    din("mglob", [P, 512], BF16)
    din("pmask", [8, 1], F32R)
    din("idx_kv", [P, 1], U32)
    din("idx_q2", [P, 2], U32)
    din("idx_cd", [P, 1], U32)
    din("idx_qk", [P, 2 * NHEAD], U32)
    din("idx_rhs", [P, NHEAD], U32)
    din("idx_v", [P, NHEAD], U32)
    t["xout"] = nc.dram_tensor("xout", [D, T], F32R, kind="ExternalOutput").ap()
    t["dbg"] = nc.dram_tensor("dbg", [P, 4096], F32, kind="ExternalOutput").ap()

    def dint(name, shape, dtype):
        t[name] = nc.dram_tensor(gs(name), list(shape), dtype).ap()

    dint("xspill", [D, T], F32R)
    dint("ag_kv_in", [65, D], F32)
    dint("ag_kv_out", [8 * 65, D], F32)
    dint("ag_qkv_in", [2 * D, T], BF16)
    dint("ag_qkv_out", [8 * 2 * D, T], BF16)
    dint("ag_q2_in", [P, 256], F32)
    dint("ag_q2_out", [8 * P, 256], F32)
    dint("ag_cd_in", [P, KC * P], F32)
    dint("ag_cd_out", [8 * P, KC * P], F32)
    dint("ag_cnt_in", [1, 16], F32)
    dint("ag_cnt_out", [8, 16], F32)
    return t


CONST_NAMES = ["eye_b", "eye_f", "eye_r", "slt_r", "ones_sq", "ones_col", "ones_row",
               "ones_rowb", "idx_rhs",
               "mloc16", "mloc", "mglob", "pmask", "idx_kv", "idx_q2", "idx_cd",
               "idx_qk", "idx_v", "L0_g", "L0_b", "L1_ln1_w", "L1_ln1_b",
               "L1_ln2_w", "L1_ln2_b", "L1_g1", "L1_g2", "L1_tau_t", "L1_tau_c"]

CONST_SHAPES = {
    "eye_b": ([P, P], BF16), "eye_f": ([P, P], F32), "eye_r": ([P, P], F32R),
    "slt_r": ([P, P], F32R),
    "ones_sq": ([P, P], F32R), "ones_col": ([P, 1], F32R), "ones_row": ([1, 512], F32), "ones_rowb": ([1, 512], BF16),
    "mloc16": ([P, NTC], BF16), "mloc": ([P, 512], BF16), "mglob": ([P, 512], BF16),
    "pmask": ([8, 1], F32R), "idx_kv": ([P, 1], U32), "idx_q2": ([P, 2], U32),
    "idx_cd": ([P, 1], U32), "idx_qk": ([P, 2 * NHEAD], U32), "idx_rhs": ([P, NHEAD], U32), "idx_v": ([P, NHEAD], U32),
    "L0_g": ([P, KC], F32), "L0_b": ([P, KC], F32),
    "L1_ln1_w": ([P, KC], F32), "L1_ln1_b": ([P, KC], F32),
    "L1_ln2_w": ([P, KC], F32), "L1_ln2_b": ([P, KC], F32),
    "L1_g1": ([P, KC], F32), "L1_g2": ([P, KC], F32),
    "L1_tau_t": ([1, 16], F32), "L1_tau_c": ([1, 8], F32),
}


PHASE_CONSTS = {
    "eye_b": ("p1",), "mloc16": ("p1",), "mloc": ("p1",), "mglob": ("p1",),
    "idx_kv": ("p1",),
    "idx_q2": ("p3",), "idx_qk": ("p3",), "idx_rhs": ("p3",), "idx_v": ("p3",),
    "L1_tau_t": ("p3",), "idx_cd": ("p4",), "L1_tau_c": ("p4",),
    "L1_ln1_w": ("p3",), "L1_ln1_b": ("p3",),
    "L1_ln2_w": ("p4",), "L1_ln2_b": ("p4",),
    "L1_g1": ("p3",), "L1_g2": ("p4",),
    "L0_g": ("p2",), "L0_b": ("p2",),
    "pmask": ("p5",), "slt_r": ("p5",), "ones_sq": ("p5",),
}


def setup_ctx(ctx, phases):
    t = ctx.t
    t["X"] = ctx.sbuf("X", [P, KC, T], F32R)
    t["psA"] = ctx.psum("psA", [P, 4, 512], F32)
    t["psB"] = ctx.psum("psB", [P, 4, 512], F32)
    names = [nm for nm in CONST_NAMES
             if not PHASE_CONSTS.get(nm) or
             (any(p in phases for p in PHASE_CONSTS[nm])
              and "p1" not in PHASE_CONSTS.get(nm, ()))]
    rope = False
    for nm in names:
        shape, dt = CONST_SHAPES[nm]
        t["c_" + nm] = ctx.sbuf("c_" + nm, shape, dt)
    if rope:
        t["c_ropeC"] = ctx.sbuf("c_ropeC", [P, NTC, HD], BF16)
        t["c_ropeS"] = ctx.sbuf("c_ropeS", [P, NTC, HD], BF16)
    ldt = ctx.sem["ldt"]

    def f_load(sync):
        sync.dma_start(out=t["X"][:, :, :],
                       in_=ctx.io["xin"].rearrange("(kc p) t -> p kc t", p=P)).then_inc(ldt.h, 16)
        for nm in names:
            sync.dma_start(out=t["c_" + nm][...], in_=ctx.io[nm]).then_inc(ldt.h, 16)
        if rope:
            sync.dma_start(out=t["c_ropeC"][:, :, :],
                           in_=ctx.io["ropeC"].rearrange("(tc p) e -> p tc e", p=P)).then_inc(ldt.h, 16)
            sync.dma_start(out=t["c_ropeS"][:, :, :],
                           in_=ctx.io["ropeS"].rearrange("(tc p) e -> p tc e", p=P)).then_inc(ldt.h, 16)
        sync.wait_ge(ldt.h, ldt.n)

    ldt.n += 16 * (1 + len(names) + (2 if rope else 0))
    blk(ctx, sync=f_load)
    for k in ["eye_b", "eye_f", "eye_r", "slt_r", "ones_sq", "ones_col", "ones_row"]:
        if "c_" + k in t:
            t[k] = t["c_" + k]


def finish(ctx):
    st = ctx.sem["st"]
    target = st.n + 16

    def f_store(sync):
        sync.dma_start(out=ctx.io["xout"].rearrange("(kc p) t -> p kc t", p=P),
                       in_=ctx.t["X"][:, :, :]).then_inc(st.h, 16)
        sync.wait_ge(st.h, target)

    st.n = target
    blk(ctx, sync=f_store)


def allgather(ctx, in_ap, out_ap):
    cc = ctx.sem["cc"]
    cc.n += 1
    target = cc.n

    def f_cc(gpsimd):
        gpsimd.collective_compute(
            "AllGather", ALU.bypass, replica_groups=[[0, 1, 2, 3, 4, 5, 6, 7]],
            ins=[in_ap.opt()], outs=[out_ap.opt()]).then_inc(cc.h, 1)
        gpsimd.wait_ge(cc.h, target)

    blk(ctx, gpsimd=f_cc)




P1_CONSTS = ["eye_b", "mloc16", "mloc", "mglob", "idx_kv"]


def phase1(ctx):
    nc, t = ctx.nc, ctx.t
    pe, ve, ac, ldg = ctx.sem["pe"], ctx.sem["ve"], ctx.sem["ac"], ctx.sem["ldg"]
    with ExitStack() as es:
        for nm in P1_CONSTS:
            shape, dt_ = CONST_SHAPES[nm]
            t["c_" + nm] = ctx.sbuf("c_" + nm, shape, dt_, es)
        t["c_ropeC"] = ctx.sbuf("c_ropeC", [P, NTC, HD], BF16, es)
        t["c_ropeS"] = ctx.sbuf("c_ropeS", [P, NTC, HD], BF16, es)
        t["eye_b"] = t["c_eye_b"]
        ldt1 = ctx.sem["ldt"]

        def f_ld1(sync):
            for nm in P1_CONSTS:
                sync.dma_start(out=t["c_" + nm][...], in_=ctx.io[nm]).then_inc(ldt1.h, 16)
            sync.dma_start(out=t["c_ropeC"][:, :, :],
                           in_=ctx.io["ropeC"].rearrange("(tc p) e -> p tc e", p=P)).then_inc(ldt1.h, 16)
            sync.dma_start(out=t["c_ropeS"][:, :, :],
                           in_=ctx.io["ropeS"].rearrange("(tc p) e -> p tc e", p=P)).then_inc(ldt1.h, 16)
            sync.wait_ge(ldt1.h, ldt1.n)

        ldt1.n += 16 * (len(P1_CONSTS) + 2)
        blk(ctx, sync=f_ld1)
        st = stats(ctx, t["X"], "ln", es)
        q_fm = ctx.sbuf("q_fm", [P, KC, T], BF16, es)
        attn_out = q_fm
        go_sb = ctx.sbuf("go_sb", [P, KC, 416], BF16, es)
        kvden = ctx.sbuf("kvden", [65, NHEAD * HD], F32, es)

        for hg in range(4):
            msl = (hg * 256, (hg + 1) * 256)
            with ExitStack() as eh:
                q_tm = ctx.sbuf("q_tm", [P, NTC, 256], BF16, eh)
                k_tm = ctx.sbuf("k_tm", [P, NTC, 256], BF16, eh)
                v_ext = ctx.sbuf("v_ext", [P, NTC, 260], BF16, eh)

                for wname, kind in [("L0_Wq", "q"), ("L0_Wk", "k"), ("L0_Wv", "v")]:
                    with ExitStack() as ew:
                        wd = load_w(ctx, ctx.io[wname], msl, F32R, ew,
                                    name="w" + kind, want_sum=True)
                        if kind == "v":
                            def evac(vector, ps, tc, j):
                                dst = v_ext[:, tc, :].rearrange(
                                    "p (h c) -> p h c", c=65)[:, :, 0:64]
                                return vector.tensor_scalar(
                                    dst, ps.rearrange("p (h e) -> p h e", e=64),
                                    st["rstd_col"][:, tc:tc + 1], None, ALU.mult)
                        else:
                            dstt = q_tm if kind == "q" else k_tm

                            def evac(vector, ps, tc, j, dstt=dstt):
                                return vector.tensor_scalar(
                                    dstt[:, tc, :], ps,
                                    st["rstd_col"][:, tc:tc + 1], None, ALU.mult)
                        proj_tm(ctx, wd, st, t["X"], 256, evac)

                def f_ones(vector):
                    vector.memset(v_ext[:, :, :].rearrange(
                        "p tc (h c) -> p tc h c", c=65)[:, :, :, 64:65], 1.0)
                blk(ctx, vector=f_ones)

                ra = ctx.sbuf("ra", [P, 256], BF16, eh)
                rb = ctx.sbuf("rb", [P, 256], BF16, eh)

                def f_rope(vector):
                    for dst in (q_tm, k_tm):
                        for tc in range(NTC):
                            x = dst[:, tc, :]
                            xr = x.rearrange("p (h half e) -> p h half e",
                                             half=2, e=32)[:, :, ::-1, :]
                            C = t["c_ropeC"][:, tc, None, :].broadcast_to([P, 4, HD])
                            S = t["c_ropeS"][:, tc, None, :].broadcast_to(
                                [P, 4, HD]).rearrange("p h (half e) -> p h half e",
                                                      half=2)
                            vector.tensor_mul(
                                ra[:, :].rearrange("p (h e) -> p h e", e=64),
                                x.rearrange("p (h e) -> p h e", e=64), C)
                            vb = vector.tensor_mul(
                                rb[:, :].rearrange("p (h half e) -> p h half e",
                                                   half=2, e=32), xr, S)
                            edge(ctx, vector, vb, "edv")
                            edge(ctx, vector,
                                 vector.tensor_add(x, ra[:, :], rb[:, :]), "edv")

                blk(ctx, vector=f_rope)

                def e_qtr(tensor, job, bi):
                    tc, pc = job
                    return tensor.transpose(ctx.bankb(bi)[:, 0:128],
                                            q_tm[:, tc, pc * 128:(pc + 1) * 128],
                                            t["eye_b"][:, :])

                def e_qev(vector, job, bi):
                    tc, pc = job
                    return vector.tensor_copy(q_fm[:, hg * 2 + pc, ts128(tc)],
                                              ctx.bankb(bi)[:, 0:128])

                pipe(ctx, [(tc, pc) for tc in range(NTC) for pc in range(2)],
                     e_qtr, e_qev, banks=(0, 1))

                E_sb = q_tm

                def f_exp(scalar):
                    scalar.activation(E_sb[:, :, :], k_tm[:, :, :], AF.Exp)
                blk(ctx, scalar=f_exp)

                def f_emask(vector):
                    vector.tensor_mul(
                        E_sb[:, :, :], E_sb[:, :, :],
                        t["c_mloc16"][:, :, None].broadcast_to([P, NTC, 256]))
                blk(ctx, vector=f_emask)

                def e_kv(tensor, h, bi):
                    ins = None
                    for tc in range(NTC):
                        ins = tensor.matmul(ctx.bank(bi)[0:65, 0:64],
                                            v_ext[:, tc, h * 65:h * 65 + 65],
                                            E_sb[:, tc, h * 64:(h + 1) * 64],
                                            start=(tc == 0), stop=(tc == NTC - 1))
                    return ins

                def e_kvev(vector, h, bi):
                    hh = hg * 4 + h
                    return vector.tensor_copy(kvden[0:65, hh * 64:(hh + 1) * 64],
                                              ctx.bank(bi)[0:65, 0:64])

                pipe(ctx, list(range(4)), e_kv, e_kvev, banks=(0, 1))

                kg_sb = ctx.sbuf("kg_sb", [P, 2, 512], BF16, eh)
                sg_sb = ctx.sbuf("sg_sb", [P, 2, 4, 416], BF16, eh)
                sgT_sb = ctx.sbuf("sgT_sb", [P, 2, 4, 416], BF16, eh)
                mrow = ctx.sbuf("mrow", [P, 2, 4, 2], F32, eh)
                ic_sizes = [128, 128, 128, 25]
                jc_sizes = [128, 128, 128, 25]

                for rnd in range(2):
                    heads = [rnd * 2, rnd * 2 + 1]

                    def e_kgt2(tensor, job, bi):
                        hh, jc = job
                        return tensor.transpose(
                            ctx.bankb(bi)[0:64, 0:jc_sizes[jc]],
                            k_tm[0:jc_sizes[jc], jc,
                                 heads[hh] * 64:(heads[hh] + 1) * 64],
                            t["eye_b"][0:jc_sizes[jc], 0:jc_sizes[jc]])

                    def e_kgev2(vector, job, bi):
                        hh, jc = job
                        hpar = ((hg * 4 + heads[hh]) % 2) * 64
                        return vector.tensor_copy(
                            kg_sb[hpar:hpar + 64, hh, jc * 128:jc * 128 + jc_sizes[jc]],
                            ctx.bankb(bi)[0:64, 0:jc_sizes[jc]])

                    pipe(ctx, [(hh, jc) for hh in range(2) for jc in range(4)],
                         e_kgt2, e_kgev2, banks=(0, 1))

                    def f_scores(tensor):
                        for hh in range(2):
                            h = heads[hh]
                            hchunk = (hg * 4 + h) // 2
                            hpar = ((hg * 4 + h) % 2) * 64
                            for ic in range(4):
                                icw = ic_sizes[ic]
                                tensor.matmul(
                                    ctx.bank(hh * 4 + ic)[0:icw, 0:G],
                                    q_fm[hpar:hpar + 64, hchunk,
                                         ic * 128:ic * 128 + icw],
                                    kg_sb[hpar:hpar + 64, hh, 0:G],
                                    start=True, stop=True)
                    blk(ctx, tensor=f_scores)

                    def f_smax(vector):
                        for hh in range(2):
                            for ic in range(4):
                                icw = ic_sizes[ic]
                                bi = hh * 4 + ic
                                edge(ctx, vector, vector.reduce_max(
                                    mrow[0:icw, hh, ic, 0:1],
                                    ctx.bank(bi)[0:icw, 0:G], axis=AX), "edv")
                                edge(ctx, vector, vector.tensor_scalar(
                                    mrow[0:icw, hh, ic, 1:2],
                                    mrow[0:icw, hh, ic, 0:1],
                                    -0.125, None, ALU.mult), "edv")
                    blk(ctx, vector=f_smax)

                    def f_sexp(scalar):
                        for hh in range(2):
                            for ic in range(4):
                                icw = ic_sizes[ic]
                                bi = hh * 4 + ic
                                scalar.activation(
                                    sg_sb[0:icw, hh, ic, 0:G],
                                    ctx.bank(bi)[0:icw, 0:G], AF.Exp,
                                    bias=mrow[0:icw, hh, ic, 1:2], scale=0.125)
                    blk(ctx, scalar=f_sexp)

                    def f_snorm(vector):
                        for hh in range(2):
                            for ic in range(4):
                                icw = ic_sizes[ic]
                                sl = sg_sb[0:icw, hh, ic, 0:G]
                                edge(ctx, vector, vector.reduce_sum(
                                    mrow[0:icw, hh, ic, 0:1], sl, axis=AX), "edv")
                                edge(ctx, vector, vector.reciprocal(
                                    mrow[0:icw, hh, ic, 1:2],
                                    mrow[0:icw, hh, ic, 0:1]), "edv")
                                edge(ctx, vector, vector.tensor_scalar(
                                    sl, sl, mrow[0:icw, hh, ic, 1:2],
                                    None, ALU.mult), "edv")
                    blk(ctx, vector=f_snorm)

                    def e_st(tensor, job, bi):
                        hh, jc = job
                        ins = None
                        for ic in range(4):
                            icw = ic_sizes[ic]
                            ins = tensor.transpose(
                                ctx.bankb(bi)[0:jc_sizes[jc],
                                              ic * 128:ic * 128 + icw],
                                sg_sb[0:icw, hh, ic,
                                      jc * 128:jc * 128 + jc_sizes[jc]],
                                t["eye_b"][0:icw, 0:icw])
                        return ins

                    def e_stev(vector, job, bi):
                        hh, jc = job
                        return vector.tensor_copy(
                            sgT_sb[0:jc_sizes[jc], hh, jc, 0:G],
                            ctx.bankb(bi)[0:jc_sizes[jc], 0:G])

                    pipe(ctx, [(hh, jc) for hh in range(2) for jc in range(4)],
                         e_st, e_stev, banks=(0, 1))

                    def e_go(tensor, hh, bi):
                        h = heads[hh]
                        ins = None
                        for jc in range(4):
                            jw = jc_sizes[jc]
                            ins = tensor.matmul(
                                ctx.bank(bi)[0:64, 0:G],
                                v_ext[0:jw, jc, h * 65:h * 65 + 64],
                                sgT_sb[0:jw, hh, jc, 0:G],
                                start=(jc == 0), stop=(jc == 3))
                        return ins

                    def e_goev(vector, hh, bi):
                        h = heads[hh]
                        hid = hg * 4 + h
                        hchunk = hid // 2
                        hpar = (hid % 2) * 64
                        i1 = vector.tensor_copy(
                            sg_sb[0:64, 0, 0, 0:G], ctx.bank(bi)[0:64, 0:G])
                        edge(ctx, vector, i1, "edv")
                        return vector.tensor_mul(
                            go_sb[hpar:hpar + 64, hchunk, 0:G],
                            sg_sb[0:64, 0, 0, 0:G],
                            t["c_mglob"][0:64, 0:G])

                    pipe(ctx, [0, 1], e_go, e_goev, banks=(2, 3))

        stc = ctx.sem["st"]

        kv_tgt = stc.n + 16

        def f_kvout(sync):
            sync.dma_start(out=ctx.io["ag_kv_in"], in_=kvden[:, :]).then_inc(stc.h, 16)
            sync.wait_ge(stc.h, kv_tgt)
        stc.n = kv_tgt
        blk(ctx, sync=f_kvout)

        allgather(ctx, ctx.io["ag_kv_in"], ctx.io["ag_kv_out"])

        kvp = ctx.sbuf("kvp", [65, NHEAD * HD], F32, es)
        ldgn = ldg.n + 16

        def f_kvp(gpsimd):
            gpsimd.indirect_dma_start(
                kvp[:, :], None, ctx.io["ag_kv_out"],
                IndirectOffsetOnAxis(t["c_idx_kv"][0:65, 0:1], 0)).then_inc(ldg.h, 16)
            gpsimd.wait_ge(ldg.h, ldgn)
        ldg.n = ldgn
        blk(ctx, gpsimd=f_kvp)

        def f_kvadd(vector):
            vector.tensor_add(kvden[:, :], kvden[:, :], kvp[:, :])
        blk(ctx, vector=f_kvadd)

        kv_bf = ctx.sbuf("kv_bf", [P, KC, HD], BF16, es)
        rdeni = ctx.sbuf("rdeni", [HD, NHEAD], F32, es)

        def f_dent(tensor):
            for h in range(NHEAD):
                tensor.transpose(ctx.bank(0)[0:64, h:h + 1],
                                 kvden[64:65, h * 64:(h + 1) * 64],
                                 t["eye_f"][64:65, 64:65])
        blk(ctx, tensor=f_dent)

        def f_denev(vector):
            edge(ctx, vector, vector.tensor_copy(rdeni[:, :],
                                                 ctx.bank(0)[0:64, 0:NHEAD]), "edv")
            edge(ctx, vector, vector.reciprocal(rdeni[:, :], rdeni[:, :]), "edv")
            src4 = kvden[0:64, :].rearrange("p (hc par e) -> p hc par e",
                                            par=2, e=64)
            vector.tensor_copy(kv_bf[0:64, :, :], src4[:, :, 0, :])
            vector.tensor_copy(kv_bf[64:128, :, :], src4[:, :, 1, :])
        blk(ctx, vector=f_denev)

        def e_lo(tensor, job, bi):
            h, ib = job
            hchunk = h // 2
            hpar = (h % 2) * 64
            return tensor.matmul(ctx.bank(bi)[0:64, :],
                                 kv_bf[hpar:hpar + 64, hchunk, :],
                                 q_fm[hpar:hpar + 64, hchunk, ts512(ib)],
                                 start=True, stop=True)

        def e_loev(vector, job, bi):
            h, ib = job
            hchunk = h // 2
            hpar = (h % 2) * 64
            dst = attn_out[hpar:hpar + 64, hchunk, ts512(ib)]
            if ib > 0:
                return vector.tensor_scalar(dst, ctx.bank(bi)[0:64, :],
                                            rdeni[0:64, h:h + 1], None, ALU.mult)
            i1 = vector.tensor_scalar(sg_tmp[0:64, :], ctx.bank(bi)[0:64, :],
                                      rdeni[0:64, h:h + 1], None, ALU.mult)
            edge(ctx, vector, i1, "edv")
            i2 = vector.tensor_mul(sg_tmp[0:64, :], sg_tmp[0:64, :],
                                   t["c_mloc"][0:64, :])
            edge(ctx, vector, i2, "edv")
            i3 = vector.tensor_copy(dst, sg_tmp[0:64, :])
            edge(ctx, vector, i3, "edv")
            return vector.tensor_add(dst[:, 0:G], dst[:, 0:G],
                                     go_sb[hpar:hpar + 64, hchunk, 0:G])

        sg_tmp = ctx.sbuf("sg_tmp", [HD, 512], BF16, es)
        pipe(ctx, [(h, ib) for h in range(NHEAD) for ib in range(NTB)],
             e_lo, e_loev, banks=(0, 1))

        with ExitStack() as ew:
            wo = load_w(ctx, ctx.io["L0_Wout"], (0, D), BF16, ew, name="wo")

            def evac_wo(vector, ps, mc, tb):
                return vector.tensor_add(t["X"][:, mc, ts512(tb)],
                                         t["X"][:, mc, ts512(tb)], ps)

            proj_fm(ctx, wo["w"], attn_out, evac_wo)


def spipe(ctx, nslabs, jps, emit_load, emit_mm, emit_ev, ev_eng="vector",
          banks=(0, 1), nsb=3):
    ldg, pe, ve = ctx.sem["ldg"], ctx.sem["pe"], ctx.sem["ve"]
    ldg0, pe0, ve0 = ldg.n, pe.n, ve.n
    nbuf = len(banks)
    NSB = nsb

    def f_g(gpsimd):
        for s in range(nslabs):
            if s >= NSB:
                gpsimd.wait_ge(pe.h, pe0 + (s - NSB + 1) * jps)
            emit_load(gpsimd, s, s % NSB).then_inc(ldg.h, 16)

    def f_mm(tensor):
        i = 0
        for s in range(nslabs):
            tensor.wait_ge(ldg.h, ldg0 + 16 * (s + 1))
            for j in range(jps):
                if i >= nbuf:
                    tensor.wait_ge(ve.h, ve0 + i - nbuf + 1)
                emit_mm(tensor, s, j, banks[i % nbuf], s % NSB).then_inc(pe.h, 1)
                i += 1

    def f_ev(eng):
        i = 0
        for s in range(nslabs):
            for j in range(jps):
                eng.wait_ge(pe.h, pe0 + i + 1)
                emit_ev(eng, s, j, banks[i % nbuf]).then_inc(ve.h, 1)
                i += 1

    ldg.n += 16 * nslabs
    pe.n += nslabs * jps
    ve.n += nslabs * jps
    blk(ctx, gpsimd=f_g, **{"tensor": f_mm, ev_eng: f_ev})


def dump_dbg(ctx, ap, cols):
    stc = ctx.sem["st"]
    tgt = stc.n + 16

    def f_d(gpsimd):
        gpsimd.dma_start(out=ctx.io["dbg"][:, 0:cols], in_=ap).then_inc(stc.h, 16)
        gpsimd.wait_ge(stc.h, tgt)
    stc.n = tgt
    blk(ctx, gpsimd=f_d)


def phase2(ctx):
    nc, t = ctx.nc, ctx.t
    pe, ve, ac, ldg = ctx.sem["pe"], ctx.sem["ve"], ctx.sem["ac"], ctx.sem["ldg"]
    with ExitStack() as es:
        xn = ctx.sbuf("xn", [P, KC, T], BF16, es)
        wrow = ctx.sbuf("wrow", [8, NTC, P], F32, es)

        with ExitStack() as eA:
            st = stats(ctx, t["X"], "lnln", eA, bcast=("mult", "mu", "plain"))
            gcol, bcol = t["c_L0_g"], t["c_L0_b"]

            t1 = ctx.sbuf("p2t1", [P, 512], F32R, eA)
            t2 = ctx.sbuf("p2t2", [P, 512], F32R, eA)

            def f_xn(vector):
                for kc in range(KC):
                    for tb in range(NTB):
                        i1 = vector.tensor_sub(t1[:, :], t["X"][:, kc, ts512(tb)],
                                               st["b_mu"][:, tb, :])
                        edge(ctx, vector, i1, "edv")
                        i2 = vector.tensor_mul(t2[:, :], t1[:, :], st["b_mult"][:, tb, :])
                        edge(ctx, vector, i2, "edv")
                        edge(ctx, vector,
                             vector.tensor_scalar(xn[:, kc, ts512(tb)], t2[:, :],
                                                  gcol[:, kc:kc + 1], bcol[:, kc:kc + 1],
                                                  ALU.mult, ALU.add), "edv")
            blk(ctx, vector=f_xn)

            logits = ctx.sbuf("logits", [P, NTC, 8], F32, eA)
            probs = ctx.sbuf("probs", [P, NTC, 8], F32, eA)
            mx = ctx.sbuf("mx", [P, NTC, 1], F32, eA)
            oh = ctx.sbuf("ohs", [P, NTC, 8], F32, eA)
            wdense = ctx.sbuf("wdense", [P, NTC, 8], F32, eA)

            with ExitStack() as ew:
                wd = load_w(ctx, ctx.io["L0_router"], (0, 8), F32R, ew,
                            name="wrt", want_sum=True, wcol=gcol, bcol=bcol)

                def evac_rt(vector, ps, tc, j):
                    return vector.tensor_scalar(logits[:, tc, :], ps[:, 0:8],
                                                st["rstd_col"][:, tc:tc + 1],
                                                None, ALU.mult)
                proj_tm(ctx, wd, st, t["X"], 8, evac_rt)

            if DEBUG == "logits":
                dump_dbg(ctx, logits[:, :, :].rearrange("p tc e -> p (tc e)"), NTC * 8)

            def f_resid(vector):
                for kc in range(KC):
                    for tb in range(NTB):
                        i1 = vector.tensor_sub(t1[:, :], t["X"][:, kc, ts512(tb)],
                                               st["b_mu"][:, tb, :])
                        edge(ctx, vector, i1, "edv")
                        i2 = vector.tensor_mul(t2[:, :], t1[:, :], st["b_plain"][:, tb, :])
                        edge(ctx, vector, i2, "edv")
                        edge(ctx, vector,
                             vector.tensor_add(t["X"][:, kc, ts512(tb)],
                                               t["X"][:, kc, ts512(tb)], t2[:, :]),
                             "edv")
            blk(ctx, vector=f_resid)

            def f_pexp(scalar):
                scalar.activation(probs[:, :, :], logits[:, :, :], AF.Exp,
                                  scale=1.0 / 1.5)
            blk(ctx, scalar=f_pexp)

            def f_top2(vector):
                edge(ctx, vector, vector.reduce_sum(mx[:, :, :], probs[:, :, :],
                                                    axis=AX), "edv")
                edge(ctx, vector, vector.reciprocal(mx[:, :, :], mx[:, :, :]), "edv")
                edge(ctx, vector, vector.tensor_mul(
                    probs[:, :, :], probs[:, :, :],
                    mx[:, :, :].broadcast_to([P, NTC, 8])), "edv")
                edge(ctx, vector, vector.reduce_max(mx[:, :, :], probs[:, :, :],
                                                    axis=AX), "edv")
                edge(ctx, vector, vector.tensor_tensor(
                    oh[:, :, :], probs[:, :, :],
                    mx[:, :, :].broadcast_to([P, NTC, 8]), ALU.is_equal), "edv")
                edge(ctx, vector, vector.tensor_mul(
                    wdense[:, :, :], oh[:, :, :],
                    mx[:, :, :].broadcast_to([P, NTC, 8])), "edv")
                edge(ctx, vector, vector.tensor_scalar(
                    oh[:, :, :], oh[:, :, :], -1.0, 1.0, ALU.mult, ALU.add), "edv")
                edge(ctx, vector, vector.tensor_mul(
                    probs[:, :, :], probs[:, :, :], oh[:, :, :]), "edv")
                edge(ctx, vector, vector.reduce_max(mx[:, :, :], probs[:, :, :],
                                                    axis=AX), "edv")
                edge(ctx, vector, vector.tensor_tensor(
                    oh[:, :, :], probs[:, :, :],
                    mx[:, :, :].broadcast_to([P, NTC, 8]), ALU.is_equal), "edv")
                edge(ctx, vector, vector.tensor_mul(
                    oh[:, :, :], oh[:, :, :],
                    mx[:, :, :].broadcast_to([P, NTC, 8])), "edv")
                edge(ctx, vector, vector.tensor_add(
                    wdense[:, :, :], wdense[:, :, :], oh[:, :, :]), "edv")
            blk(ctx, vector=f_top2)

            def e_wt(tensor, tc, bi):
                return tensor.transpose(ctx.bank(bi)[0:8, 0:128],
                                        wdense[:, tc, :], t["eye_f"][:, :])

            def e_wtev(vector, tc, bi):
                return vector.tensor_copy(wrow[:, tc, :], ctx.bank(bi)[0:8, 0:128])

            pipe(ctx, list(range(NTC)), e_wt, e_wtev, banks=(0, 1))
            if DEBUG == "wdense":
                dump_dbg(ctx, wdense[:, :, :].rearrange("p tc e -> p (tc e)"), NTC * 8)

        if DEBUG == "noexp":
            return
        brow = ctx.sbuf("brow", [1, T], BF16, es)
        Bcur = ctx.sbuf("Bcur", [P, NTB, 512], BF16, es)
        tmp = ctx.sbuf("p2tmp", [P, 512], F32R, es)
        h = ctx.sbuf("hbuf", [P, 32, 1024], BF16, es)
        wslab = ctx.sbuf("wslab", [P, 96, P], BF16, es)
        w1sb = [wslab[:, 0:KC, :], wslab[:, 32:32 + KC, :], wslab[:, 64:64 + KC, :]]
        w2sb = [wslab[:, 0:32, :], wslab[:, 32:64, :], wslab[:, 64:96, :]]

        for e in range(8):
            stc = ctx.sem["st"]
            br_tgt = stc.n + 16

            def f_brow(gpsimd, e=e, br_tgt=br_tgt):
                gpsimd.dma_start(out=brow[0:1, :],
                                 in_=wrow[e:e + 1, :, :].rearrange("o tc p -> o (tc p)")
                                 ).then_inc(stc.h, 16)
                gpsimd.wait_ge(stc.h, br_tgt)
            stc.n = br_tgt
            blk(ctx, gpsimd=f_brow)

            def e_bmm(tensor, tb, bi):
                return tensor.matmul(ctx.bank(bi), t["c_ones_rowb"][0:1, 0:P],
                                     brow[0:1, ts512(tb)], start=True, stop=True)

            def e_bev(vector, tb, bi):
                return vector.tensor_copy(Bcur[:, tb, :], ctx.bank(bi))

            pipe(ctx, list(range(NTB)), e_bmm, e_bev, banks=(0, 1))

            for tb2 in range(2):
                def el_h(gpsimd, s, buf, e=e):
                    return gpsimd.dma_start(
                        out=w1sb[buf][:, :, :].rearrange("p kc m -> p (kc m)"),
                        in_=ctx.io["L0_W1H"][e, s].rearrange("p kc m -> p (kc m)"))

                def em_h(tensor, s, j, bi, buf, tb2=tb2):
                    ins = None
                    for kc in range(KC):
                        ins = tensor.matmul(
                            ctx.bank(bi), w1sb[buf][:, kc, :],
                            xn[:, kc, tb2 * 1024 + j * 512:tb2 * 1024 + (j + 1) * 512],
                            start=(kc == 0), stop=(kc == KC - 1))
                    return ins

                def ev_h(scalar, s, j, bi):
                    return scalar.activation(h[:, s, ts512(j)], ctx.bank(bi), AF.Silu)

                spipe(ctx, 32, 2, el_h, em_h, ev_h, ev_eng="scalar", banks=(0, 1, 2, 3))

                def el_y(gpsimd, s, buf, e=e):
                    return gpsimd.dma_start(
                        out=w2sb[buf][:, :, :].rearrange("p fc m -> p (fc m)"),
                        in_=ctx.io["L0_W2H"][e, s].rearrange("p fc m -> p (fc m)"),
                        max_dma_last_dim=2048)

                def em_y(tensor, s, j, bi, buf):
                    ins = None
                    for fc in range(32):
                        ins = tensor.matmul(ctx.bank(bi), w2sb[buf][:, fc, :],
                                            h[:, fc, ts512(j)],
                                            start=(fc == 0), stop=(fc == 31))
                    return ins

                def ev_y(vector, s, j, bi, tb2=tb2):
                    tb = tb2 * 2 + j
                    i1 = vector.tensor_mul(tmp[:, :], ctx.bank(bi), Bcur[:, tb, :])
                    edge(ctx, vector, i1, "edv")
                    return vector.tensor_add(t["X"][:, s, ts512(tb)],
                                             t["X"][:, s, ts512(tb)], tmp[:, :])

                spipe(ctx, 8, 2, el_y, em_y, ev_y, ev_eng="vector", banks=(0, 1, 2, 3))



def phase3(ctx):
    nc, t = ctx.nc, ctx.t
    pe, ve, ac, ldg = ctx.sem["pe"], ctx.sem["ve"], ctx.sem["ac"], ctx.sem["ldg"]
    stc, cc = ctx.sem["st"], ctx.sem["cc"]
    psA, psB = t["psA"], t["psB"]
    with ExitStack() as es:
        rows3 = ctx.sbuf("rows3", [1, 8, 16], F32, es)
        with ExitStack() as eA:
            st = stats(ctx, t["X"], "lnln", eA, bcast=("mult", "mu", "plain"))
            qk_sb = ctx.sbuf("qk_sb", [P, KC, T], BF16, eA)
            q2own = ctx.sbuf("q2own", [P, NTC, 16], F32, eA)
            sqq = ctx.sbuf("sqq", [P, T], F32R, eA)
            q2st = st["_rows"][0:1, 0, :]

            for wname in ["L1_Wqk_t", "L1_Wv_t"]:
                isv = wname == "L1_Wv_t"
                for quarter in range(4):
                    m0 = quarter * 2
                    with ExitStack() as ew:
                        wd = load_w(ctx, ctx.io[wname], (m0 * 128, (m0 + 2) * 128),
                                    F32R, ew, name="w3", want_sum=True,
                                    wcol=t["c_L1_ln1_w"], bcol=t["c_L1_ln1_b"])

                        def evac(vector, ps, mc, tb):
                            return vector.tensor_mul(qk_sb[:, mc, ts512(tb)], ps,
                                                     st["b_mult"][:, tb, :])
                        proj_fm(ctx, wd["w"], t["X"], evac, mcs=range(m0, m0 + 2),
                                st=st, wd=wd, m0=m0)

                if not isv:
                    for kc in range(KC):
                        def f_sq(scalar, kc=kc):
                            scalar.activation(sqq[:, :], qk_sb[:, kc, :], AF.Square)
                        blk(ctx, scalar=f_sq)
                        for par in range(2):
                            h = kc * 2 + par
                            base = par * 64

                            def f_rmm(tensor, base=base):
                                for tb in range(NTB):
                                    tensor.matmul(psA[0:1, tb, :],
                                                  t["ones_col"][base:base + 64, 0:1],
                                                  sqq[base:base + 64, ts512(tb)],
                                                  start=True, stop=True)
                            blk(ctx, tensor=f_rmm)

                            def f_rev(vector):
                                for tb in range(NTB):
                                    vector.tensor_copy(q2st[0:1, ts512(tb)],
                                                       psA[0:1, tb, :])
                            blk(ctx, vector=f_rev)

                            def f_rtr(tensor):
                                for tc in range(NTC):
                                    tensor.transpose(psB[:, 0, tc:tc + 1],
                                                     q2st[0:1, ts128(tc)],
                                                     t["eye_f"][0:1, 0:1])
                            blk(ctx, tensor=f_rtr)

                            def f_q2ev(vector, h=h):
                                vector.tensor_copy(q2own[:, :, h],
                                                   psB[:, 0, 0:NTC])
                            blk(ctx, vector=f_q2ev)

                    tg1 = stc.n + 32

                    def f_out1(sync):
                        sync.dma_start(
                            out=ctx.io["ag_q2_in"],
                            in_=q2own[:, :, :].rearrange("p tc h -> p (tc h)")
                        ).then_inc(stc.h, 16)
                        sync.dma_start(
                            out=ctx.io["ag_qkv_in"][0:D, :].rearrange(
                                "(kc p) t -> p kc t", p=P),
                            in_=qk_sb[:, :, :]).then_inc(stc.h, 16)
                        sync.wait_ge(stc.h, tg1)
                    stc.n = tg1
                    blk(ctx, sync=f_out1)

            tg2 = stc.n + 16

            def f_out2(sync):
                sync.dma_start(
                    out=ctx.io["ag_qkv_in"][D:2 * D, :].rearrange(
                        "(kc p) t -> p kc t", p=P),
                    in_=qk_sb[:, :, :]).then_inc(stc.h, 16)
                sync.wait_ge(stc.h, tg2)
            stc.n = tg2
            blk(ctx, sync=f_out2)

            allgather(ctx, ctx.io["ag_qkv_in"], ctx.io["ag_qkv_out"])
            allgather(ctx, ctx.io["ag_q2_in"], ctx.io["ag_q2_out"])

            tg3 = stc.n + 16

            def f_spill(sync):
                sync.dma_start(out=ctx.io["xspill"].rearrange("(kc p) t -> p kc t", p=P),
                               in_=t["X"][:, :, :]).then_inc(stc.h, 16)
                sync.wait_ge(stc.h, tg3)
            stc.n = tg3
            blk(ctx, sync=f_spill)

            def f_xo(vector):
                for kc in range(KC):
                    for tb in range(NTB):
                        sl = t["X"][:, kc, ts512(tb)]
                        i1 = vector.tensor_sub(sqq[:, ts512(tb)], sl,
                                               st["b_mu"][:, tb, :])
                        edge(ctx, vector, i1, "edv")
                        edge(ctx, vector,
                             vector.tensor_mul(sl, sqq[:, ts512(tb)],
                                               st["b_plain"][:, tb, :]), "edv")
            blk(ctx, vector=f_xo)

        q2T = ctx.sbuf("q2T", [P, 2, NTC, 16], F32, es)
        q2m = ctx.sbuf("q2m", [P, 16], F32, es)
        bc3 = ctx.sbuf("bc3", [P, 4, 16], F32, es)
        bias_sb = ctx.sbuf("bias3", [P, 2, NTC, 16], F32, es)
        attn_heads = ctx.sbuf("ath", [P, KC, T], BF16, es)
        v_ext2 = [ctx.sbuf("v_ext3", [P, 32, 130], BF16, es) for _ in range(2)]
        v_fm2 = [ctx.sbuf("vfp", [P, 2, T], BF16, es) for _ in range(2)]
        vtmp2 = [ctx.sbuf("vtmp", [P, 32, P], BF16, es) for _ in range(2)]
        qk_head2 = [ctx.sbuf("qkh", [HD, 2, T], BF16, es) for _ in range(2)]
        rhs_tile2 = [ctx.sbuf("rhst", [HD, T], BF16, es) for _ in range(2)]
        E2 = ctx.sbuf("E2", [P, 2, T], BF16, es)
        denst = ctx.sbuf("denst", [1, 512], F32, es)
        tmp3 = ctx.sbuf("tmp3", [P, 512], F32R, es)

        tg4 = ldg.n + 32

        def f_gq2(gpsimd):
            for half in range(2):
                gpsimd.indirect_dma_start(
                    q2T[:, half, :, :].rearrange("p tc h -> p (tc h)"), None,
                    ctx.io["ag_q2_out"],
                    IndirectOffsetOnAxis(t["c_idx_q2"][:, half:half + 1], 0)
                ).then_inc(ldg.h, 16)
            gpsimd.wait_ge(ldg.h, tg4)
        ldg.n = tg4
        blk(ctx, gpsimd=f_gq2)

        def f_q2m(vector):
            for h in range(NHEAD):
                vector.tensor_reduce(q2m[:, h:h + 1], q2T[:, :, :, h],
                                     mybir.AxisListType.XY, ALU.max)
        blk(ctx, vector=f_q2m)

        def f_q2t(tensor):
            tensor.transpose(psA[0:16, 0, 0:P], q2m[:, :], t["eye_f"][:, :])
        blk(ctx, tensor=f_q2t)

        q2mt = ctx.sbuf("q2mt", [16, P], F32, es)

        def f_q2r(vector):
            edge(ctx, vector, vector.tensor_copy(q2mt[:, :], psA[0:16, 0, 0:P]), "edv")
            edge(ctx, vector, vector.reduce_max(q2mt[:, 0:1], q2mt[:, :],
                                                axis=AX), "edv")
        blk(ctx, vector=f_q2r)

        def f_q2t2(tensor):
            tensor.transpose(psA[0:1, 0, 0:16], q2mt[0:16, 0:1],
                             t["eye_f"][0:16, 0:16])
        blk(ctx, tensor=f_q2t2)

        r3 = lambda i: rows3[0:1, i, :]

        def f_m0(vector):
            edge(ctx, vector, vector.tensor_copy(r3(0), t["c_L1_tau_t"][0:1, :]), "edv")
            edge(ctx, vector, vector.tensor_scalar(r3(1), r3(0), 0.125, None, ALU.mult), "edv")
            edge(ctx, vector, vector.tensor_scalar(r3(2), r3(1), -1.0, None, ALU.mult), "edv")
            edge(ctx, vector, vector.tensor_copy(r3(3), psA[0:1, 0, 0:16]), "edv")
            edge(ctx, vector, vector.tensor_scalar(r3(4), r3(1), -1.0, None, ALU.mult), "edv")
            edge(ctx, vector, vector.tensor_max(r3(4), r3(4), r3(1)), "edv")
            edge(ctx, vector, vector.tensor_scalar(r3(5), r3(1), 0.0, None, ALU.is_gt), "edv")
            edge(ctx, vector, vector.tensor_scalar(r3(5), r3(5), -2.0, 3.0, ALU.mult, ALU.add), "edv")
            edge(ctx, vector, vector.tensor_mul(r3(6), r3(4), r3(3)), "edv")
            edge(ctx, vector, vector.tensor_mul(r3(6), r3(6), r3(5)), "edv")
            edge(ctx, vector, vector.tensor_scalar(r3(7), r3(1), 2.0, None, ALU.mult), "edv")
        blk(ctx, vector=f_m0)

        def e_b3(tensor, i, bi):
            sl = [7, 2, 6][i]
            return tensor.matmul(ctx.bank(bi)[:, 0:16], t["ones_row"][0:1, 0:P],
                                 rows3[0:1, sl, :], start=True, stop=True)

        def e_b3ev(vector, i, bi):
            return vector.tensor_copy(bc3[:, i, :], ctx.bank(bi)[:, 0:16])

        pipe(ctx, [0, 1, 2], e_b3, e_b3ev, banks=(0, 1))

        def f_bias(vector):
            for h in range(NHEAD):
                vector.tensor_scalar(
                    bias_sb[:, :, :, h].rearrange("p a b -> p (a b)"),
                    q2T[:, :, :, h].rearrange("p a b -> p (a b)"),
                    bc3[:, 1, h:h + 1], bc3[:, 2, h:h + 1],
                    ALU.mult, ALU.subtract)
        blk(ctx, vector=f_bias)

        def f_vones(vector):
            for ve_ in v_ext2:
                vector.memset(ve_[:, :, :].rearrange(
                    "p j (g c) -> p j g c", c=65)[:, :, :, 64:65], 1.0)
        blk(ctx, vector=f_vones)

        vg_tgt = [0] * 9
        vt_tgt = [0] * 9
        qk_tgt = [0] * 17

        def emit_gv(gpsimd, p):
            for si in range(2):
                gpsimd.indirect_dma_start(
                    v_fm2[p % 2][:, si, :], None, ctx.io["ag_qkv_out"],
                    IndirectOffsetOnAxis(
                        t["c_idx_v"][:, p * 2 + si:p * 2 + si + 1], 0)
                ).then_inc(ldg.h, 16)
            ldg.n += 32
            vg_tgt[p] = ldg.n

        def emit_gqk(gpsimd, hh):
            for si in range(2):
                gpsimd.indirect_dma_start(
                    qk_head2[hh % 2][:, si, :], None, ctx.io["ag_qkv_out"],
                    IndirectOffsetOnAxis(
                        t["c_idx_qk"][:, 2 * hh + si:2 * hh + si + 1], 0)
                ).then_inc(ldg.h, 16)
            gpsimd.indirect_dma_start(
                rhs_tile2[hh % 2][:, :], None, ctx.io["ag_qkv_out"],
                IndirectOffsetOnAxis(t["c_idx_rhs"][:, hh:hh + 1], 0)
            ).then_inc(ldg.h, 16)
            ldg.n += 48
            qk_tgt[hh] = ldg.n

        def emit_vtr(sync, p):
            sync.wait_ge(ldg.h, vg_tgt[p])
            for half in range(2):
                for tcl in range(NTC):
                    sync.dma_start(out=vtmp2[p % 2][:, half * NTC + tcl, :],
                                   in_=v_fm2[p % 2][:, half, ts128(tcl)],
                                   transpose=True).then_inc(stc.h, 16)
            stc.n += 16 * 32
            vt_tgt[p] = stc.n

        def f_pro(gpsimd):
            emit_gv(gpsimd, 0)
            emit_gqk(gpsimd, 0)
            gpsimd.wait_ge(ldg.h, qk_tgt[0])
        blk(ctx, gpsimd=f_pro)

        def f_pro2(sync):
            emit_vtr(sync, 0)
            sync.wait_ge(stc.h, vt_tgt[0])
        blk(ctx, sync=f_pro2)

        def f_pro3(vector):
            vector.tensor_copy(
                v_ext2[0][:, :, :].rearrange("p j (g c) -> p j g c",
                                             c=65)[:, :, :, 0:64],
                vtmp2[0][:, :, :].rearrange("p j (g c) -> p j g c", c=64))
        blk(ctx, vector=f_pro3)

        for h in range(NHEAD):
            c2 = h // 2
            v_ext = v_ext2[c2 % 2]
            qk_head = qk_head2[h % 2]
            rhs_tile = rhs_tile2[h % 2]

            def f_pref(gpsimd, h=h, c2=c2):
                if h + 1 < NHEAD:
                    emit_gqk(gpsimd, h + 1)
                if h % 2 == 0 and c2 + 1 < 8:
                    emit_gv(gpsimd, c2 + 1)
                gpsimd.wait_ge(ldg.h, max(qk_tgt[h], vg_tgt[c2]))

            def f_pref_t(sync, h=h, c2=c2):
                if h % 2 == 1 and c2 + 1 < 8:
                    emit_vtr(sync, c2 + 1)

            def f_pref_v(vector, h=h, c2=c2):
                pass
            blk(ctx, gpsimd=f_pref, sync=f_pref_t)

            ac0, pe0 = ac.n, pe.n
            gslot = h % 2
            NU = 64

            def dot_count(u):
                return u + 1 if u < 2 else 2 * u

            def f_att_t(tensor, h=h, ac0=ac0, pe0=pe0, gslot=gslot):
                def emit_av(u):
                    jc, hf = u // 2, u % 2
                    ins = None
                    for ib in range(2):
                        ins = tensor.matmul(
                            psB[0:65, hf * 2 + ib, :],
                            v_ext[:, jc, gslot * 65:gslot * 65 + 65],
                            E2[:, u % 2, ts512(ib)],
                            start=(jc == 0), stop=(jc == 31))
                    ins.then_inc(pe.h, 1)

                for u in range(NU):
                    jc, hf = u // 2, u % 2
                    if u >= 2:
                        tensor.wait_ge(ac.h, ac0 + u - 1)
                        emit_av(u - 2)
                    lhsT = qk_head[0:64, jc // NTC, ts128(jc % NTC)]
                    ins = None
                    for ib in range(2):
                        ins = tensor.matmul(psA[:, 2 * (u % 2) + ib, :], lhsT,
                                            rhs_tile[0:64, ts512(hf * 2 + ib)],
                                            start=True, stop=True)
                    ins.then_inc(pe.h, 1)
                for u in (NU - 2, NU - 1):
                    tensor.wait_ge(ac.h, ac0 + u + 1)
                    emit_av(u)

            def f_att_s(scalar, h=h, ac0=ac0, pe0=pe0):
                for u in range(NU):
                    jc = u // 2
                    scalar.wait_ge(pe.h, pe0 + dot_count(u))
                    scalar.activation(E2[:, u % 2, 0:1024],
                                      psA[:, 2 * (u % 2):2 * (u % 2) + 2, :],
                                      AF.Exp,
                                      bias=bias_sb[:, jc // NTC, jc % NTC, h:h + 1],
                                      scale=bc3[:, 0, h:h + 1]).then_inc(ac.h, 1)

            ac.n += NU
            pe.n += 2 * NU
            blk(ctx, tensor=f_att_t, scalar=f_att_s)

            hchunk, hpar = h // 2, (h % 2) * 64

            pe0e, ve0e = pe.n, ve.n

            def f_ev_t(tensor):
                for ib in range(NTB):
                    tensor.wait_ge(ve.h, ve0e + 2 * ib + 1)
                    tensor.matmul(psA[:, 0, :], t["ones_row"][0:1, 0:P],
                                  denst[0:1, :], start=True,
                                  stop=True).then_inc(pe.h, 1)

            def f_ev_v(vector, h=h, c2=c2):
                if h % 2 == 1 and c2 + 1 < 8:
                    vector.wait_ge(stc.h, vt_tgt[c2 + 1])
                    vector.tensor_copy(
                        v_ext2[(c2 + 1) % 2][:, :, :].rearrange(
                            "p j (g c) -> p j g c", c=65)[:, :, :, 0:64],
                        vtmp2[(c2 + 1) % 2][:, :, :].rearrange(
                            "p j (g c) -> p j g c", c=64))
                for ib in range(NTB):
                    i1 = vector.tensor_copy(denst[0:1, :], psB[64:65, ib, :])
                    edge(ctx, vector, i1, "edv")
                    vector.reciprocal(denst[0:1, :],
                                      denst[0:1, :]).then_inc(ve.h, 1)
                    edge(ctx, vector, vector.tensor_copy(
                        attn_heads[hpar:hpar + 64, hchunk, ts512(ib)],
                        psB[0:64, ib, :]), "edv")
                    vector.wait_ge(pe.h, pe0e + ib + 1)
                    vector.tensor_mul(
                        attn_heads[hpar:hpar + 64, hchunk, ts512(ib)],
                        attn_heads[hpar:hpar + 64, hchunk, ts512(ib)],
                        psA[0:64, 0, :]).then_inc(ve.h, 1)

            pe.n += NTB
            ve.n += 2 * NTB
            blk(ctx, tensor=f_ev_t, vector=f_ev_v)

        with ExitStack() as ew:
            wo = load_w(ctx, ctx.io["L1_Wo_t"], (0, D), BF16, ew, name="wo3")

            def evac_wo(vector, ps, mc, tb):
                i1 = vector.tensor_scalar(tmp3[:, :], ps,
                                          t["c_L1_g1"][:, mc:mc + 1], None, ALU.mult)
                edge(ctx, vector, i1, "edv")
                return vector.tensor_add(t["X"][:, mc, ts512(tb)],
                                         t["X"][:, mc, ts512(tb)], tmp3[:, :])

            proj_fm(ctx, wo["w"], attn_heads, evac_wo)



def phase4(ctx):
    nc, t = ctx.nc, ctx.t
    pe, ve, ac, ldg = ctx.sem["pe"], ctx.sem["ve"], ctx.sem["ac"], ctx.sem["ldg"]
    stc = ctx.sem["st"]
    psA, psB = t["psA"], t["psB"]
    with ExitStack() as es:
        qk_tm = ctx.sbuf("qk_tm", [P, NTC, D], BF16, es)
        eB = ExitStack()
        st4 = stats(ctx, t["X"], "ln", eB, bcast=("mult",), nsq=1)
        v_c = ctx.sbuf("v_c", [P, KC, T], BF16, eB)
        dot_sb = ctx.sbuf("dot_sb", [P, KC, P], F32, eB)
        rows4 = ctx.sbuf("rows4", [1, 2, 8], F32, eB)
        sgc = ctx.sbuf("sgc", [P, 8], F32, eB)
        q2c = ctx.sbuf("q2c", [P, 1], F32, eB)
        t4 = ctx.sbuf("t4", [P, P], F32, eB)
        lt = ctx.sbuf("lt", [P, P], F32, eB)
        sm4 = ctx.sbuf("sm4", [P, P], F32, eB)
        attT = ctx.sbuf("attT", [P, P], BF16, eB)
        q2rst = ctx.sbuf("q2rst", [1, P], F32, eB)
        mx4 = ctx.sbuf("mx4", [P, 2], F32, eB)

        for wname, istm in [("L1_Wqk_c", True), ("L1_Wv_c", False)]:
            for m0 in range(KC):
                with ExitStack() as ew:
                    wd = load_w(ctx, ctx.io[wname], (m0 * 128, (m0 + 1) * 128),
                                F32R, ew, name="w4", want_sum=True,
                                wcol=t["c_L1_ln2_w"], bcol=t["c_L1_ln2_b"])
                    if istm:
                        def evac_tm(vector, ps, tc, j, m0=m0):
                            return vector.tensor_scalar(
                                qk_tm[:, tc, m0 * 128:(m0 + 1) * 128], ps,
                                st4["rstd_col"][:, tc:tc + 1], None, ALU.mult)
                        proj_tm(ctx, wd, st4, t["X"], 128, evac_tm)
                    else:
                        def evac_fm(vector, ps, mc, tb):
                            return vector.tensor_mul(v_c[:, mc, ts512(tb)], ps,
                                                     st4["b_mult"][:, tb, :])
                        proj_fm(ctx, wd["w"], t["X"], evac_fm,
                                mcs=[m0], st=st4, wd=wd, m0=m0)

        def e_dot(tensor, g, bi):
            ins = None
            for tc in range(NTC):
                ins = tensor.matmul(ctx.bank(bi)[:, 0:P],
                                    qk_tm[:, tc, ts128(g)],
                                    qk_tm[:, tc, ts128(g)],
                                    start=(tc == 0), stop=(tc == NTC - 1))
            return ins

        def e_dotev(vector, g, bi):
            return vector.tensor_copy(dot_sb[:, g, :], ctx.bank(bi)[:, 0:P])

        pipe(ctx, list(range(KC)), e_dot, e_dotev, banks=(0, 1))

        tg = stc.n + 16

        def f_dout(sync, tg=tg):
            sync.dma_start(out=ctx.io["ag_cd_in"],
                           in_=dot_sb[:, :, :].rearrange("p g c -> p (g c)")
                           ).then_inc(stc.h, 16)
            sync.wait_ge(stc.h, tg)
        stc.n = tg
        blk(ctx, sync=f_dout)

        allgather(ctx, ctx.io["ag_cd_in"], ctx.io["ag_cd_out"])

        dotp = ctx.sbuf("dotp", [P, KC, P], F32, eB)
        tg = ldg.n + 16

        def f_gdot(gpsimd, tg=tg):
            gpsimd.indirect_dma_start(
                dotp[:, :, :].rearrange("p g c -> p (g c)"), None,
                ctx.io["ag_cd_out"],
                IndirectOffsetOnAxis(t["c_idx_cd"][:, 0:1], 0)).then_inc(ldg.h, 16)
            gpsimd.wait_ge(ldg.h, tg)
        ldg.n = tg
        blk(ctx, gpsimd=f_gdot)

        def f_dadd(vector):
            vector.tensor_add(dot_sb[:, :, :], dot_sb[:, :, :], dotp[:, :, :])
        blk(ctx, vector=f_dadd)

        r4 = lambda i: rows4[0:1, i, :]

        def f_tau(vector):
            edge(ctx, vector, vector.tensor_scalar(r4(0), t["c_L1_tau_c"][0:1, :],
                                                   1.0 / 64.0, None, ALU.mult), "edv")
        blk(ctx, vector=f_tau)

        def f_taub(tensor):
            tensor.matmul(psA[:, 0, 0:8], t["ones_row"][0:1, 0:P], r4(0),
                          start=True, stop=True)
        blk(ctx, tensor=f_taub)

        def f_taubv(vector):
            vector.tensor_copy(sgc[:, :], psA[:, 0, 0:8])
        blk(ctx, vector=f_taubv)

        cd_alias = qk_tm[:, :, :].rearrange("p a b -> p (a b)").rearrange(
            "p (kc tt) -> p kc tt", tt=T)

        for g in range(KC):
            def fv1(vector, g=g):
                i1 = vector.tensor_mul(t4[:, :], dot_sb[:, g, :], t["eye_f"][:, :])
                edge(ctx, vector, i1, "edv")
                edge(ctx, vector, vector.reduce_sum(q2c[:, 0:1], t4[:, :],
                                                    axis=AX), "edv")
            blk(ctx, vector=fv1)

            def ft1(tensor):
                tensor.transpose(psA[0:1, 1, 0:P], q2c[:, 0:1], t["eye_f"][:, :])
            blk(ctx, tensor=ft1)

            def fv2(vector):
                vector.tensor_copy(q2rst[0:1, :], psA[0:1, 1, 0:P])
            blk(ctx, vector=fv2)

            def ft2(tensor):
                tensor.matmul(psA[:, 2, 0:P], t["ones_row"][0:1, 0:P],
                              q2rst[0:1, :], start=True, stop=True)
            blk(ctx, tensor=ft2)

            def fv3(vector, g=g):
                i1 = vector.tensor_scalar(lt[:, :], dot_sb[:, g, :], 2.0,
                                          q2c[:, 0:1], ALU.mult, ALU.subtract)
                edge(ctx, vector, i1, "edv")
                i2 = vector.tensor_sub(lt[:, :], lt[:, :], psA[:, 2, 0:P])
                edge(ctx, vector, i2, "edv")
                i3 = vector.tensor_scalar(lt[:, :], lt[:, :], sgc[:, g:g + 1],
                                          None, ALU.mult)
                edge(ctx, vector, i3, "edv")
                edge(ctx, vector, vector.reduce_max(mx4[:, 0:1], lt[:, :],
                                                    axis=AX), "edv")
                edge(ctx, vector, vector.tensor_scalar(mx4[:, 1:2], mx4[:, 0:1],
                                                       -1.0, None, ALU.mult), "edv")
            blk(ctx, vector=fv3)

            def fa1(scalar):
                scalar.activation(sm4[:, :], lt[:, :], AF.Exp,
                                  bias=mx4[:, 1:2])
            blk(ctx, scalar=fa1)

            def fv4(vector):
                edge(ctx, vector, vector.reduce_sum(mx4[:, 0:1], sm4[:, :],
                                                    axis=AX), "edv")
                edge(ctx, vector, vector.reciprocal(mx4[:, 0:1], mx4[:, 0:1]), "edv")
                edge(ctx, vector, vector.tensor_scalar(sm4[:, :], sm4[:, :],
                                                       mx4[:, 0:1], None,
                                                       ALU.mult), "edv")
            blk(ctx, vector=fv4)

            def ft3(tensor):
                tensor.transpose(ctx.bank(0)[:, 0:P], sm4[:, :], t["eye_f"][:, :])
            blk(ctx, tensor=ft3)

            def fv5(vector):
                vector.tensor_copy(attT[:, :], ctx.bank(0)[:, 0:P])
            blk(ctx, vector=fv5)

            def ft4(tensor, g=g):
                for tb in range(NTB):
                    tensor.matmul(psB[:, tb, :], attT[:, :],
                                  v_c[:, g, ts512(tb)], start=True, stop=True)
            blk(ctx, tensor=ft4)

            def fv6(vector, g=g):
                for tb in range(NTB):
                    vector.tensor_copy(cd_alias[:, g, ts512(tb)], psB[:, tb, :])
            blk(ctx, vector=fv6)

        eB.close()
        tmp4 = ctx.sbuf("tmp4", [P, 512], F32R, es)
        with ExitStack() as ew:
            wo = load_w(ctx, ctx.io["L1_Wo_c"], (0, D), BF16, ew, name="wo4")

            def evac_wo(vector, ps, mc, tb):
                i1 = vector.tensor_scalar(tmp4[:, :], ps,
                                          t["c_L1_g2"][:, mc:mc + 1], None, ALU.mult)
                edge(ctx, vector, i1, "edv")
                return vector.tensor_add(t["X"][:, mc, ts512(tb)],
                                         t["X"][:, mc, ts512(tb)], tmp4[:, :])

            proj_fm(ctx, wo["w"], cd_alias, evac_wo)

        xs = ctx.sbuf("xs4", [P, 2, T], F32R, es)
        ldt = ctx.sem["ldt"]
        ldt0 = ldt.n

        ve4 = ctx.sem["aux"]
        ve40 = ve4.n

        def f_back_d2(sync):
            for kc in range(KC):
                if kc >= 2:
                    sync.wait_ge(ve4.h, ve40 + kc - 1)
                sync.dma_start(
                    out=xs[:, kc % 2, :],
                    in_=ctx.io["xspill"].rearrange("(kc p) t -> p kc t", p=P)[:, kc, :]
                ).then_inc(ldt.h, 16)

        def f_back_v(vector):
            for kc in range(KC):
                vector.wait_ge(ldt.h, ldt0 + 16 * (kc + 1))
                vector.tensor_add(t["X"][:, kc, :], t["X"][:, kc, :],
                                  xs[:, kc % 2, :]).then_inc(ve4.h, 1)

        ldt.n += 16 * KC
        ve4.n += KC
        blk(ctx, sync=f_back_d2, vector=f_back_v)



def phase5(ctx):
    nc, t = ctx.nc, ctx.t
    pe, ve, ac, ldg = ctx.sem["pe"], ctx.sem["ve"], ctx.sem["ac"], ctx.sem["ldg"]
    stc, cc = ctx.sem["st"], ctx.sem["cc"]
    psA, psB = t["psA"], t["psB"]
    CAP = 4096.0
    with ExitStack() as es:
        xn = ctx.sbuf("xn5", [P, KC, T], BF16, es)
        wrow = ctx.sbuf("wrow5", [8, NTC, P], BF16, es)

        with ExitStack() as eA:
            st = stats(ctx, t["X"], "ln", eA, bcast=("mult", "mu"))
            t1 = ctx.sbuf("p5t1", [P, 512], F32R, eA)

            def f_xn(vector):
                for kc in range(KC):
                    for tb in range(NTB):
                        i1 = vector.tensor_sub(t1[:, :], t["X"][:, kc, ts512(tb)],
                                               st["b_mu"][:, tb, :])
                        edge(ctx, vector, i1, "edv")
                        edge(ctx, vector,
                             vector.tensor_mul(xn[:, kc, ts512(tb)], t1[:, :],
                                               st["b_mult"][:, tb, :]), "edv")
            blk(ctx, vector=f_xn)

            logits = ctx.sbuf("logits5", [P, NTC, 8], F32, eA)
            probs = ctx.sbuf("probs5", [P, NTC, 8], F32, eA)
            mx = ctx.sbuf("mx5", [P, NTC, 1], F32, eA)
            g1 = ctx.sbuf("g1_5", [P, NTC, 1], F32, eA)
            g2 = ctx.sbuf("g2_5", [P, NTC, 1], F32, eA)
            oh1 = ctx.sbuf("oh1_5", [P, NTC, 8], F32R, eA)
            oh2 = ctx.sbuf("oh2_5", [P, NTC, 8], F32R, eA)
            rank = ctx.sbuf("rank5", [P, NTC, 2, 8], F32, eA)
            G = ctx.sbuf("G5", [P, NTC, 8], F32, eA)
            offs = ctx.sbuf("offs5", [1, 2, 16], F32, eA)
            offb = ctx.sbuf("offb5", [P, 16], F32, eA)
            cnt_r = ctx.sbuf("cntr5", [8, 16], F32R, eA)

            with ExitStack() as ew:
                wd = load_w(ctx, ctx.io["L1_wg"], (0, 8), F32R, ew,
                            name="wg5", want_sum=True)

                def evac_rt(vector, ps, tc, j):
                    return vector.tensor_scalar(logits[:, tc, :], ps[:, 0:8],
                                                st["rstd_col"][:, tc:tc + 1],
                                                None, ALU.mult)
                proj_tm(ctx, wd, st, t["X"], 8, evac_rt)

            def f_pexp(scalar):
                scalar.activation(probs[:, :, :], logits[:, :, :], AF.Exp)
            blk(ctx, scalar=f_pexp)

            def f_top2(vector):
                edge(ctx, vector, vector.reduce_sum(mx[:, :, :], probs[:, :, :],
                                                    axis=AX), "edv")
                edge(ctx, vector, vector.reciprocal(mx[:, :, :], mx[:, :, :]), "edv")
                edge(ctx, vector, vector.tensor_mul(
                    probs[:, :, :], probs[:, :, :],
                    mx[:, :, :].broadcast_to([P, NTC, 8])), "edv")
                edge(ctx, vector, vector.reduce_max(g1[:, :, :], probs[:, :, :],
                                                    axis=AX), "edv")
                edge(ctx, vector, vector.tensor_tensor(
                    oh1[:, :, :], probs[:, :, :],
                    g1[:, :, :].broadcast_to([P, NTC, 8]), ALU.is_equal), "edv")
                edge(ctx, vector, vector.tensor_scalar(
                    G[:, :, :], oh1[:, :, :], -1.0, 1.0, ALU.mult, ALU.add), "edv")
                edge(ctx, vector, vector.tensor_mul(
                    probs[:, :, :], probs[:, :, :], G[:, :, :]), "edv")
                edge(ctx, vector, vector.reduce_max(g2[:, :, :], probs[:, :, :],
                                                    axis=AX), "edv")
                edge(ctx, vector, vector.tensor_tensor(
                    oh2[:, :, :], probs[:, :, :],
                    g2[:, :, :].broadcast_to([P, NTC, 8]), ALU.is_equal), "edv")
                edge(ctx, vector, vector.tensor_add(mx[:, :, :], g1[:, :, :],
                                                    g2[:, :, :]), "edv")
                edge(ctx, vector, vector.tensor_scalar(mx[:, :, :], mx[:, :, :],
                                                       1e-9, None, ALU.max), "edv")
                edge(ctx, vector, vector.reciprocal(mx[:, :, :], mx[:, :, :]), "edv")
                edge(ctx, vector, vector.tensor_mul(g1[:, :, :], g1[:, :, :],
                                                    mx[:, :, :]), "edv")
                edge(ctx, vector, vector.tensor_mul(g2[:, :, :], g2[:, :, :],
                                                    mx[:, :, :]), "edv")
            blk(ctx, vector=f_top2)

            def e_rank(tensor, job, bi):
                tc, k = job
                oh = oh1 if k == 0 else oh2
                ps = ctx.bank(bi)[:, 0:8]
                ins = None
                for tcp in range(tc + 1):
                    lhsT = (t["slt_r"][:, :] if tcp == tc else t["ones_sq"][:, :])
                    ins = tensor.matmul(ps, lhsT, oh[:, tcp, :],
                                        start=(tcp == 0), stop=(tcp == tc))
                return ins

            def e_rankev(vector, job, bi):
                tc, k = job
                return vector.tensor_copy(rank[:, tc, k, :], ctx.bank(bi)[:, 0:8])

            pipe(ctx, [(tc, k) for tc in range(NTC) for k in range(2)],
                 e_rank, e_rankev, banks=(0, 1))

            def f_cnt(tensor):
                for k in range(2):
                    oh = oh1 if k == 0 else oh2
                    for tc in range(NTC):
                        tensor.matmul(psA[0:1, 2 + k, 0:8],
                                      t["ones_col"][:, 0:1], oh[:, tc, :],
                                      start=(tc == 0), stop=(tc == NTC - 1))
            blk(ctx, tensor=f_cnt)

            def f_cntev(vector):
                vector.tensor_copy(offs[0:1, 0, 0:8], psA[0:1, 2, 0:8])
                vector.tensor_copy(offs[0:1, 0, 8:16], psA[0:1, 3, 0:8])
            blk(ctx, vector=f_cntev)

            tg = stc.n + 16

            def f_cout(sync, tg=tg):
                sync.dma_start(out=ctx.io["ag_cnt_in"],
                               in_=offs[0:1, 0, :]).then_inc(stc.h, 16)
                sync.wait_ge(stc.h, tg)
            stc.n = tg
            blk(ctx, sync=f_cout)

            allgather(ctx, ctx.io["ag_cnt_in"], ctx.io["ag_cnt_out"])

            tg = ldg.n + 16

            def f_cin(gpsimd, tg=tg):
                gpsimd.dma_start(out=cnt_r[:, :],
                                 in_=ctx.io["ag_cnt_out"]).then_inc(ldg.h, 16)
                gpsimd.wait_ge(ldg.h, tg)
            ldg.n = tg
            blk(ctx, gpsimd=f_cin)

            def f_off(tensor):
                tensor.matmul(psA[0:1, 2, 0:16], t["c_pmask"][:, 0:1],
                              cnt_r[:, :], start=True, stop=True)
                tensor.matmul(psA[0:1, 3, 0:8], t["ones_col"][0:8, 0:1],
                              cnt_r[:, 0:8], start=True, stop=True)
            blk(ctx, tensor=f_off)

            def f_offv(vector):
                i1 = vector.tensor_copy(offs[0:1, 1, :], psA[0:1, 2, 0:16])
                edge(ctx, vector, i1, "edv")
                edge(ctx, vector, vector.tensor_add(
                    offs[0:1, 1, 8:16], offs[0:1, 1, 8:16], psA[0:1, 3, 0:8]), "edv")
            blk(ctx, vector=f_offv)

            def f_offb_mm(tensor):
                tensor.matmul(psA[:, 2, 0:16], t["ones_row"][0:1, 0:P],
                              offs[0:1, 1, :], start=True, stop=True)
            blk(ctx, tensor=f_offb_mm)

            def f_offb_ev(vector):
                vector.tensor_copy(offb[:, :], psA[:, 2, 0:16])
            blk(ctx, vector=f_offb_ev)

            pos = probs

            def f_G(vector):
                for k in range(2):
                    oh = oh1 if k == 0 else oh2
                    gk = g1 if k == 0 else g2
                    i1 = vector.tensor_add(
                        pos[:, :, :], rank[:, :, k, :],
                        offb[:, None, k * 8:(k + 1) * 8].broadcast_to([P, NTC, 8]))
                    edge(ctx, vector, i1, "edv")
                    i2 = vector.tensor_scalar(pos[:, :, :], pos[:, :, :], CAP,
                                              None, ALU.is_lt)
                    edge(ctx, vector, i2, "edv")
                    i3 = vector.tensor_mul(pos[:, :, :], pos[:, :, :], oh[:, :, :])
                    edge(ctx, vector, i3, "edv")
                    i4 = vector.tensor_mul(pos[:, :, :], pos[:, :, :],
                                           gk[:, :, :].broadcast_to([P, NTC, 8]))
                    edge(ctx, vector, i4, "edv")
                    if k == 0:
                        i5 = vector.tensor_copy(G[:, :, :], pos[:, :, :])
                    else:
                        i5 = vector.tensor_add(G[:, :, :], G[:, :, :], pos[:, :, :])
                    edge(ctx, vector, i5, "edv")
            blk(ctx, vector=f_G)

            def e_wt(tensor, tc, bi):
                return tensor.transpose(ctx.bank(bi)[0:8, 0:128],
                                        G[:, tc, :], t["eye_f"][:, :])

            def e_wtev(vector, tc, bi):
                return vector.tensor_copy(wrow[:, tc, :], ctx.bank(bi)[0:8, 0:128])

            pipe(ctx, list(range(NTC)), e_wt, e_wtev, banks=(0, 1))
            if DEBUG == "G5":
                dump_dbg(ctx, G[:, :, :].rearrange("p tc e -> p (tc e)"), NTC * 8)
            if DEBUG == "logits5":
                dump_dbg(ctx, logits[:, :, :].rearrange("p tc e -> p (tc e)"), NTC * 8)
            if DEBUG == "rank5":
                dump_dbg(ctx, rank[:, :, 0, :].rearrange("p tc e -> p (tc e)"), NTC * 8)
            if DEBUG == "offs5":
                dump_dbg(ctx, offb[:, :], 16)

        if DEBUG == "xn5":
            dump_dbg(ctx, xn[:, 0, :], T)
        if DEBUG == "noexp5":
            return
        brow = ctx.sbuf("brow5", [1, T], BF16, es)
        Bcur = ctx.sbuf("Bcur5", [P, NTB, 512], BF16, es)
        tmp = ctx.sbuf("p5tmp", [P, 512], F32R, es)
        b1r = ctx.sbuf("b1r", [1, 4096], BF16, es)
        b2r = ctx.sbuf("b2r", [1, D], BF16, es)
        h = ctx.sbuf("hbuf5", [P, 32, 1024], BF16, es)
        wslab = ctx.sbuf("wslab5", [P, 64, P], BF16, es)
        w1sb = [wslab[:, 0:KC, :], wslab[:, 32:32 + KC, :]]
        w2sb = [wslab[:, 0:32, :], wslab[:, 32:64, :]]

        for e in range(8):
            tg = ldg.n + 48

            def f_brow(gpsimd, e=e, tg=tg):
                gpsimd.dma_start(out=brow[0:1, :],
                                 in_=wrow[e:e + 1, :, :].rearrange("o tc p -> o (tc p)")
                                 ).then_inc(ldg.h, 16)
                gpsimd.dma_start(out=b1r[0:1, :],
                                 in_=ctx.io["L1_b1"][e:e + 1, :]).then_inc(ldg.h, 16)
                gpsimd.dma_start(out=b2r[0:1, :],
                                 in_=ctx.io["L1_b2"][e:e + 1, :]).then_inc(ldg.h, 16)
                gpsimd.wait_ge(ldg.h, tg)
            ldg.n = tg
            blk(ctx, gpsimd=f_brow)

            def e_bmm(tensor, tb, bi):
                return tensor.matmul(ctx.bank(bi), t["c_ones_rowb"][0:1, 0:P],
                                     brow[0:1, ts512(tb)], start=True, stop=True)

            def e_bev(vector, tb, bi):
                return vector.tensor_copy(Bcur[:, tb, :], ctx.bank(bi))

            pipe(ctx, list(range(NTB)), e_bmm, e_bev, banks=(0, 1))

            for tb2 in range(2):
                def el_h(gpsimd, s, buf, e=e):
                    return gpsimd.dma_start(
                        out=w1sb[buf][:, :, :].rearrange("p kc m -> p (kc m)"),
                        in_=ctx.io["L1_w1H"][e, s].rearrange("p kc m -> p (kc m)"))

                def em_h(tensor, s, j, bi, buf, tb2=tb2):
                    for kc in range(KC):
                        tensor.matmul(
                            ctx.bank(bi), w1sb[buf][:, kc, :],
                            xn[:, kc, tb2 * 1024 + j * 512:tb2 * 1024 + (j + 1) * 512],
                            start=(kc == 0), stop=False)
                    return tensor.matmul(
                        ctx.bank(bi), b1r[0:1, s * 128:(s + 1) * 128],
                        t["c_ones_rowb"][0:1, 0:512], start=False, stop=True)

                def ev_h(scalar, s, j, bi):
                    return scalar.activation(h[:, s, ts512(j)], ctx.bank(bi), AF.Gelu)

                spipe(ctx, 32, 2, el_h, em_h, ev_h, ev_eng="scalar",
                      banks=(0, 1, 2, 3), nsb=2)

                def el_y(gpsimd, s, buf, e=e):
                    return gpsimd.dma_start(
                        out=w2sb[buf][:, :, :].rearrange("p fc m -> p (fc m)"),
                        in_=ctx.io["L1_w2H"][e, s].rearrange("p fc m -> p (fc m)"),
                        max_dma_last_dim=2048)

                def em_y(tensor, s, j, bi, buf):
                    for fc in range(32):
                        tensor.matmul(ctx.bank(bi), w2sb[buf][:, fc, :],
                                      h[:, fc, ts512(j)],
                                      start=(fc == 0), stop=False)
                    return tensor.matmul(
                        ctx.bank(bi), b2r[0:1, s * 128:(s + 1) * 128],
                        t["c_ones_rowb"][0:1, 0:512], start=False, stop=True)

                def ev_y(vector, s, j, bi, tb2=tb2):
                    tb = tb2 * 2 + j
                    i1 = vector.tensor_mul(tmp[:, :], ctx.bank(bi), Bcur[:, tb, :])
                    edge(ctx, vector, i1, "edv")
                    return vector.tensor_add(t["X"][:, s, ts512(tb)],
                                             t["X"][:, s, ts512(tb)], tmp[:, :])

                spipe(ctx, 8, 2, el_y, em_y, ev_y, ev_eng="vector", banks=(0, 1, 2, 3), nsb=2)


PHASE_FNS = {"p1": phase1, "p2": phase2, "p3": phase3, "p4": phase4, "p5": phase5}

_BUILT = {}


def build_launch(phases):
    key = tuple(phases)
    if key in _BUILT:
        return _BUILT[key]
    nc = bass.Bass("TRN2", target_bir_lowering=False, debug=False)
    io = declare_io(nc)
    with ExitStack() as es:
        ctx = Ctx(nc, es, io)
        setup_ctx(ctx, phases)
        for ph in phases:
            PHASE_FNS[ph](ctx)
        finish(ctx)
    _BUILT[key] = nc
    return nc


def _host_consts():
    import ml_dtypes
    bf = ml_dtypes.bfloat16
    c = {}
    c["eye_b"] = np.eye(P, dtype=np.float32).astype(bf)
    c["eye_f"] = np.eye(P, dtype=np.float32)
    c["eye_r"] = np.eye(P, dtype=np.float32)
    c["slt_r"] = np.triu(np.ones((P, P), dtype=np.float32), 1)
    c["ones_sq"] = np.ones((P, P), dtype=np.float32)
    c["ones_col"] = np.ones((P, 1), dtype=np.float32)
    c["ones_row"] = np.ones((1, 512), dtype=np.float32)
    c["ones_rowb"] = np.ones((1, 512), dtype=np.float32).astype(bf)
    return c


def _core_consts(core):
    import ml_dtypes
    bf = ml_dtypes.bfloat16
    b, s = core // 2, core % 2
    partner = core ^ 1
    cc = {}
    inv_freq = (1.0 / 10000.0 ** (np.arange(0, HD, 2, dtype=np.float32) / HD))
    pos = (s * T + np.arange(T, dtype=np.float32))[:, None]
    th = pos * inv_freq[None, :]
    cos, sin = np.cos(th).astype(np.float32), np.sin(th).astype(np.float32)
    cc["ropeC"] = np.concatenate([cos, cos], axis=1).astype(bf)
    cc["ropeS"] = np.concatenate([-sin, sin], axis=1).astype(bf)
    tok = s * T + np.arange(T)
    is_local = (tok >= G).astype(np.float32)
    m16 = is_local.reshape(NTC, P).T
    cc["mloc16"] = m16.astype(bf)
    cc["mloc"] = np.broadcast_to(is_local[None, 0:512], (P, 512)).astype(bf).copy()
    is_glob = ((tok < G) & (s == 0)).astype(np.float32)
    cc["mglob"] = np.broadcast_to(is_glob[None, 0:512], (P, 512)).astype(bf).copy()
    cc["pmask"] = (np.arange(8) < core).astype(np.float32).reshape(8, 1)
    p = np.arange(P, dtype=np.uint32)
    cc["idx_kv"] = (partner * 65 + np.minimum(p, 64)).astype(np.uint32).reshape(P, 1)
    iq2 = np.zeros((P, 2), dtype=np.uint32)
    for si in range(2):
        iq2[:, si] = (2 * b + si) * P + p
    cc["idx_q2"] = iq2
    cc["idx_cd"] = (partner * P + p).astype(np.uint32).reshape(P, 1)
    iq = np.zeros((P, 2 * NHEAD), dtype=np.uint32)
    for h in range(NHEAD):
        for si in range(2):
            iq[0:64, 2 * h + si] = (2 * b + si) * 2048 + h * 64 + np.arange(64)
    cc["idx_qk"] = iq
    ir = np.zeros((P, NHEAD), dtype=np.uint32)
    for h in range(NHEAD):
        ir[0:64, h] = (2 * b + s) * 2048 + h * 64 + np.arange(64)
    cc["idx_rhs"] = ir
    iv = np.zeros((P, NHEAD), dtype=np.uint32)
    for c2 in range(8):
        for si in range(2):
            iv[:, c2 * 2 + si] = (2 * b + si) * 2048 + 1024 + c2 * 128 + p
    cc["idx_v"] = iv
    return cc


def _packw1(w):
    return np.ascontiguousarray(
        np.asarray(w, np.float32).reshape(8, 8, P, 32, P).transpose(0, 3, 2, 1, 4))


def _packw2(w):
    return np.ascontiguousarray(
        np.asarray(w, np.float32).reshape(8, 32, P, 8, P).transpose(0, 3, 2, 1, 4))


def _vec8(v):
    return np.ascontiguousarray(np.asarray(v, dtype=np.float32).reshape(KC, P).T)


def kernel(**inp):
    inp = {k: np.asarray(v) for k, v in inp.items()}
    x = inp["L0_attn_x"].astype(np.float32, copy=False)
    shared = {
        "L0_Wq": inp["L0_attn_Wq"], "L0_Wk": inp["L0_attn_Wk"],
        "L0_Wv": inp["L0_attn_Wv"], "L0_Wout": inp["L0_attn_Wout"],
        "L0_g": _vec8(inp["L0_ffn_gamma"]), "L0_b": _vec8(inp["L0_ffn_beta"]),
        "L0_router": np.ascontiguousarray(inp["L0_ffn_router_w"].T),
        "L0_W1H": _packw1(inp["L0_ffn_W1"]), "L0_W2H": _packw2(inp["L0_ffn_W2"]),
        "L1_Wqk_t": inp["L1_attn_Wqk_t"], "L1_Wv_t": inp["L1_attn_Wv_t"],
        "L1_Wo_t": inp["L1_attn_Wo_t"], "L1_Wqk_c": inp["L1_attn_Wqk_c"],
        "L1_Wv_c": inp["L1_attn_Wv_c"], "L1_Wo_c": inp["L1_attn_Wo_c"],
        "L1_tau_t": np.ascontiguousarray(inp["L1_attn_tau_t"].reshape(1, 16)),
        "L1_tau_c": np.ascontiguousarray(inp["L1_attn_tau_c"].reshape(1, 8)),
        "L1_ln1_w": _vec8(inp["L1_attn_ln1_w"]), "L1_ln1_b": _vec8(inp["L1_attn_ln1_b"]),
        "L1_ln2_w": _vec8(inp["L1_attn_ln2_w"]), "L1_ln2_b": _vec8(inp["L1_attn_ln2_b"]),
        "L1_g1": _vec8(inp["L1_attn_gamma1"]), "L1_g2": _vec8(inp["L1_attn_gamma2"]),
        "L1_wg": inp["L1_ffn_wg"], "L1_w1H": _packw1(inp["L1_ffn_w1"]),
        "L1_b1": inp["L1_ffn_b1"], "L1_w2H": _packw2(inp["L1_ffn_w2"]),
        "L1_b2": inp["L1_ffn_b2"],
    }
    shared = {k: np.ascontiguousarray(v.astype(np.float32, copy=False))
              for k, v in shared.items()}
    shared.update(_host_consts())

    in_maps = []
    for c in range(8):
        b, s = c // 2, c % 2
        m = dict(shared)
        m.update(_core_consts(c))
        m["xin"] = np.ascontiguousarray(x[b, s * T:(s + 1) * T, :].T)
        in_maps.append(m)

    for phases in LAUNCHES:
        nc = build_launch(phases)
        res = run_bass_kernel_spmd(nc, in_maps, core_ids=list(range(8)))
        for c in range(8):
            in_maps[c]["xin"] = np.ascontiguousarray(res.results[c]["xout"])

    out = np.empty((4, 4096, D), dtype=np.float32)
    for c in range(8):
        b, s = c // 2, c % 2
        out[b, s * T:(s + 1) * T, :] = in_maps[c]["xin"].T
    return out
